# version 01
# baseline (speedup 1.0000x reference)
import os
import contextlib
import numpy as np
import concourse.bass as bass
import concourse.mybir as mybir
from concourse.bass_utils import run_bass_kernel_spmd

F32 = mybir.dt.float32
BF16 = mybir.dt.bfloat16
AF = mybir.ActivationFunctionType
ALU = mybir.AluOpType
AX = mybir.AxisListType

NCORES = 8
B, S, H = 4, 2048, 2048
HC = H // 128
TC = 1024
NBLK = 8
NH = 16
QLORA = 1536
KVL = 512
DR = 64
DN = 128
SA = float((DN + DR) ** (-0.5))
NE = 8
FIC = 11
FI = 1408
FSC = 22
FSI = 2816
ROT = 32
SB_SC = float(128 ** (-0.5))

N_SUB = int(os.environ.get("BASS_MK_SUBLAYERS", "4"))
TAPS = os.environ.get("BASS_MK_TAPS", "0") == "1"
NDBG = int(os.environ.get("BASS_MK_DBG", "0"))
SYNCDBG = os.environ.get("BASS_MK_SYNC", "0") == "1"

POS_OF_BUF = [2 * j for j in range(8)] + [2 * j + 1 for j in range(8)]



def _tile_weight(w, och_i=128):
    K, O = w.shape
    kc = K // 128
    oc = O // och_i
    b = w.reshape(kc, 128, oc, och_i).transpose(2, 1, 0, 3).reshape(oc, 128, kc * och_i)
    return np.ascontiguousarray(b.astype(np.float32))


def _perm_head_b():
    x1 = list(range(0, 16))
    x2 = list(range(16, 32))
    rest = list(range(32, 128))
    return np.array(x1 + rest[:16] + x2 + rest[16:32] + rest[32:], dtype=np.int64)


def _cos_sin(positions, dim):
    inv = 1.0 / (10000.0 ** (np.arange(0, dim, 2, dtype=np.float64) / dim))
    fr = np.outer(positions.astype(np.float64), inv)
    emb = np.concatenate([fr, fr], axis=-1)
    return np.cos(emb), np.sin(emb)


def build_host_inputs(core, ins):
    par = core % 2
    bidx = core // 2
    own_pos = np.array([2 * i + par for i in range(NBLK)])
    d = {}

    xc = np.asarray(ins["L0_attn_a_x"], dtype=np.float32)[bidx]
    xc = xc.reshape(16, 128, H)[own_pos]
    d["xin"] = np.ascontiguousarray(xc.reshape(TC, H).T.reshape(HC, 128, TC))

    d["wqa"] = _tile_weight(np.asarray(ins["L0_attn_a_wq_a"]))
    wqb = np.asarray(ins["L0_attn_a_wq_b"])
    cols = []
    for h in range(NH):
        cols.extend(range(h * 192, h * 192 + DN))
    for g in range(NH // 2):
        cols.extend(range((2 * g) * 192 + DN, (2 * g) * 192 + 192))
        cols.extend(range((2 * g + 1) * 192 + DN, (2 * g + 1) * 192 + 192))
    d["wqb"] = _tile_weight(wqb[:, np.array(cols)])
    wkva = np.asarray(ins["L0_attn_a_wkv_a"])
    d["wkvac"] = _tile_weight(wkva[:, :KVL])
    d["wkvap"] = _tile_weight(wkva[:, KVL:], och_i=64)[0]
    wkvb = np.asarray(ins["L0_attn_a_wkv_b"])
    cols = []
    for h in range(NH):
        cols.extend(range(h * 256, h * 256 + DN))
    for h in range(NH):
        cols.extend(range(h * 256 + DN, h * 256 + 256))
    d["wkvb"] = _tile_weight(wkvb[:, np.array(cols)])
    d["wo"] = _tile_weight(np.asarray(ins["L0_attn_a_wo"]))

    d["rw"] = _tile_weight(np.ascontiguousarray(np.asarray(ins["L0_ffn_router_w"]).T),
                           och_i=8)[0]
    d["gw"] = np.stack([_tile_weight(np.asarray(ins["L0_ffn_gate_w"])[e]) for e in range(NE)])
    d["uw"] = np.stack([_tile_weight(np.asarray(ins["L0_ffn_up_w"])[e]) for e in range(NE)])
    d["dw"] = np.stack([_tile_weight(np.asarray(ins["L0_ffn_down_w"])[e]) for e in range(NE)])
    d["sgw"] = _tile_weight(np.asarray(ins["L0_ffn_s_gate_w"]))
    d["suw"] = _tile_weight(np.asarray(ins["L0_ffn_s_up_w"]))
    d["sdw"] = _tile_weight(np.asarray(ins["L0_ffn_s_down_w"]))

    wqkv = np.asarray(ins["L0_attn_b_W_qkv"])
    bqkv = np.asarray(ins["L0_attn_b_b_qkv"])
    perm = _perm_head_b()
    cols = []
    for h in range(NH):
        cols.extend((h * 384 + perm).tolist())
    for h in range(NH):
        cols.extend((h * 384 + 128 + perm).tolist())
    for h in range(NH):
        cols.extend(range(h * 384 + 256, h * 384 + 256 + 128))
    cols = np.array(cols)
    d["wqkv"] = _tile_weight(wqkv[:, cols])
    bq = bqkv[cols].astype(np.float32)
    d["wdense"] = _tile_weight(np.asarray(ins["L0_attn_b_W_dense"]))

    vec_list = []
    def addvec(v):
        v = np.asarray(v, dtype=np.float32).reshape(-1)
        vec_list.append(v.reshape(-1, 128).T)
    addvec(ins["L0_attn_a_q_a_ln_w"])
    addvec(ins["L0_attn_a_kv_a_ln_w"])
    addvec(bq)
    addvec(ins["L0_attn_b_b_dense"])
    addvec(ins["L0_ffn_s_gate_b"])
    addvec(ins["L0_ffn_s_up_b"])
    addvec(ins["L0_ffn_s_down_b"])
    d["vecs"] = np.ascontiguousarray(np.concatenate(vec_list, axis=1))

    d["identf"] = np.eye(128, dtype=np.float32)
    d["ones1"] = np.ones((1, 128), dtype=np.float32)
    d["onesc"] = np.ones((128, 1), dtype=np.float32)

    tri = np.where(np.arange(128)[None, :] <= np.arange(128)[:, None], 0.0, -1e30
                   ).astype(np.float32)
    zer = np.zeros((128, 128), np.float32)
    neg = np.full((128, 128), -1e30, np.float32)
    am = np.zeros((8, 2, 128, 128), np.float32)
    for qt in range(8):
        am[qt, 0] = tri if par == 0 else zer
        am[qt, 1] = neg if par == 0 else tri
    d["amask"] = np.ascontiguousarray(am.transpose(2, 0, 1, 3).reshape(128, 2048))

    own_tok = (own_pos[:, None] * 128 + np.arange(128)[None, :]).reshape(-1)
    buf_tok = (np.array(POS_OF_BUF)[:, None] * 128 + np.arange(128)[None, :]).reshape(-1)

    ca, sa_ = _cos_sin(own_tok, DR)
    t = np.zeros((2, 128, TC), np.float32)
    t[0, 0:64] = ca.T * SA
    t[0, 64:128] = ca.T * SA
    t[1, 0:64] = sa_.T * SA
    t[1, 64:128] = sa_.T * SA
    d["ra_q"] = t
    ck, sk = _cos_sin(buf_tok, DR)
    t = np.zeros((4, 128, 1024), np.float32)
    t[0, 0:64] = ck.T[:, 0:1024]
    t[1, 0:64] = ck.T[:, 1024:2048]
    t[2, 0:64] = sk.T[:, 0:1024]
    t[3, 0:64] = sk.T[:, 1024:2048]
    d["ra_k"] = t
    cb, sb = _cos_sin(own_tok, ROT)
    def tb(scale):
        cos_t = np.full((128, TC), scale, np.float32)
        sin_t = np.zeros((128, TC), np.float32)
        cos_t[0:16] = cb[:, 0:16].T * scale
        cos_t[32:48] = cb[:, 16:32].T * scale
        sin_t[0:16] = sb[:, 0:16].T * scale
        sin_t[32:48] = sb[:, 16:32].T * scale
        return np.stack([cos_t, sin_t])
    d["rb_q"] = tb(SB_SC)
    d["rb_k"] = tb(1.0)
    return d



ENGINES = ("pe", "dve", "act", "sp", "pool")


class Plan:
    def __init__(self, sems):
        self.sems = sems
        self.q = {e: [] for e in ENGINES}
        self.c = {n: 0 for n in sems}
        self._lastwait = {e: {} for e in ENGINES}
        self.bank = {}

    def wait(self, eng, sem, val=None):
        v = self.c[sem] if val is None else val
        if v <= 0:
            return
        if self._lastwait[eng].get(sem, -1) >= v:
            return
        self._lastwait[eng][sem] = v
        h = self.sems[sem]
        self.q[eng].append(lambda E, h=h, v=v: E.wait_ge(h, v))

    def run(self, eng, fn, inc=None, guard=False):
        incs = []
        if inc:
            for s_, d_ in inc:
                self.c[s_] += d_
                incs.append((self.sems[s_], d_))
        need_drain = (bool(incs) and eng in ("dve", "act")) or guard

        def cl(E, fn=fn, incs=incs, need_drain=need_drain):
            inst = fn(E)
            if need_drain:
                inst = E.drain(fusable=False)
            for h, d_ in incs:
                inst.then_inc(h, d_)
        self.q[eng].append(cl)

    def val(self, sem):
        return self.c[sem]

    def barrier(self):
        for eng in ENGINES:
            for sem in ("pe", "dve", "act", "w", "io"):
                self.wait(eng, sem, self.c[sem])

    def bank_wait(self, bank):
        g = self.bank.get(bank)
        if g:
            self.wait("pe", g[0], g[1])

    def bank_set(self, bank, eng):
        self.bank[bank] = (eng, self.c[eng])


class Ring:
    def __init__(self, pl, arena, col0, cols_per_slot, nslots, sempfx):
        self.pl = pl
        self.arena = arena
        self.base = [col0 + i * cols_per_slot for i in range(nslots)]
        self.maxcols = cols_per_slot
        self.free_at = [0] * nslots
        self.semname = [f"{sempfx}{i}" for i in range(nslots)]
        self.nloads = [0] * nslots
        self.i = 0

    def load(self, src_ap, cols):
        assert cols <= self.maxcols
        pl = self.pl
        s = self.i % len(self.base)
        self.i += 1
        pl.wait("pool", "pe", self.free_at[s])
        ap = self.arena[:, self.base[s]:self.base[s] + cols]
        pl.run("pool", lambda E, ap=ap, src=src_ap: E.dma_start(out=ap, in_=src),
               inc=[(self.semname[s], 16)])
        self.nloads[s] += 1
        return s

    def wait_ready(self, eng, s):
        self.pl.wait(eng, self.semname[s], self.nloads[s] * 16)

    def ap(self, s, c0, cols):
        return self.arena[:, self.base[s] + c0:self.base[s] + c0 + cols]

    def consumed(self, s):
        self.free_at[s] = self.pl.val("pe")


class Feeder:
    def __init__(self, rng, srcs, ahead):
        self.rng = rng
        self.srcs = srcs
        self.ahead = ahead
        self.slots = []
        self.next = 0
        self._fill(ahead)

    def _fill(self, upto_extra):
        while self.next < len(self.srcs) and self.next < upto_extra:
            self.slots.append(self.rng.load(*self.srcs[self.next]))
            self.next += 1

    def get(self, i):
        while self.next < len(self.srcs) and self.next <= i + self.ahead:
            self.slots.append(self.rng.load(*self.srcs[self.next]))
            self.next += 1
        return self.slots[i]



def build_nc():
    nc = bass.Bass("TRN2", target_bir_lowering=False)

    DI = lambda name, shape: nc.dram_tensor(name, shape, F32, kind="ExternalInput")
    xin = DI("xin", [HC, 128, TC])
    wqa = DI("wqa", [12, 128, 2048])
    wqb = DI("wqb", [24, 128, 1536])
    wkvac = DI("wkvac", [4, 128, 2048])
    wkvap = DI("wkvap", [128, 1024])
    wkvb = DI("wkvb", [32, 128, 512])
    wo = DI("wo", [16, 128, 2048])
    rw = DI("rw", [128, 128])
    gw = DI("gw", [NE, FIC, 128, 2048])
    uw = DI("uw", [NE, FIC, 128, 2048])
    dwt = DI("dw", [NE, 16, 128, FI])
    sgw = DI("sgw", [FSC, 128, 2048])
    suw = DI("suw", [FSC, 128, 2048])
    sdw = DI("sdw", [16, 128, FSI])
    wqkv = DI("wqkv", [48, 128, 2048])
    wdense = DI("wdense", [16, 128, 2048])
    vecs = DI("vecs", [128, 140])
    identf = DI("identf", [128, 128])
    ones1 = DI("ones1", [1, 128])
    onesc = DI("onesc", [128, 1])
    amask = DI("amask", [128, 2048])
    ra_q = DI("ra_q", [2, 128, TC])
    ra_k = DI("ra_k", [4, 128, 1024])
    rb_q = DI("rb_q", [2, 128, TC])
    rb_k = DI("rb_k", [2, 128, TC])

    yout = nc.dram_tensor("yout", [HC, 128, TC], F32, kind="ExternalOutput")
    dbg = (nc.dram_tensor("dbg", [NDBG, 128, 2048], F32, kind="ExternalOutput")
           if NDBG else None)
    taps = [nc.dram_tensor(f"tap{i+1}", [HC, 128, TC], F32, kind="ExternalOutput")
            for i in range(4)] if TAPS else []

    ag_a_in = nc.dram_tensor("ag_a_in", [576, TC], BF16)
    ag_a_out = nc.dram_tensor("ag_a_out", [1152, TC], BF16)
    ag_bk_in = nc.dram_tensor("ag_bk_in", [2048, TC], BF16)
    ag_bk_out = nc.dram_tensor("ag_bk_out", [4096, TC], BF16)
    ag_bv_in = nc.dram_tensor("ag_bv_in", [NH * TC, 128], BF16)
    ag_bv_out = nc.dram_tensor("ag_bv_out", [2 * NH * TC, 128], BF16)

    with contextlib.ExitStack() as stack:
        E = stack.enter_context

        X = [E(nc.sbuf_tensor(f"x{c}", [128, TC], BF16)) for c in range(HC)]
        LNX = [E(nc.sbuf_tensor(f"lnx{c}", [128, TC], BF16)) for c in range(HC)]
        QN = [E(nc.sbuf_tensor(f"qn{h}", [128, TC], BF16)) for h in range(NH)]
        QP = [E(nc.sbuf_tensor(f"qp{g}", [128, TC], BF16)) for g in range(8)]
        KVH = [E(nc.sbuf_tensor(f"kvh{j}", [128, S], BF16)) for j in range(2)]
        WRB = E(nc.sbuf_tensor("wrb", [128, 16384], BF16))
        ROPS = E(nc.sbuf_tensor("rops", [128, 3072], BF16))
        GSC = E(nc.sbuf_tensor("gsc", [128, 2048], BF16))
        DSC = E(nc.sbuf_tensor("dsc", [128, 1024], BF16))
        F32S = E(nc.sbuf_tensor("f32s", [128, 1024], F32))
        SROWS = E(nc.sbuf_tensor("srows", [1, 2048], F32))
        SCOL = E(nc.sbuf_tensor("scol", [128, 32], F32))
        RTR = E(nc.sbuf_tensor("rtr", [128, 192], F32))
        AM = E(nc.sbuf_tensor("am", [128, 2048], F32))
        IDF = E(nc.sbuf_tensor("idf", [128, 128], F32))
        IDB = E(nc.sbuf_tensor("idb", [128, 128], BF16))
        ONEF = E(nc.sbuf_tensor("onef", [1, 128], F32))
        ONEB = E(nc.sbuf_tensor("oneb", [128, 1], BF16))
        VEC = E(nc.sbuf_tensor("vec", [128, 140], F32))

        PS = [E(nc.psum_tensor(f"ps{i}", [128, 512], F32)) for i in range(6)]
        PT = [E(nc.psum_tensor(f"pt{i}", [128, 1024], BF16)) for i in range(2)]

        sems = {}
        for name in ("w", "wi", "wt", "io", "pe", "dve", "act", "out",
                     "cc0", "cc1", "cc2",
                     "rg0", "rg1", "rg2", "rg3", "rm0", "rm1", "rm2", "rm3"):
            sems[name] = E(nc.semaphore(name))
        pl = Plan(sems)

        RING0 = 0
        RINGM = 4 * 2048
        WTAB = RINGM + 4 * 512
        PTS0 = WTAB + 4 * 1024
        ring = Ring(pl, WRB, RING0, 2048, 4, "rg")
        ringm = Ring(pl, WRB, RINGM, 512, 4, "rm")

        def wtab(i):
            return WRB[:, WTAB + i * 1024:WTAB + (i + 1) * 1024]

        def pts_ap(i, cols=512):
            return WRB[:, PTS0 + i * 512:PTS0 + i * 512 + cols]

        def bias_col(i):
            return VEC[:, i:i + 1]

        pl.run("sp", lambda E: E.dma_start(out=IDF[:, :], in_=identf[:, :]), inc=[("io", 16)])
        pl.run("pool", lambda E: E.dma_start(out=IDB[:, :], in_=identf[:, :]), inc=[("wi", 16)])
        pl.run("sp", lambda E: E.dma_start(out=ONEF[:, :], in_=ones1[:, :]), inc=[("io", 16)])
        pl.run("pool", lambda E: E.dma_start(out=ONEB[:, :], in_=onesc[:, :]), inc=[("wi", 16)])
        pl.run("sp", lambda E: E.dma_start(out=VEC[:, :], in_=vecs[:, :]), inc=[("io", 16)])
        pl.run("sp", lambda E: E.dma_start(
            out=AM[:, :], in_=amask[:, :]), inc=[("io", 16)])
        for c in range(HC):
            pl.run("pool", lambda E, c=c: E.dma_start(out=X[c][:, :], in_=xin[c, :, :]),
                   inc=[("wi", 16)])
        init_io = pl.val("io")
        init_w = pl.val("wi")

        def am_ap(qt, side):
            c0 = (qt * 2 + side) * 128
            return AM[:, c0:c0 + 128]

        dbg_n = [0]

        def dbgdump(ap):
            if not NDBG or dbg_n[0] >= NDBG:
                return
            i = dbg_n[0]
            dbg_n[0] += 1
            for eng_s in ("dve", "act", "pe", "io"):
                pl.wait("pool", eng_s, pl.val(eng_s))
            pl.run("pool", lambda E, ap=ap, i=i: E.dma_start(
                out=dbg[i, 0:ap.shape[0], 0:ap.shape[1]], in_=ap), inc=[("out", 16)])
            for eng_s in ("dve", "act", "pe", "sp", "pool"):
                pl.wait(eng_s, "out", pl.val("out"))

        def layernorm():
            pl.wait("act", "wi", init_w)
            pl.wait("pe", "wi", init_w)
            pl.wait("pe", "io", init_io)
            x2 = [ROPS[:, 0:1024], ROPS[:, 1024:2048]]
            for bk in range(4):
                pl.bank_wait(bk)
            chunk_pe = []
            for c in range(HC):
                if c >= 2:
                    pl.wait("act", "pe", chunk_pe[c - 2])
                pl.run("act", lambda E, c=c: E.activation(x2[c % 2], X[c][:, :], AF.Square),
                       inc=[("act", 1)])
                pl.wait("pe", "act", pl.val("act"))
                for qt in range(2):
                    pl.run("pe", lambda E, c=c, qt=qt: E.matmul(
                        PS[qt][0:1, :], ONEB[:, :], X[c][:, qt * 512:(qt + 1) * 512],
                        start=(c == 0), stop=(c == HC - 1)))
                for qt in range(2):
                    pl.run("pe", lambda E, c=c, qt=qt, t=x2[c % 2]: E.matmul(
                        PS[2 + qt][0:1, :], ONEB[:, :], t[:, qt * 512:(qt + 1) * 512],
                        start=(c == 0), stop=(c == HC - 1)),
                        inc=([("pe", 1)] if qt == 1 else None))
                chunk_pe.append(pl.val("pe"))
            stats_pe = pl.val("pe")
            MR = SROWS[0:1, 0:1024]
            R2 = SROWS[0:1, 1024:2048]
            pl.wait("dve", "pe", stats_pe)
            for qt in range(2):
                pl.run("dve", lambda E, qt=qt: E.tensor_scalar_mul(
                    MR[:, qt * 512:(qt + 1) * 512], PS[qt][0:1, :], 1.0 / H))
            for qt in range(2):
                pl.run("dve", lambda E, qt=qt: E.tensor_mul(
                    R2[:, qt * 512:(qt + 1) * 512], MR[:, qt * 512:(qt + 1) * 512],
                    MR[:, qt * 512:(qt + 1) * 512]))
            for qt in range(2):
                pl.run("dve", lambda E, qt=qt: E.scalar_tensor_tensor(
                    R2[:, qt * 512:(qt + 1) * 512], PS[2 + qt][0:1, :], 1.0 / H,
                    R2[:, qt * 512:(qt + 1) * 512], ALU.mult, ALU.subtract),
                    inc=[("dve", 1)])
            for bk in range(4):
                pl.bank_set(bk, "dve")
            pl.run("dve", lambda E: E.tensor_scalar_add(R2[:, :], R2[:, :], 1e-5),
                   inc=[("dve", 1)])
            pl.wait("act", "dve", pl.val("dve"))
            pl.run("act", lambda E: E.activation(R2[:, :], R2[:, :], AF.Sqrt),
                   inc=[("act", 1)])
            pl.wait("dve", "act", pl.val("act"))
            pl.run("dve", lambda E: E.reciprocal(R2[:, :], R2[:, :]))
            pl.run("dve", lambda E: E.tensor_mul(MR[:, :], MR[:, :], R2[:, :]))
            pl.run("dve", lambda E: E.tensor_scalar_mul(MR[:, :], MR[:, :], -1.0),
                   inc=[("dve", 1)])
            rows = pl.val("dve")
            RBCb = ROPS[:, 0:1024]
            B2Cb = ROPS[:, 1024:2048]
            pl.wait("pe", "dve", rows)
            pl.bank_wait(4)
            pl.bank_wait(5)
            for qt in range(2):
                pl.run("pe", lambda E, qt=qt: E.matmul(
                    PS[4][:, :], ONEF[:, :], R2[:, qt * 512:(qt + 1) * 512],
                    start=True, stop=True))
                pl.run("pe", lambda E, qt=qt: E.matmul(
                    PS[5][:, :], ONEF[:, :], MR[:, qt * 512:(qt + 1) * 512],
                    start=True, stop=True), inc=[("pe", 1)])
                pl.wait("dve", "pe", pl.val("pe"))
                pl.run("dve", lambda E, qt=qt: E.tensor_copy(
                    RBCb[:, qt * 512:(qt + 1) * 512], PS[4][:, :]))
                pl.run("dve", lambda E, qt=qt: E.tensor_copy(
                    B2Cb[:, qt * 512:(qt + 1) * 512], PS[5][:, :]), inc=[("dve", 1)])
                pl.wait("pe", "dve", pl.val("dve"))
            pl.bank_set(4, "dve")
            pl.bank_set(5, "dve")
            for c in range(HC):
                pl.run("dve", lambda E, c=c: E.tensor_mul(LNX[c][:, :], X[c][:, :], RBCb))
                pl.run("dve", lambda E, c=c: E.tensor_add(LNX[c][:, :], LNX[c][:, :], B2Cb),
                       inc=[("dve", 1)])
            return pl.val("dve")

        dsc_hist = []

        def add_to_x(och, qt, psum, mmv, bias_idx=None):
            dcol = (len(dsc_hist) % 2) * 512
            pl.wait("act", "pe", mmv)
            if len(dsc_hist) >= 2:
                pl.wait("act", "dve", dsc_hist[-2])
            if bias_idx is None:
                pl.run("act", lambda E, psum=psum, dcol=dcol: E.activation(
                    DSC[:, dcol:dcol + 512], psum[:, :], AF.Copy), inc=[("act", 1)])
            else:
                pl.run("act", lambda E, psum=psum, dcol=dcol, b=bias_idx: E.activation(
                    DSC[:, dcol:dcol + 512], psum[:, :], AF.Identity, bias=bias_col(b)),
                    inc=[("act", 1)])
            pl.wait("dve", "act", pl.val("act"))
            pl.run("dve", lambda E, och=och, qt=qt, dcol=dcol: E.tensor_add(
                X[och][:, qt * 512:(qt + 1) * 512], X[och][:, qt * 512:(qt + 1) * 512],
                DSC[:, dcol:dcol + 512]), inc=[("dve", 1)])
            dsc_hist.append(pl.val("dve"))
            return "dve"

        def proj_add(blob_fn, kc, src_fn, bias_base=None, pre_pe=None):
            fd = Feeder(ring, [(blob_fn(o), kc * 128) for o in range(16)], 2)
            for och in range(16):
                s = fd.get(och)
                ring.wait_ready("pe", s)
                if pre_pe:
                    for se, v in pre_pe:
                        pl.wait("pe", se, v)
                for qt in range(2):
                    bank = (och % 2) * 2 + qt
                    pl.bank_wait(bank)
                    psum = PS[bank]
                    for k in range(kc):
                        pl.run("pe", lambda E, s=s, k=k, qt=qt, psum=psum: E.matmul(
                            psum[:, :], ring.ap(s, k * 128, 128),
                            src_fn(k, qt), start=(k == 0), stop=(k == kc - 1)),
                            inc=([("pe", 1)] if k == kc - 1 else None))
                    add_to_x(och, qt, psum, pl.val("pe"),
                             bias_idx=(None if bias_base is None else bias_base + och))
                    pl.bank_set(bank, "dve")
                ring.consumed(s)

        def attn_inner(h, scores_ops, vblk_ap, probs_of, dbg_head=False):
            for g in range(2):
                njmax = 4 * g + 4
                for qi in range(4):
                    qt = g * 4 + qi
                    tiles = []
                    bank = 0
                    for (j0, side) in ((0, 0), (8, 1)):
                        off = 0
                        nj = qt + 1
                        while off < nj:
                            n = min(4, nj - off)
                            tiles.append((bank, j0 + off, n,
                                          side if off + n == nj else None))
                            bank += 1
                            off += n
                    if SYNCDBG:
                        pl.barrier()
                    for (bk, j0, n, side) in tiles:
                        pl.bank_wait(bk)
                        scores_ops(qt, j0, n, PS[bk])
                    mmv = pl.val("pe")
                    pl.wait("dve", "pe", mmv)
                    for (bk, j0, n, side) in tiles:
                        if side is not None:
                            c0 = (n - 1) * 128
                            pl.run("dve", lambda E, bk=bk, c0=c0, qt=qt, side=side:
                                   E.tensor_add(PS[bk][:, c0:c0 + 128],
                                                PS[bk][:, c0:c0 + 128],
                                                am_ap(qt, side)))
                    nt = len(tiles)
                    for i, (bk, j0, n, side) in enumerate(tiles):
                        last = (i == nt - 1) and nt == 1
                        pl.run("dve", lambda E, bk=bk, n=n, i=i: E.reduce_max(
                            SCOL[:, i:i + 1], PS[bk][:, 0:n * 128], axis=AX.X,
                            negate=True), inc=([("dve", 1)] if last else None),
                            guard=(i == nt - 1))
                    for i in range(1, nt):
                        pl.run("dve", lambda E, i=i: E.tensor_tensor(
                            SCOL[:, 0:1], SCOL[:, 0:1], SCOL[:, i:i + 1], ALU.min),
                            inc=([("dve", 1)] if i == nt - 1 else None), guard=True)
                    dv = pl.val("dve")
                    if SYNCDBG:
                        pl.barrier()
                    pl.wait("act", "dve", dv)
                    for i, (bk, j0, n, side) in enumerate(tiles):
                        pl.run("act", lambda E, bk=bk, j0=j0, n=n, i=i, qi=qi, qt=qt:
                               E.activation(probs_of(qi, j0, n),
                                            PS[bk][:, 0:n * 128], AF.Exp,
                                            bias=SCOL[:, 0:1],
                                            accum_out=SCOL[:, 8 + i:9 + i]))
                    pl.run("act", lambda E: E.drain(), inc=[("act", 1)])
                    for (bk, j0, n, side) in tiles:
                        pl.bank_set(bk, "act")
                    av = pl.val("act")
                    pl.wait("dve", "act", av)
                    for i in range(1, nt):
                        pl.run("dve", lambda E, i=i: E.tensor_add(
                            SCOL[:, 8:9], SCOL[:, 8:9], SCOL[:, 8 + i:9 + i]),
                            guard=True)
                    pl.run("dve", lambda E, qi=qi: E.reciprocal(
                        SCOL[:, 16 + qi:17 + qi], SCOL[:, 8:9]), inc=[("dve", 1)])
                    if dbg_head and g == 0 and qi == 1:
                        dbgdump(SCOL[:, 0:32])
                if dbg_head and g == 0:
                    dbgdump(LNX[0][:, :])
                    dbgdump(LNX[1][:, :])
                    dbgdump(LNX[2][:, :])
                    dbgdump(LNX[3][:, :])
                av_started = [False] * 4
                probs_dve = pl.val("dve")
                for qi in range(4):
                    pl.bank_wait(qi)
                for jj in range(2 * njmax):
                    j = (jj // 2) if jj % 2 == 0 else 8 + (jj - 1) // 2
                    jq = j % 8
                    if SYNCDBG:
                        pl.barrier()
                    pl.wait("pe", "act", pl.val("act"))
                    trs = [qi for qi in range(4) if jq <= g * 4 + qi]
                    pl.bank_wait(6 + jj % 2)
                    for idx, qi in enumerate(trs):
                        pl.run("pe", lambda E, qi=qi, j=j, jj=jj: E.transpose(
                            PT[jj % 2][:, qi * 128:(qi + 1) * 128],
                            probs_of(qi, j, 1), IDB[:, :]))
                    pl.run("pe", lambda E: E.drain(), inc=[("pe", 1)])
                    trv = pl.val("pe")
                    pl.wait("act", "pe", trv)
                    pl.run("act", lambda E, jj=jj: E.activation(
                        pts_ap(jj % 2), PT[jj % 2][:, 0:512], AF.Copy),
                        inc=[("act", 1)])
                    pl.bank_set(6 + jj % 2, "act")
                    if dbg_head and g == 0 and jj in (0, 1):
                        dbgdump(pts_ap(jj % 2))
                    ev = pl.val("act")
                    if SYNCDBG:
                        pl.barrier()
                    pl.wait("pe", "act", ev)
                    for qi in trs:
                        qt = g * 4 + qi
                        st_ = not av_started[qi]
                        sp_ = (j == 8 + qt)
                        pl.run("pe", lambda E, qi=qi, j=j, jj=jj, st_=st_, sp_=sp_:
                               E.matmul(
                                   PS[qi][:, 0:128],
                                   pts_ap(jj % 2)[:, qi * 128:qi * 128 + 128],
                                   vblk_ap(j),
                                   start=st_, stop=sp_),
                               inc=([("pe", 1)] if sp_ else None))
                        av_started[qi] = True
                avv = pl.val("pe")
                if SYNCDBG:
                    pl.barrier()
                pl.wait("dve", "pe", avv)
                if dbg_head and g == 0:
                    for qi2 in range(4):
                        pl.run("dve", lambda E, qi2=qi2: E.tensor_copy(
                            ROPS[:, 2048 + qi2 * 128:2048 + qi2 * 128 + 128]
                            if False else GSC[:, 1024 + qi2 * 128:1024 + qi2 * 128 + 128],
                            PS[qi2][:, 0:128]), inc=[("dve", 1)])
                    dbgdump(GSC[:, 1024:1536])
                    dbgdump(SCOL[:, 0:32])
                for qi in range(4):
                    pl.run("dve", lambda E, qi=qi: E.tensor_scalar_mul(
                        GSC[:, qi * 128:(qi + 1) * 128], PS[qi][:, 0:128],
                        SCOL[:, 16 + qi:17 + qi]), inc=[("dve", 1)])
                    pl.bank_set(qi, "dve")
                odv = pl.val("dve")
                if SYNCDBG:
                    pl.barrier()
                if dbg_head and g == 0:
                    dbgdump(GSC[:, 0:512])
                pl.wait("pe", "dve", odv)
                pl.bank_wait(6)
                for qi in range(4):
                    pl.run("pe", lambda E, qi=qi: E.transpose(
                        PT[0][:, qi * 128:(qi + 1) * 128],
                        GSC[:, qi * 128:(qi + 1) * 128], IDB[:, :]))
                pl.run("pe", lambda E: E.drain(), inc=[("pe", 1)])
                pl.wait("act", "pe", pl.val("pe"))
                pl.run("act", lambda E, h=h, g=g: E.activation(
                    QN[h][:, g * 512:(g + 1) * 512], PT[0][:, 0:512], AF.Copy),
                    inc=[("act", 1)])
                pl.bank_set(6, "act")

        def attn_a():
            lnx_dve = layernorm()
            dbgdump(LNX[0][:, :])
            q2scr = [QN[4][:, 0:1024], QN[5][:, 0:1024]]
            ckv_own = [QN[0], QN[1], QN[2], QN[3]]

            pl.bank_wait(4)
            fd = Feeder(ring, [(wkvac[oi], 2048) for oi in range(4)], 2)
            for oi in range(4):
                s = fd.get(oi)
                ring.wait_ready("pe", s)
                pl.wait("pe", "dve", lnx_dve)
                for qt in range(2):
                    bank = (oi % 2) * 2 + qt
                    pl.bank_wait(bank)
                    psum = PS[bank]
                    for k in range(HC):
                        pl.run("pe", lambda E, s=s, k=k, qt=qt, psum=psum: E.matmul(
                            psum[:, :], ring.ap(s, k * 128, 128),
                            LNX[k][:, qt * 512:(qt + 1) * 512],
                            start=(k == 0), stop=(k == HC - 1)),
                            inc=([("pe", 1)] if k == HC - 1 else None))
                    mmv = pl.val("pe")
                    pl.wait("act", "pe", mmv)
                    pl.run("act", lambda E, oi=oi, qt=qt, psum=psum: E.activation(
                        ckv_own[oi][:, qt * 512:(qt + 1) * 512], psum[:, :], AF.Copy,
                        scale=bias_col(12 + oi)))
                    pl.run("act", lambda E, oi=oi, qt=qt, psum=psum: E.activation(
                        q2scr[qt][:, (oi % 2) * 512:(oi % 2) * 512 + 512], psum[:, :],
                        AF.Square), inc=[("act", 1)])
                    pl.bank_set(bank, "act")
                    pl.wait("pe", "act", pl.val("act"))
                    pl.run("pe", lambda E, oi=oi, qt=qt: E.matmul(
                        PS[4 + qt][0:1, :], ONEB[:, :],
                        q2scr[qt][:, (oi % 2) * 512:(oi % 2) * 512 + 512],
                        start=(oi == 0), stop=(oi == 3)),
                        inc=([("pe", 1)] if oi == 3 else None))
                ring.consumed(s)
            ckv_stats_pe = pl.val("pe")

            CKR = SROWS[0:1, 0:1024]
            pl.wait("dve", "pe", ckv_stats_pe)
            for qt in range(2):
                pl.run("dve", lambda E, qt=qt: E.tensor_scalar(
                    CKR[:, qt * 512:(qt + 1) * 512], PS[4 + qt][0:1, :],
                    1.0 / KVL, 1e-6, ALU.mult, ALU.add),
                    inc=([("dve", 1)] if qt == 1 else None))
            pl.bank_set(4, "dve")
            pl.bank_set(5, "dve")
            pl.wait("act", "dve", pl.val("dve"))
            pl.run("act", lambda E: E.activation(CKR[:, :], CKR[:, :], AF.Sqrt),
                   inc=[("act", 1)])
            pl.wait("dve", "act", pl.val("act"))
            pl.run("dve", lambda E: E.reciprocal(CKR[:, :], CKR[:, :]), inc=[("dve", 1)])
            pl.bank_set(4, "dve")
            pl.wait("pe", "dve", pl.val("dve"))
            CKB = ROPS[:, 0:1024]
            pl.bank_wait(4)
            for qt in range(2):
                pl.run("pe", lambda E, qt=qt: E.matmul(
                    PS[4][:, :], ONEF[:, :], CKR[:, qt * 512:(qt + 1) * 512],
                    start=True, stop=True), inc=[("pe", 1)])
                pl.wait("dve", "pe", pl.val("pe"))
                pl.run("dve", lambda E, qt=qt: E.tensor_copy(
                    CKB[:, qt * 512:(qt + 1) * 512], PS[4][:, :]), inc=[("dve", 1)])
                pl.wait("pe", "dve", pl.val("dve"))
            pl.bank_set(4, "dve")
            for oi in range(4):
                pl.run("dve", lambda E, oi=oi: E.tensor_mul(
                    ckv_own[oi][:, :], ckv_own[oi][:, :], CKB[:, :]), inc=[("dve", 1)])
            ckv_rms_dve = pl.val("dve")
            dbgdump(QN[0][:, :])

            s = ring.load(wkvap[:, :], cols=1024)
            ring.wait_ready("pe", s)
            for qt in range(2):
                pl.bank_wait(5)
                for k in range(HC):
                    pl.run("pe", lambda E, s=s, k=k, qt=qt: E.matmul(
                        PS[5][0:64, :], ring.ap(s, k * 64, 64),
                        LNX[k][:, qt * 512:(qt + 1) * 512],
                        start=(k == 0), stop=(k == HC - 1)),
                        inc=([("pe", 1)] if k == HC - 1 else None))
                pl.wait("act", "pe", pl.val("pe"))
                pl.run("act", lambda E, qt=qt: E.activation(
                    QN[6][0:64, qt * 512:(qt + 1) * 512], PS[5][0:64, :], AF.Copy),
                    inc=[("act", 1)])
                pl.bank_set(5, "act")
            ring.consumed(s)
            kpe_act = pl.val("act")

            pl.wait("sp", "dve", ckv_rms_dve)
            for oi in range(4):
                pl.run("sp", lambda E, oi=oi: E.dma_start(
                    out=ag_a_in[oi * 128:(oi + 1) * 128, :], in_=ckv_own[oi][:, :]),
                    inc=[("io", 16)])
            pl.wait("sp", "act", kpe_act)
            pl.run("sp", lambda E: E.dma_start(
                out=ag_a_in[512:576, :], in_=QN[6][0:64, 0:1024]), inc=[("io", 16)])
            a_store_io = pl.val("io")
            pl.wait("pool", "io", a_store_io)
            pl.run("pool", lambda E: E.collective_compute(
                "AllGather", ALU.bypass,
                replica_groups=[[0, 1], [2, 3], [4, 5], [6, 7]],
                ins=[ag_a_in.ap()], outs=[ag_a_out.ap()]), inc=[("cc0", 1)])

            def ql_tile(i):
                if i < 8:
                    return QP[i]
                return KVH[(i - 8) // 2][:, ((i - 8) % 2) * 1024:((i - 8) % 2 + 1) * 1024]
            pl.bank_wait(4)
            pl.bank_wait(5)
            fd = Feeder(ring, [(wqa[oi], 2048) for oi in range(12)], 2)
            for oi in range(12):
                s = fd.get(oi)
                ring.wait_ready("pe", s)
                for qt in range(2):
                    bank = (oi % 2) * 2 + qt
                    pl.bank_wait(bank)
                    psum = PS[bank]
                    for k in range(HC):
                        pl.run("pe", lambda E, s=s, k=k, qt=qt, psum=psum: E.matmul(
                            psum[:, :], ring.ap(s, k * 128, 128),
                            LNX[k][:, qt * 512:(qt + 1) * 512],
                            start=(k == 0), stop=(k == HC - 1)),
                            inc=([("pe", 1)] if k == HC - 1 else None))
                    mmv = pl.val("pe")
                    pl.wait("act", "pe", mmv)
                    pl.run("act", lambda E, oi=oi, qt=qt, psum=psum: E.activation(
                        ql_tile(oi)[:, qt * 512:(qt + 1) * 512], psum[:, :], AF.Copy,
                        scale=bias_col(oi)))
                    pl.run("act", lambda E, oi=oi, qt=qt, psum=psum: E.activation(
                        q2scr[qt][:, (oi % 2) * 512:(oi % 2) * 512 + 512], psum[:, :],
                        AF.Square), inc=[("act", 1)])
                    pl.bank_set(bank, "act")
                    pl.wait("pe", "act", pl.val("act"))
                    pl.run("pe", lambda E, oi=oi, qt=qt: E.matmul(
                        PS[4 + qt][0:1, :], ONEB[:, :],
                        q2scr[qt][:, (oi % 2) * 512:(oi % 2) * 512 + 512],
                        start=(oi == 0), stop=(oi == 11)),
                        inc=([("pe", 1)] if oi == 11 else None))
                ring.consumed(s)
            ql_stats_pe = pl.val("pe")

            QR = SROWS[0:1, 1024:2048]
            pl.wait("dve", "pe", ql_stats_pe)
            for qt in range(2):
                pl.run("dve", lambda E, qt=qt: E.tensor_scalar(
                    QR[:, qt * 512:(qt + 1) * 512], PS[4 + qt][0:1, :],
                    1.0 / QLORA, 1e-6, ALU.mult, ALU.add),
                    inc=([("dve", 1)] if qt == 1 else None))
            pl.bank_set(4, "dve")
            pl.bank_set(5, "dve")
            pl.wait("act", "dve", pl.val("dve"))
            pl.run("act", lambda E: E.activation(QR[:, :], QR[:, :], AF.Sqrt),
                   inc=[("act", 1)])
            pl.wait("dve", "act", pl.val("dve") if False else pl.val("act"))
            pl.run("dve", lambda E: E.reciprocal(QR[:, :], QR[:, :]), inc=[("dve", 1)])
            pl.bank_set(5, "dve")
            pl.wait("pe", "dve", pl.val("dve"))
            QRB = F32S[:, 0:1024]
            pl.bank_wait(4)
            for qt in range(2):
                pl.run("pe", lambda E, qt=qt: E.matmul(
                    PS[4][:, :], ONEF[:, :], QR[:, qt * 512:(qt + 1) * 512],
                    start=True, stop=True), inc=[("pe", 1)])
                pl.wait("dve", "pe", pl.val("pe"))
                pl.run("dve", lambda E, qt=qt: E.tensor_copy(
                    QRB[:, qt * 512:(qt + 1) * 512], PS[4][:, :]), inc=[("dve", 1)])
                pl.wait("pe", "dve", pl.val("dve"))
            pl.bank_set(4, "dve")

            RQC = wtab(0)
            RQS = wtab(1)
            pl.run("pool", lambda E: E.dma_start(out=RQC, in_=ra_q[0]), inc=[("wt", 16)])
            pl.run("pool", lambda E: E.dma_start(out=RQS, in_=ra_q[1]), inc=[("wt", 16)])
            rq_w = pl.val("wt")
            pl.wait("dve", "io", a_store_io)
            fd = Feeder(ring, [(wqb[oi], 1536) for oi in range(24)], 2)
            for oi in range(24):
                s = fd.get(oi)
                ring.wait_ready("pe", s)
                for qt in range(2):
                    bank = (oi % 2) * 2 + qt
                    pl.bank_wait(bank)
                    psum = PS[bank]
                    for k in range(12):
                        pl.run("pe", lambda E, s=s, k=k, qt=qt, psum=psum: E.matmul(
                            psum[:, :], ring.ap(s, k * 128, 128),
                            ql_tile(k)[:, qt * 512:(qt + 1) * 512],
                            start=(k == 0), stop=(k == 11)),
                            inc=([("pe", 1)] if k == 11 else None))
                    mmv = pl.val("pe")
                    pl.wait("dve", "pe", mmv)
                    if oi < 16:
                        pl.run("dve", lambda E, oi=oi, qt=qt, psum=psum:
                               E.scalar_tensor_tensor(
                                   QN[oi][:, qt * 512:(qt + 1) * 512], psum[:, :], SA,
                                   QRB[:, qt * 512:(qt + 1) * 512], ALU.mult, ALU.mult),
                               inc=[("dve", 1)])
                    else:
                        pl.run("dve", lambda E, qt=qt, psum=psum: E.tensor_mul(
                            ROPS[:, 1024 + qt * 512:1024 + qt * 512 + 512],
                            psum[:, :], QRB[:, qt * 512:(qt + 1) * 512]),
                            inc=[("dve", 1)])
                    pl.bank_set(bank, "dve")
                if oi >= 16:
                    g = oi - 16
                    dst = LNX[8 + g]
                    src = ROPS[:, 1024:2048]
                    rot = ROPS[:, 0:1024]
                    pl.wait("dve", "wt", rq_w)
                    pl.run("dve", lambda E: E.tensor_scalar_mul(rot[0:32, :], src[32:64, :], -1.0))
                    pl.run("dve", lambda E: E.tensor_copy(rot[32:64, :], src[0:32, :]))
                    pl.run("dve", lambda E: E.tensor_scalar_mul(rot[64:96, :], src[96:128, :], -1.0))
                    pl.run("dve", lambda E: E.tensor_copy(rot[96:128, :], src[64:96, :]))
                    pl.run("dve", lambda E, dst=dst: E.tensor_mul(dst[:, :], src[:, :], RQC))
                    pl.run("dve", lambda E: E.tensor_mul(rot[:, :], rot[:, :], RQS))
                    pl.run("dve", lambda E, dst=dst: E.tensor_add(dst[:, :], dst[:, :], rot[:, :]),
                           inc=[("dve", 1)])
                ring.consumed(s)
            q_dve = pl.val("dve")
            dbgdump(QN[0][:, :])
            dbgdump(LNX[8][:, :])
            dbgdump(ql_tile(0)[:, :])
            dbgdump(QRB)

            pl.wait("sp", "cc0", 1)
            pl.wait("sp", "pe", pl.val("pe"))
            for oi in range(4):
                for r in range(2):
                    pl.run("sp", lambda E, oi=oi, r=r: E.dma_start(
                        out=QP[2 * oi + r][:, :],
                        in_=ag_a_out[r * 576 + oi * 128:r * 576 + (oi + 1) * 128, :]),
                        inc=[("io", 16)])
            pl.wait("pool", "dve", q_dve)
            for i in range(4):
                pl.run("pool", lambda E, i=i: E.dma_start(out=wtab(i), in_=ra_k[i]),
                       inc=[("wt", 16)])
            rk_w = pl.val("wt")
            kw = ROPS[:, 2048:3072]
            rot = ROPS[:, 0:1024]
            t1 = ROPS[:, 1024:2048]
            for r in range(2):
                pl.wait("sp", "dve", pl.val("dve"))
                pl.run("sp", lambda E, r=r: E.dma_start(
                    out=kw[0:64, 0:1024], in_=ag_a_out[r * 576 + 512:r * 576 + 576, :]),
                    inc=[("io", 16)])
                kio = pl.val("io")
                pl.wait("dve", "io", kio)
                pl.wait("dve", "wt", rk_w)
                ctab = wtab(0 + r)
                stab = wtab(2 + r)
                pl.run("dve", lambda E: E.tensor_scalar_mul(rot[0:32, :], kw[32:64, 0:1024], -1.0))
                pl.run("dve", lambda E: E.tensor_copy(rot[32:64, :], kw[0:32, 0:1024]))
                pl.run("dve", lambda E, ctab=ctab: E.tensor_mul(
                    t1[0:64, :], kw[0:64, 0:1024], ctab[0:64, :]))
                pl.run("dve", lambda E, stab=stab: E.tensor_mul(
                    rot[0:64, :], rot[0:64, :], stab[0:64, :]))
                pl.run("dve", lambda E, r=r: E.tensor_add(
                    KVH[0][0:64, r * 1024:(r + 1) * 1024], t1[0:64, :], rot[0:64, :]))
                pl.run("dve", lambda E, r=r: E.tensor_copy(
                    KVH[0][64:128, r * 1024:(r + 1) * 1024],
                    KVH[0][0:64, r * 1024:(r + 1) * 1024]), inc=[("dve", 1)])
            kpe_dve = pl.val("dve")
            ckv_io = pl.val("io")
            dbgdump(KVH[0][0:64, :])
            dbgdump(QP[0][:, :])

            srcs = []
            for h in range(NH):
                srcs.append((wkvb[h], 512))
                srcs.append((wkvb[16 + h], 512))
            fdm = Feeder(ringm, srcs, 2)
            for h in range(NH):
                sk_ = fdm.get(2 * h)
                sv_ = fdm.get(2 * h + 1)
                ringm.wait_ready("pe", sv_)
                ringm.wait_ready("pe", sk_)
                pl.wait("pe", "io", ckv_io)
                pl.wait("pe", "dve", kpe_dve)
                for tk in range(4):
                    pl.bank_wait(4)
                    for k in range(4):
                        pl.run("pe", lambda E, sk_=sk_, k=k, tk=tk: E.matmul(
                            PS[4][:, :], ringm.ap(sk_, k * 128, 128),
                            QP[2 * k + tk // 2][:, (tk % 2) * 512:(tk % 2) * 512 + 512],
                            start=(k == 0), stop=(k == 3)),
                            inc=([("pe", 1)] if k == 3 else None))
                    pl.wait("act", "pe", pl.val("pe"))
                    pl.run("act", lambda E, tk=tk: E.activation(
                        KVH[1][:, tk * 512:(tk + 1) * 512], PS[4][:, :], AF.Copy),
                        inc=[("act", 1)])
                    pl.bank_set(4, "act")
                for jq in range(4):
                    pl.bank_wait(5)
                    for jj4 in range(4):
                        j = jq * 4 + jj4
                        for k in range(4):
                            pl.run("pe", lambda E, sv_=sv_, k=k, j=j, jj4=jj4: E.matmul(
                                PS[5][:, jj4 * 128:(jj4 + 1) * 128],
                                QP[2 * k + j // 8][:, (j % 8) * 128:(j % 8) * 128 + 128],
                                ringm.ap(sv_, k * 128, 128),
                                start=(k == 0), stop=(k == 3)),
                                inc=([("pe", 1)] if (k == 3 and jj4 == 3) else None))
                    pl.wait("act", "pe", pl.val("pe"))
                    pl.run("act", lambda E, jq=jq: E.activation(
                        ROPS[:, jq * 512:(jq + 1) * 512], PS[5][:, :], AF.Copy),
                        inc=[("act", 1)])
                    pl.bank_set(5, "act")
                kv_act = pl.val("act")
                ringm.consumed(sk_)
                ringm.consumed(sv_)
                pl.wait("pe", "act", kv_act)

                def scores_ops(qt, j0, nj, psum, h=h):
                    pl.run("pe", lambda E, qt=qt, j0=j0, nj=nj, psum=psum, h=h: E.matmul(
                        psum[:, 0:nj * 128], QN[h][:, qt * 128:(qt + 1) * 128],
                        KVH[1][:, j0 * 128:(j0 + nj) * 128],
                        start=True, stop=False))
                    hb = 64 * (h % 2)
                    pl.run("pe", lambda E, qt=qt, j0=j0, nj=nj, psum=psum, h=h, hb=hb:
                           E.matmul(
                               psum[:, 0:nj * 128],
                               LNX[8 + h // 2][hb:hb + 64, qt * 128:(qt + 1) * 128],
                               KVH[0][hb:hb + 64, j0 * 128:(j0 + nj) * 128],
                               start=False, stop=True), inc=[("pe", 1)])

                def vblk_ap(j):
                    return ROPS[:, j * 128:(j + 1) * 128]

                def probs_of(qi, j0, nj):
                    t = LNX[2 * qi] if j0 < 8 else LNX[2 * qi + 1]
                    c0 = (j0 % 8) * 128
                    return t[:, c0:c0 + nj * 128]

                if h == 0:
                    dbgdump(KVH[1][:, :])
                    dbgdump(ROPS[:, 0:2048])
                attn_inner(h, scores_ops, vblk_ap, probs_of,
                           dbg_head=(h == 0 and NDBG > 0))
                if h == 0:
                    dbgdump(QN[0][:, :])

            proj_add(lambda o: wo[o], 16,
                     lambda k, qt: QN[k][:, qt * 512:(qt + 1) * 512],
                     pre_pe=[("act", pl.val("act"))])

        def ffn():
            lnx_dve = layernorm()

            s = ring.load(rw[:, :], cols=128)
            ring.wait_ready("pe", s)
            pl.wait("pe", "dve", lnx_dve)
            LGF = F32S[0:8, 0:1024]
            for qt in range(2):
                pl.bank_wait(4)
                for k in range(HC):
                    pl.run("pe", lambda E, s=s, k=k, qt=qt: E.matmul(
                        PS[4][0:8, :], ring.ap(s, k * 8, 8),
                        LNX[k][:, qt * 512:(qt + 1) * 512],
                        start=(k == 0), stop=(k == HC - 1)),
                        inc=([("pe", 1)] if k == HC - 1 else None))
                pl.wait("act", "pe", pl.val("pe"))
                pl.run("act", lambda E, qt=qt: E.activation(
                    LGF[:, qt * 512:(qt + 1) * 512], PS[4][0:8, :], AF.Copy),
                    inc=[("act", 1)])
                pl.bank_set(4, "act")
            ring.consumed(s)
            lg_act = pl.val("act")
            pl.wait("pe", "act", lg_act)
            pl.bank_wait(5)
            for half in range(2):
                for b4 in range(4):
                    blk = half * 4 + b4
                    pl.run("pe", lambda E, blk=blk, b4=b4: E.transpose(
                        PS[5][:, b4 * 8:b4 * 8 + 8],
                        LGF[:, blk * 128:(blk + 1) * 128], IDF[0:8, 0:8]))
                pl.run("pe", lambda E: E.drain(), inc=[("pe", 1)])
                pl.wait("dve", "pe", pl.val("pe"))
                for b4 in range(4):
                    blk = half * 4 + b4
                    pl.run("dve", lambda E, blk=blk, b4=b4: E.tensor_copy(
                        RTR[:, blk * 8:blk * 8 + 8], PS[5][:, b4 * 8:b4 * 8 + 8]),
                        inc=([("dve", 1)] if b4 == 3 else None))
                pl.wait("pe", "dve", pl.val("dve"))
            pl.bank_set(5, "dve")
            for blk in range(8):
                P8 = RTR[:, blk * 8:blk * 8 + 8]
                CMB = RTR[:, 64 + blk * 8:64 + blk * 8 + 8]
                T8 = RTR[:, 128 + blk * 8:128 + blk * 8 + 8]
                pl.run("dve", lambda E, P8=P8: E.reduce_max(
                    SCOL[:, 0:1], P8, axis=AX.X, negate=True), inc=[("dve", 1)], guard=True)
                pl.wait("act", "dve", pl.val("dve"))
                pl.run("act", lambda E, P8=P8: E.activation(
                    P8, P8, AF.Exp, bias=SCOL[:, 0:1], accum_out=SCOL[:, 1:2]))
                pl.run("act", lambda E: E.drain(), inc=[("act", 1)])
                pl.wait("dve", "act", pl.val("act"))
                pl.run("dve", lambda E: E.reciprocal(SCOL[:, 2:3], SCOL[:, 1:2]), guard=True)
                pl.run("dve", lambda E, P8=P8: E.tensor_scalar_mul(P8, P8, SCOL[:, 2:3]), guard=True)
                pl.run("dve", lambda E, P8=P8: E.reduce_max(SCOL[:, 3:4], P8, axis=AX.X), guard=True)
                pl.run("dve", lambda E, P8=P8, CMB=CMB: E.tensor_scalar(
                    CMB, P8, SCOL[:, 3:4], None, ALU.is_ge), guard=True)
                pl.run("dve", lambda E, CMB=CMB, T8=T8: E.tensor_scalar(
                    T8, CMB, -1.0, 1.0, ALU.mult, ALU.add), guard=True)
                pl.run("dve", lambda E, P8=P8, T8=T8: E.tensor_mul(T8, T8, P8), guard=True)
                pl.run("dve", lambda E, T8=T8: E.reduce_max(SCOL[:, 4:5], T8, axis=AX.X), guard=True)
                pl.run("dve", lambda E, T8=T8: E.tensor_scalar(
                    T8, T8, SCOL[:, 4:5], None, ALU.is_ge), guard=True)
                pl.run("dve", lambda E: E.tensor_add(SCOL[:, 5:6], SCOL[:, 3:4], SCOL[:, 4:5]), guard=True)
                pl.run("dve", lambda E: E.reciprocal(SCOL[:, 5:6], SCOL[:, 5:6]), guard=True)
                pl.run("dve", lambda E: E.tensor_mul(SCOL[:, 6:7], SCOL[:, 3:4], SCOL[:, 5:6]), guard=True)
                pl.run("dve", lambda E: E.tensor_mul(SCOL[:, 7:8], SCOL[:, 4:5], SCOL[:, 5:6]), guard=True)
                pl.run("dve", lambda E, CMB=CMB: E.tensor_scalar_mul(CMB, CMB, SCOL[:, 6:7]), guard=True)
                pl.run("dve", lambda E, CMB=CMB, T8=T8: E.scalar_tensor_tensor(
                    CMB, T8, SCOL[:, 7:8], CMB, ALU.mult, ALU.add), inc=[("dve", 1)])
            cmb_dve = pl.val("dve")

            CBCb = ROPS[:, 2048:3072]
            CROW = SROWS[0:1, 0:1024]

            def compute_cbc(e):
                pl.wait("pe", "dve", cmb_dve)
                for half in range(2):
                    pl.bank_wait(4)
                    for b4 in range(4):
                        blk = half * 4 + b4
                        pl.run("pe", lambda E, blk=blk, b4=b4, e=e: E.transpose(
                            PS[4][0:1, b4 * 128:(b4 + 1) * 128],
                            RTR[:, 64 + blk * 8 + e:64 + blk * 8 + e + 1], IDF[:, :]))
                    pl.run("pe", lambda E: E.drain(), inc=[("pe", 1)])
                    pl.wait("dve", "pe", pl.val("pe"))
                    pl.run("dve", lambda E, half=half: E.tensor_copy(
                        CROW[:, half * 512:(half + 1) * 512], PS[4][0:1, 0:512]),
                        inc=[("dve", 1)])
                    pl.bank_set(4, "dve")
                pl.wait("pe", "dve", pl.val("dve"))
                for qt in range(2):
                    pl.bank_wait(4)
                    pl.run("pe", lambda E, qt=qt: E.matmul(
                        PS[4][:, :], ONEF[:, :], CROW[:, qt * 512:(qt + 1) * 512],
                        start=True, stop=True), inc=[("pe", 1)])
                    pl.wait("dve", "pe", pl.val("pe"))
                    pl.run("dve", lambda E, qt=qt: E.tensor_copy(
                        CBCb[:, qt * 512:(qt + 1) * 512], PS[4][:, :]), inc=[("dve", 1)])
                    pl.bank_set(4, "dve")
                    pl.wait("pe", "dve", pl.val("dve"))

            def gu_tile(i, ei):
                if ei == 0:
                    return QN[i]
                return QN[11 + i] if i < 5 else QP[i - 5]

            def gate_up(e, ei, with_bias, dst_fn):
                noch = FIC if not with_bias else FSC
                srcs = []
                for oi in range(noch):
                    srcs.append(((gw[e, oi] if not with_bias else sgw[oi]), 2048))
                    srcs.append(((uw[e, oi] if not with_bias else suw[oi]), 2048))
                fd = Feeder(ring, srcs, 2)
                for oi in range(noch):
                    sg_ = fd.get(2 * oi)
                    su_ = fd.get(2 * oi + 1)
                    ring.wait_ready("pe", su_)
                    ring.wait_ready("pe", sg_)
                    pl.wait("pe", "dve", lnx_dve)
                    for qt in range(2):
                        bank = qt
                        bank2 = 2 + qt
                        pl.bank_wait(bank)
                        for k in range(HC):
                            pl.run("pe", lambda E, sg_=sg_, k=k, qt=qt, bank=bank: E.matmul(
                                PS[bank][:, :], ring.ap(sg_, k * 128, 128),
                                LNX[k][:, qt * 512:(qt + 1) * 512],
                                start=(k == 0), stop=(k == HC - 1)),
                                inc=([("pe", 1)] if k == HC - 1 else None))
                        gate_pe = pl.val("pe")
                        pl.bank_wait(bank2)
                        for k in range(HC):
                            pl.run("pe", lambda E, su_=su_, k=k, qt=qt, bank2=bank2: E.matmul(
                                PS[bank2][:, :], ring.ap(su_, k * 128, 128),
                                LNX[k][:, qt * 512:(qt + 1) * 512],
                                start=(k == 0), stop=(k == HC - 1)),
                                inc=([("pe", 1)] if k == HC - 1 else None))
                        up_pe = pl.val("pe")
                        gcol = qt * 512
                        pl.wait("act", "pe", gate_pe)
                        if with_bias:
                            pl.run("act", lambda E, bank=bank, gcol=gcol, oi=oi: E.activation(
                                GSC[:, gcol:gcol + 512], PS[bank][:, :], AF.Silu,
                                bias=bias_col(80 + oi)), inc=[("act", 1)])
                        else:
                            pl.run("act", lambda E, bank=bank, gcol=gcol: E.activation(
                                GSC[:, gcol:gcol + 512], PS[bank][:, :], AF.Silu),
                                inc=[("act", 1)])
                        pl.bank_set(bank, "act")
                        pl.wait("act", "pe", up_pe)
                        pl.run("act", lambda E, bank2=bank2, gcol=gcol: E.activation(
                            GSC[:, 1024 + gcol:1024 + gcol + 512], PS[bank2][:, :],
                            AF.Copy), inc=[("act", 1)])
                        pl.bank_set(bank2, "act")
                        pl.wait("dve", "act", pl.val("act"))
                        dst_fn(oi, qt, gcol)
                    ring.consumed(sg_)
                    ring.consumed(su_)

            def expert_pair(p):
                for ei in range(2):
                    e = 2 * p + ei
                    compute_cbc(e)

                    def dst_fn(oi, qt, gcol, ei=ei):
                        pl.run("dve", lambda E, gcol=gcol, qt=qt: E.tensor_mul(
                            GSC[:, 1024 + gcol:1024 + gcol + 512],
                            GSC[:, 1024 + gcol:1024 + gcol + 512],
                            CBCb[:, qt * 512:(qt + 1) * 512]))
                        pl.run("dve", lambda E, oi=oi, qt=qt, gcol=gcol, ei=ei: E.tensor_mul(
                            gu_tile(oi, ei)[:, qt * 512:(qt + 1) * 512],
                            GSC[:, 1024 + gcol:1024 + gcol + 512],
                            GSC[:, gcol:gcol + 512]), inc=[("dve", 1)])
                    gate_up(e, ei, False, dst_fn)
                gu_dve = pl.val("dve")

                srcs = []
                for och in range(16):
                    srcs.append((dwt[2 * p, och], FI))
                    srcs.append((dwt[2 * p + 1, och], FI))
                fd = Feeder(ring, srcs, 2)
                for och in range(16):
                    sd0 = fd.get(2 * och)
                    sd1 = fd.get(2 * och + 1)
                    ring.wait_ready("pe", sd1)
                    ring.wait_ready("pe", sd0)
                    pl.wait("pe", "dve", gu_dve)
                    for qt in range(2):
                        bank = (och % 2) * 2 + qt
                        pl.bank_wait(bank)
                        psum = PS[bank]
                        for ei in range(2):
                            sd = sd0 if ei == 0 else sd1
                            for k in range(FIC):
                                pl.run("pe", lambda E, sd=sd, k=k, qt=qt, psum=psum, ei=ei:
                                       E.matmul(psum[:, :], ring.ap(sd, k * 128, 128),
                                                gu_tile(k, ei)[:, qt * 512:(qt + 1) * 512],
                                                start=(ei == 0 and k == 0),
                                                stop=(ei == 1 and k == FIC - 1)),
                                       inc=([("pe", 1)] if (ei == 1 and k == FIC - 1)
                                            else None))
                        add_to_x(och, qt, psum, pl.val("pe"))
                        pl.bank_set(bank, "dve")
                    ring.consumed(sd0)
                    ring.consumed(sd1)

            for p in range(4):
                expert_pair(p)

            def sgu_tile(i):
                return QN[i] if i < 16 else QP[i - 16]

            def sdst_fn(oi, qt, gcol):
                pl.run("dve", lambda E, oi=oi, qt=qt, gcol=gcol: E.scalar_tensor_tensor(
                    sgu_tile(oi)[:, qt * 512:(qt + 1) * 512],
                    GSC[:, 1024 + gcol:1024 + gcol + 512], bias_col(102 + oi),
                    GSC[:, gcol:gcol + 512], ALU.add, ALU.mult), inc=[("dve", 1)])
            gate_up(0, 0, True, sdst_fn)
            sgu_dve = pl.val("dve")
            srcs = []
            for och in range(16):
                srcs.append((sdw[och][:, 0:2048], 2048))
                srcs.append((sdw[och][:, 2048:2816], 768))
            fd = Feeder(ring, srcs, 2)
            for och in range(16):
                sda = fd.get(2 * och)
                sdb = fd.get(2 * och + 1)
                ring.wait_ready("pe", sdb)
                ring.wait_ready("pe", sda)
                pl.wait("pe", "dve", sgu_dve)
                for qt in range(2):
                    bank = (och % 2) * 2 + qt
                    pl.bank_wait(bank)
                    psum = PS[bank]
                    for k in range(FSC):
                        sd = sda if k < 16 else sdb
                        kk = k if k < 16 else k - 16
                        pl.run("pe", lambda E, sd=sd, kk=kk, k=k, qt=qt, psum=psum: E.matmul(
                            psum[:, :], ring.ap(sd, kk * 128, 128),
                            sgu_tile(k)[:, qt * 512:(qt + 1) * 512],
                            start=(k == 0), stop=(k == FSC - 1)),
                            inc=([("pe", 1)] if k == FSC - 1 else None))
                    add_to_x(och, qt, psum, pl.val("pe"), bias_idx=124 + och)
                    pl.bank_set(bank, "dve")
                ring.consumed(sda)
                ring.consumed(sdb)

        def attn_b():
            lnx_dve = layernorm()
            RQC = wtab(0)
            RQS = wtab(1)
            RKC = wtab(2)
            RKS = wtab(3)
            pl.run("pool", lambda E: E.dma_start(out=RQC, in_=rb_q[0]), inc=[("wt", 16)])
            pl.run("pool", lambda E: E.dma_start(out=RQS, in_=rb_q[1]), inc=[("wt", 16)])
            pl.run("pool", lambda E: E.dma_start(out=RKC, in_=rb_k[0]), inc=[("wt", 16)])
            pl.run("pool", lambda E: E.dma_start(out=RKS, in_=rb_k[1]), inc=[("wt", 16)])
            rb_w = pl.val("wt")

            kv_store_io = [0]
            fd = Feeder(ring, [(wqkv[oi], 2048) for oi in range(48)], 2)
            for oi in range(48):
                s = fd.get(oi)
                ring.wait_ready("pe", s)
                pl.wait("pe", "dve", lnx_dve)
                for qt in range(2):
                    bank = qt
                    pl.bank_wait(bank)
                    for k in range(HC):
                        pl.run("pe", lambda E, s=s, k=k, qt=qt, bank=bank: E.matmul(
                            PS[bank][:, :], ring.ap(s, k * 128, 128),
                            LNX[k][:, qt * 512:(qt + 1) * 512],
                            start=(k == 0), stop=(k == HC - 1)),
                            inc=([("pe", 1)] if k == HC - 1 else None))
                    mmv = pl.val("pe")
                    pl.wait("act", "pe", mmv)
                    pl.run("act", lambda E, oi=oi, qt=qt, bank=bank: E.activation(
                        ROPS[:, qt * 512:(qt + 1) * 512], PS[bank][:, :], AF.Identity,
                        bias=bias_col(16 + oi)), inc=[("act", 1)])
                    pl.bank_set(bank, "act")
                pj_act = pl.val("act")
                h = oi % 16
                if oi < 32:
                    is_q = oi < 16
                    dst = QN[h] if is_q else QP[h % 2]
                    src = ROPS[:, 0:1024]
                    rot = ROPS[:, 1024:2048]
                    ctab = RQC if is_q else RKC
                    stab = RQS if is_q else RKS
                    pl.wait("dve", "act", pj_act)
                    pl.wait("dve", "wt", rb_w)
                    if not is_q:
                        pl.wait("dve", "io", kv_store_io[0])
                    pl.run("dve", lambda E: E.tensor_scalar_mul(rot[0:16, :], src[32:48, :], -1.0))
                    pl.run("dve", lambda E: E.tensor_copy(rot[32:48, :], src[0:16, :]))
                    pl.run("dve", lambda E, dst=dst, ctab=ctab: E.tensor_mul(
                        dst[:, :], src[:, :], ctab))
                    pl.run("dve", lambda E, stab=stab: E.tensor_mul(
                        rot[0:48, :], rot[0:48, :], stab[0:48, :]))
                    pl.run("dve", lambda E, dst=dst: E.tensor_add(
                        dst[0:48, :], dst[0:48, :], rot[0:48, :]), inc=[("dve", 1)])
                    if not is_q:
                        pl.wait("sp", "dve", pl.val("dve"))
                        pl.run("sp", lambda E, h=h, dst=dst: E.dma_start(
                            out=ag_bk_in[h * 128:(h + 1) * 128, :], in_=dst[:, :]),
                            inc=[("io", 16)])
                        kv_store_io[0] = pl.val("io")
                else:
                    pl.wait("pe", "act", pj_act)
                    pl.wait("pe", "io", kv_store_io[0])
                    for blk in range(8):
                        pb = 6 + blk % 2
                        pl.bank_wait(pb)
                        pl.run("pe", lambda E, blk=blk: E.transpose(
                            PT[blk % 2][:, 0:128], ROPS[:, blk * 128:(blk + 1) * 128],
                            IDB[:, :]))
                        pl.run("pe", lambda E: E.drain(), inc=[("pe", 1)])
                        pl.wait("dve", "pe", pl.val("pe"))
                        pl.run("dve", lambda E, blk=blk: E.tensor_copy(
                            QP[2 + blk // 4][:, (blk % 4) * 128:(blk % 4) * 128 + 128],
                            PT[blk % 2][:, 0:128]), inc=[("dve", 1)])
                        pl.bank_set(pb, "dve")
                    pl.wait("sp", "dve", pl.val("dve"))
                    for half in range(2):
                        pl.run("sp", lambda E, h=h, half=half: E.dma_start(
                            out=ag_bv_in[h * 1024 + half * 512:h * 1024 + (half + 1) * 512, :]
                            .rearrange("(a p) d -> p a d", p=128),
                            in_=QP[2 + half][:, 0:512]), inc=[("io", 16)])
                    kv_store_io[0] = pl.val("io")
                ring.consumed(s)

            pl.wait("pool", "io", pl.val("io"))
            pl.run("pool", lambda E: E.collective_compute(
                "AllGather", ALU.bypass,
                replica_groups=[[0, 1], [2, 3], [4, 5], [6, 7]],
                ins=[ag_bk_in.ap()], outs=[ag_bk_out.ap()]), inc=[("cc1", 1)])
            pl.run("pool", lambda E: E.collective_compute(
                "AllGather", ALU.bypass,
                replica_groups=[[0, 1], [2, 3], [4, 5], [6, 7]],
                ins=[ag_bv_in.ap()], outs=[ag_bv_out.ap()]), inc=[("cc2", 1)])

            pl.wait("sp", "cc1", 1)
            pl.wait("sp", "cc2", 1)
            for h in range(NH):
                pl.wait("sp", "pe", pl.val("pe"))
                pl.wait("sp", "act", pl.val("act"))
                for r in range(2):
                    pl.run("sp", lambda E, h=h, r=r: E.dma_start(
                        out=KVH[0][:, r * 1024:(r + 1) * 1024],
                        in_=ag_bk_out[r * 2048 + h * 128:r * 2048 + (h + 1) * 128, :]),
                        inc=[("io", 16)])
                    pl.run("sp", lambda E, h=h, r=r: E.dma_start(
                        out=KVH[1][:, r * 1024:(r + 1) * 1024],
                        in_=ag_bv_out[r * NH * 1024 + h * 1024:
                                      r * NH * 1024 + (h + 1) * 1024, :]
                        .rearrange("(a p) d -> p a d", p=128)), inc=[("io", 16)])
                kv_io = pl.val("io")
                pl.wait("pe", "io", kv_io)

                def scores_ops(qt, j0, nj, psum, h=h):
                    pl.run("pe", lambda E, qt=qt, j0=j0, nj=nj, psum=psum, h=h: E.matmul(
                        psum[:, 0:nj * 128], QN[h][:, qt * 128:(qt + 1) * 128],
                        KVH[0][:, j0 * 128:(j0 + nj) * 128],
                        start=True, stop=True), inc=[("pe", 1)])

                def vblk_ap(j):
                    return KVH[1][:, j * 128:(j + 1) * 128]

                def probs_of(qi, j0, nj):
                    t = LNX[2 * qi] if j0 < 8 else LNX[2 * qi + 1]
                    c0 = (j0 % 8) * 128
                    return t[:, c0:c0 + nj * 128]

                attn_inner(h, scores_ops, vblk_ap, probs_of)

            proj_add(lambda o: wdense[o], 16,
                     lambda k, qt: QN[k][:, qt * 512:(qt + 1) * 512],
                     bias_base=64, pre_pe=[("act", pl.val("act"))])

        def tap_store(i):
            if not TAPS:
                return
            pl.wait("pool", "dve", pl.val("dve"))
            pl.wait("pool", "act", pl.val("act"))
            pl.wait("pool", "pe", pl.val("pe"))
            for c in range(HC):
                pl.run("pool", lambda E, c=c, i=i: E.dma_start(
                    out=taps[i][c, :, :], in_=X[c][:, :]), inc=[("out", 16)])

        if N_SUB >= 1:
            attn_a()
            tap_store(0)
        if N_SUB >= 2:
            ffn()
            tap_store(1)
        if N_SUB >= 3:
            attn_b()
            tap_store(2)
        if N_SUB >= 4:
            ffn()
            tap_store(3)

        pl.wait("pool", "dve", pl.val("dve"))
        pl.wait("pool", "act", pl.val("act"))
        pl.wait("pool", "pe", pl.val("pe"))
        for c in range(HC):
            pl.run("pool", lambda E, c=c: E.dma_start(out=yout[c, :, :], in_=X[c][:, :]),
                   inc=[("out", 16)])
        pl.wait("pool", "out", pl.val("out"))
        pl.wait("sp", "out", pl.val("out"))

        blk = E(nc.Block())

        @blk.tensor
        def _(Eng):
            for f in pl.q["pe"]:
                f(Eng)

        @blk.vector
        def _(Eng):
            for f in pl.q["dve"]:
                f(Eng)

        @blk.scalar
        def _(Eng):
            for f in pl.q["act"]:
                f(Eng)

        @blk.sync
        def _(Eng):
            for f in pl.q["sp"]:
                f(Eng)

        @blk.gpsimd
        def _(Eng):
            for f in pl.q["pool"]:
                f(Eng)

    return nc


def kernel(**inputs):
    nc = build_nc()
    in_maps = [build_host_inputs(c, inputs) for c in range(NCORES)]
    res = run_bass_kernel_spmd(nc, in_maps, core_ids=list(range(NCORES)))
    out = np.zeros((B, S, H), dtype=np.float32)
    for c in range(NCORES):
        par, bidx = c % 2, c // 2
        y = np.asarray(res.results[c]["yout"])
        xt = y.reshape(H, TC).T.reshape(NBLK, 128, H)
        for i in range(NBLK):
            p = 2 * i + par
            out[bidx, p * 128:(p + 1) * 128, :] = xt[i]
    return out


# version 02
# speedup vs baseline: 1.0425x; 1.0425x over previous
import os
import contextlib
import numpy as np
import concourse.bass as bass
import concourse.mybir as mybir
from concourse.bass_utils import run_bass_kernel_spmd

F32 = mybir.dt.float32
BF16 = mybir.dt.bfloat16
AF = mybir.ActivationFunctionType
ALU = mybir.AluOpType
AX = mybir.AxisListType

NCORES = 8
B, S, H = 4, 2048, 2048
HC = H // 128
TC = 1024
NBLK = 8
NH = 16
QLORA = 1536
KVL = 512
DR = 64
DN = 128
SA = float((DN + DR) ** (-0.5))
NE = 8
FIC = 11
FI = 1408
FSC = 22
FSI = 2816
ROT = 32
SB_SC = float(128 ** (-0.5))

N_SUB = int(os.environ.get("BASS_MK_SUBLAYERS", "4"))
TAPS = os.environ.get("BASS_MK_TAPS", "0") == "1"
NDBG = int(os.environ.get("BASS_MK_DBG", "0"))
SYNCDBG = os.environ.get("BASS_MK_SYNC", "0") == "1"

POS_OF_BUF = [2 * j for j in range(8)] + [2 * j + 1 for j in range(8)]



def _tile_weight(w, och_i=128):
    K, O = w.shape
    kc = K // 128
    oc = O // och_i
    b = w.reshape(kc, 128, oc, och_i).transpose(2, 1, 0, 3).reshape(oc, 128, kc * och_i)
    return np.ascontiguousarray(b.astype(np.float32))


def _perm_head_b():
    x1 = list(range(0, 16))
    x2 = list(range(16, 32))
    rest = list(range(32, 128))
    return np.array(x1 + rest[:16] + x2 + rest[16:32] + rest[32:], dtype=np.int64)


def _cos_sin(positions, dim):
    inv = 1.0 / (10000.0 ** (np.arange(0, dim, 2, dtype=np.float64) / dim))
    fr = np.outer(positions.astype(np.float64), inv)
    emb = np.concatenate([fr, fr], axis=-1)
    return np.cos(emb), np.sin(emb)


def build_host_inputs(core, ins):
    par = core % 2
    bidx = core // 2
    own_pos = np.array([2 * i + par for i in range(NBLK)])
    d = {}

    xc = np.asarray(ins["L0_attn_a_x"], dtype=np.float32)[bidx]
    xc = xc.reshape(16, 128, H)[own_pos]
    d["xin"] = np.ascontiguousarray(xc.reshape(TC, H).T.reshape(HC, 128, TC))

    d["wqa"] = _tile_weight(np.asarray(ins["L0_attn_a_wq_a"]))
    wqb = np.asarray(ins["L0_attn_a_wq_b"])
    cols = []
    for h in range(NH):
        cols.extend(range(h * 192, h * 192 + DN))
    for g in range(NH // 2):
        cols.extend(range((2 * g) * 192 + DN, (2 * g) * 192 + 192))
        cols.extend(range((2 * g + 1) * 192 + DN, (2 * g + 1) * 192 + 192))
    d["wqb"] = _tile_weight(wqb[:, np.array(cols)])
    wkva = np.asarray(ins["L0_attn_a_wkv_a"])
    d["wkvac"] = _tile_weight(wkva[:, :KVL])
    d["wkvap"] = _tile_weight(wkva[:, KVL:], och_i=64)[0]
    wkvb = np.asarray(ins["L0_attn_a_wkv_b"])
    cols = []
    for h in range(NH):
        cols.extend(range(h * 256, h * 256 + DN))
    for h in range(NH):
        cols.extend(range(h * 256 + DN, h * 256 + 256))
    d["wkvb"] = _tile_weight(wkvb[:, np.array(cols)])
    d["wo"] = _tile_weight(np.asarray(ins["L0_attn_a_wo"]))

    d["rw"] = _tile_weight(np.ascontiguousarray(np.asarray(ins["L0_ffn_router_w"]).T),
                           och_i=8)[0]
    d["gw"] = np.stack([_tile_weight(np.asarray(ins["L0_ffn_gate_w"])[e]) for e in range(NE)])
    d["uw"] = np.stack([_tile_weight(np.asarray(ins["L0_ffn_up_w"])[e]) for e in range(NE)])
    d["dw"] = np.stack([_tile_weight(np.asarray(ins["L0_ffn_down_w"])[e]) for e in range(NE)])
    d["sgw"] = _tile_weight(np.asarray(ins["L0_ffn_s_gate_w"]))
    d["suw"] = _tile_weight(np.asarray(ins["L0_ffn_s_up_w"]))
    d["sdw"] = _tile_weight(np.asarray(ins["L0_ffn_s_down_w"]))

    wqkv = np.asarray(ins["L0_attn_b_W_qkv"])
    bqkv = np.asarray(ins["L0_attn_b_b_qkv"])
    perm = _perm_head_b()
    cols = []
    for h in range(NH):
        cols.extend((h * 384 + perm).tolist())
    for h in range(NH):
        cols.extend((h * 384 + 128 + perm).tolist())
    for h in range(NH):
        cols.extend(range(h * 384 + 256, h * 384 + 256 + 128))
    cols = np.array(cols)
    d["wqkv"] = _tile_weight(wqkv[:, cols])
    bq = bqkv[cols].astype(np.float32)
    d["wdense"] = _tile_weight(np.asarray(ins["L0_attn_b_W_dense"]))

    vec_list = []
    def addvec(v):
        v = np.asarray(v, dtype=np.float32).reshape(-1)
        vec_list.append(v.reshape(-1, 128).T)
    addvec(ins["L0_attn_a_q_a_ln_w"])
    addvec(ins["L0_attn_a_kv_a_ln_w"])
    addvec(bq)
    addvec(ins["L0_attn_b_b_dense"])
    addvec(ins["L0_ffn_s_gate_b"])
    addvec(ins["L0_ffn_s_up_b"])
    addvec(ins["L0_ffn_s_down_b"])
    d["vecs"] = np.ascontiguousarray(np.concatenate(vec_list, axis=1))

    d["identf"] = np.eye(128, dtype=np.float32)
    d["ones1"] = np.ones((1, 128), dtype=np.float32)
    d["onesc"] = np.ones((128, 1), dtype=np.float32)

    tri = np.where(np.arange(128)[None, :] <= np.arange(128)[:, None], 0.0, -1e30
                   ).astype(np.float32)
    zer = np.zeros((128, 128), np.float32)
    neg = np.full((128, 128), -1e30, np.float32)
    am = np.zeros((8, 2, 128, 128), np.float32)
    for qt in range(8):
        am[qt, 0] = tri if par == 0 else zer
        am[qt, 1] = neg if par == 0 else tri
    d["amask"] = np.ascontiguousarray(am.transpose(2, 0, 1, 3).reshape(128, 2048))

    own_tok = (own_pos[:, None] * 128 + np.arange(128)[None, :]).reshape(-1)
    buf_tok = (np.array(POS_OF_BUF)[:, None] * 128 + np.arange(128)[None, :]).reshape(-1)

    ca, sa_ = _cos_sin(own_tok, DR)
    t = np.zeros((2, 128, TC), np.float32)
    t[0, 0:64] = ca.T * SA
    t[0, 64:128] = ca.T * SA
    t[1, 0:64] = sa_.T * SA
    t[1, 64:128] = sa_.T * SA
    d["ra_q"] = t
    ck, sk = _cos_sin(buf_tok, DR)
    t = np.zeros((4, 128, 1024), np.float32)
    t[0, 0:64] = ck.T[:, 0:1024]
    t[1, 0:64] = ck.T[:, 1024:2048]
    t[2, 0:64] = sk.T[:, 0:1024]
    t[3, 0:64] = sk.T[:, 1024:2048]
    d["ra_k"] = t
    cb, sb = _cos_sin(own_tok, ROT)
    def tb(scale):
        cos_t = np.full((128, TC), scale, np.float32)
        sin_t = np.zeros((128, TC), np.float32)
        cos_t[0:16] = cb[:, 0:16].T * scale
        cos_t[32:48] = cb[:, 16:32].T * scale
        sin_t[0:16] = sb[:, 0:16].T * scale
        sin_t[32:48] = sb[:, 16:32].T * scale
        return np.stack([cos_t, sin_t])
    d["rb_q"] = tb(SB_SC)
    d["rb_k"] = tb(1.0)
    return d



ENGINES = ("pe", "dve", "act", "sp", "pool")


class Plan:
    def __init__(self, sems):
        self.sems = sems
        self.q = {e: [] for e in ENGINES}
        self.c = {n: 0 for n in sems}
        self._lastwait = {e: {} for e in ENGINES}
        self.bank = {}

    def wait(self, eng, sem, val=None):
        v = self.c[sem] if val is None else val
        if v <= 0:
            return
        if self._lastwait[eng].get(sem, -1) >= v:
            return
        self._lastwait[eng][sem] = v
        h = self.sems[sem]
        self.q[eng].append(lambda E, h=h, v=v: E.wait_ge(h, v))

    def run(self, eng, fn, inc=None, guard=False):
        incs = []
        if inc:
            for s_, d_ in inc:
                self.c[s_] += d_
                incs.append((self.sems[s_], d_))
        need_drain = (bool(incs) and eng in ("dve", "act")) or guard

        def cl(E, fn=fn, incs=incs, need_drain=need_drain):
            inst = fn(E)
            if need_drain:
                inst = E.drain(fusable=False)
            for h, d_ in incs:
                inst.then_inc(h, d_)
        self.q[eng].append(cl)

    def val(self, sem):
        return self.c[sem]

    def phase(self, name):
        for e in ENGINES:
            self.q[e].append(name)

    def barrier(self):
        for eng in ENGINES:
            for sem in ("pe", "dve", "act", "w", "io"):
                self.wait(eng, sem, self.c[sem])

    def bank_wait(self, bank):
        g = self.bank.get(bank)
        if g:
            self.wait("pe", g[0], g[1])

    def bank_set(self, bank, eng):
        self.bank[bank] = (eng, self.c[eng])


class Ring:
    def __init__(self, pl, arena, col0, cols_per_slot, nslots, sempfx):
        self.pl = pl
        self.arena = arena
        self.base = [col0 + i * cols_per_slot for i in range(nslots)]
        self.maxcols = cols_per_slot
        self.free_at = [0] * nslots
        self.semname = [f"{sempfx}{i}" for i in range(nslots)]
        self.nloads = [0] * nslots
        self.i = 0

    def load(self, src_ap, cols):
        assert cols <= self.maxcols
        pl = self.pl
        s = self.i % len(self.base)
        self.i += 1
        pl.wait("pool", "pe", self.free_at[s])
        ap = self.arena[:, self.base[s]:self.base[s] + cols]
        pl.run("pool", lambda E, ap=ap, src=src_ap: E.dma_start(out=ap, in_=src),
               inc=[(self.semname[s], 16)])
        self.nloads[s] += 1
        return s

    def wait_ready(self, eng, s):
        self.pl.wait(eng, self.semname[s], self.nloads[s] * 16)

    def ap(self, s, c0, cols):
        return self.arena[:, self.base[s] + c0:self.base[s] + c0 + cols]

    def consumed(self, s):
        self.free_at[s] = self.pl.val("pe")


class Feeder:
    def __init__(self, rng, srcs, ahead):
        self.rng = rng
        self.srcs = srcs
        self.ahead = ahead
        self.slots = []
        self.next = 0
        self._fill(ahead)

    def _fill(self, upto_extra):
        while self.next < len(self.srcs) and self.next < upto_extra:
            self.slots.append(self.rng.load(*self.srcs[self.next]))
            self.next += 1

    def get(self, i):
        while self.next < len(self.srcs) and self.next <= i + self.ahead:
            self.slots.append(self.rng.load(*self.srcs[self.next]))
            self.next += 1
        return self.slots[i]



def build_nc():
    nc = bass.Bass("TRN2", target_bir_lowering=False)

    DI = lambda name, shape: nc.dram_tensor(name, shape, F32, kind="ExternalInput")
    xin = DI("xin", [HC, 128, TC])
    wqa = DI("wqa", [12, 128, 2048])
    wqb = DI("wqb", [24, 128, 1536])
    wkvac = DI("wkvac", [4, 128, 2048])
    wkvap = DI("wkvap", [128, 1024])
    wkvb = DI("wkvb", [32, 128, 512])
    wo = DI("wo", [16, 128, 2048])
    rw = DI("rw", [128, 128])
    gw = DI("gw", [NE, FIC, 128, 2048])
    uw = DI("uw", [NE, FIC, 128, 2048])
    dwt = DI("dw", [NE, 16, 128, FI])
    sgw = DI("sgw", [FSC, 128, 2048])
    suw = DI("suw", [FSC, 128, 2048])
    sdw = DI("sdw", [16, 128, FSI])
    wqkv = DI("wqkv", [48, 128, 2048])
    wdense = DI("wdense", [16, 128, 2048])
    vecs = DI("vecs", [128, 140])
    identf = DI("identf", [128, 128])
    ones1 = DI("ones1", [1, 128])
    onesc = DI("onesc", [128, 1])
    amask = DI("amask", [128, 2048])
    ra_q = DI("ra_q", [2, 128, TC])
    ra_k = DI("ra_k", [4, 128, 1024])
    rb_q = DI("rb_q", [2, 128, TC])
    rb_k = DI("rb_k", [2, 128, TC])

    yout = nc.dram_tensor("yout", [HC, 128, TC], F32, kind="ExternalOutput")
    dbg = (nc.dram_tensor("dbg", [NDBG, 128, 2048], F32, kind="ExternalOutput")
           if NDBG else None)
    taps = [nc.dram_tensor(f"tap{i+1}", [HC, 128, TC], F32, kind="ExternalOutput")
            for i in range(4)] if TAPS else []

    ag_a_in = nc.dram_tensor("ag_a_in", [576, TC], BF16)
    ag_a_out = nc.dram_tensor("ag_a_out", [1152, TC], BF16)
    ag_bk_in = nc.dram_tensor("ag_bk_in", [2048, TC], BF16)
    ag_bk_out = nc.dram_tensor("ag_bk_out", [4096, TC], BF16)
    ag_bv_in = nc.dram_tensor("ag_bv_in", [NH * TC, 128], BF16)
    ag_bv_out = nc.dram_tensor("ag_bv_out", [2 * NH * TC, 128], BF16)

    with contextlib.ExitStack() as stack:
        E = stack.enter_context

        X = [E(nc.sbuf_tensor(f"x{c}", [128, TC], BF16)) for c in range(HC)]
        LNX = [E(nc.sbuf_tensor(f"lnx{c}", [128, TC], BF16)) for c in range(HC)]
        QN = [E(nc.sbuf_tensor(f"qn{h}", [128, TC], BF16)) for h in range(NH)]
        QP = [E(nc.sbuf_tensor(f"qp{g}", [128, TC], BF16)) for g in range(8)]
        KVH = [E(nc.sbuf_tensor(f"kvh{j}", [128, S], BF16)) for j in range(2)]
        WRB = E(nc.sbuf_tensor("wrb", [128, 16384], BF16))
        ROPS = E(nc.sbuf_tensor("rops", [128, 3072], BF16))
        GSC = E(nc.sbuf_tensor("gsc", [128, 2048], BF16))
        DSC = E(nc.sbuf_tensor("dsc", [128, 1024], BF16))
        F32S = E(nc.sbuf_tensor("f32s", [128, 1024], F32))
        SROWS = E(nc.sbuf_tensor("srows", [1, 2048], F32))
        SCOL = E(nc.sbuf_tensor("scol", [128, 32], F32))
        RTR = E(nc.sbuf_tensor("rtr", [128, 192], F32))
        AM = E(nc.sbuf_tensor("am", [128, 2048], F32))
        IDF = E(nc.sbuf_tensor("idf", [128, 128], F32))
        IDB = E(nc.sbuf_tensor("idb", [128, 128], BF16))
        ONEF = E(nc.sbuf_tensor("onef", [1, 128], F32))
        ONEB = E(nc.sbuf_tensor("oneb", [128, 1], BF16))
        VEC = E(nc.sbuf_tensor("vec", [128, 140], F32))

        PS = [E(nc.psum_tensor(f"ps{i}", [128, 512], F32)) for i in range(6)]
        PT = [E(nc.psum_tensor(f"pt{i}", [128, 1024], BF16)) for i in range(2)]

        sems = {}
        for name in ("w", "wi", "wt", "io", "pe", "dve", "act", "out",
                     "cc0", "cc1", "cc2",
                     "rg0", "rg1", "rg2", "rg3", "rm0", "rm1", "rm2", "rm3"):
            sems[name] = E(nc.semaphore(name))
        pl = Plan(sems)

        RING0 = 0
        RINGM = 4 * 2048
        WTAB = RINGM + 4 * 512
        PTS0 = WTAB + 4 * 1024
        ring = Ring(pl, WRB, RING0, 2048, 4, "rg")
        ringm = Ring(pl, WRB, RINGM, 512, 4, "rm")

        def wtab(i):
            return WRB[:, WTAB + i * 1024:WTAB + (i + 1) * 1024]

        def pts_ap(i, cols=512):
            return WRB[:, PTS0 + i * 512:PTS0 + i * 512 + cols]

        def bias_col(i):
            return VEC[:, i:i + 1]

        pl.run("sp", lambda E: E.dma_start(out=IDF[:, :], in_=identf[:, :]), inc=[("io", 16)])
        pl.run("pool", lambda E: E.dma_start(out=IDB[:, :], in_=identf[:, :]), inc=[("wi", 16)])
        pl.run("sp", lambda E: E.dma_start(out=ONEF[:, :], in_=ones1[:, :]), inc=[("io", 16)])
        pl.run("pool", lambda E: E.dma_start(out=ONEB[:, :], in_=onesc[:, :]), inc=[("wi", 16)])
        pl.run("sp", lambda E: E.dma_start(out=VEC[:, :], in_=vecs[:, :]), inc=[("io", 16)])
        pl.run("sp", lambda E: E.dma_start(
            out=AM[:, :], in_=amask[:, :]), inc=[("io", 16)])
        for c in range(HC):
            pl.run("pool", lambda E, c=c: E.dma_start(out=X[c][:, :], in_=xin[c, :, :]),
                   inc=[("wi", 16)])
        init_io = pl.val("io")
        init_w = pl.val("wi")

        def am_ap(qt, side):
            c0 = (qt * 2 + side) * 128
            return AM[:, c0:c0 + 128]

        dbg_n = [0]

        def dbgdump(ap):
            if not NDBG or dbg_n[0] >= NDBG:
                return
            i = dbg_n[0]
            dbg_n[0] += 1
            for eng_s in ("dve", "act", "pe", "io"):
                pl.wait("pool", eng_s, pl.val(eng_s))
            pl.run("pool", lambda E, ap=ap, i=i: E.dma_start(
                out=dbg[i, 0:ap.shape[0], 0:ap.shape[1]], in_=ap), inc=[("out", 16)])
            for eng_s in ("dve", "act", "pe", "sp", "pool"):
                pl.wait(eng_s, "out", pl.val("out"))

        def layernorm():
            pl.wait("act", "wi", init_w)
            pl.wait("pe", "wi", init_w)
            pl.wait("pe", "io", init_io)
            x2 = [ROPS[:, 0:1024], ROPS[:, 1024:2048]]
            for bk in range(4):
                pl.bank_wait(bk)
            chunk_pe = []
            for c in range(HC):
                if c >= 2:
                    pl.wait("act", "pe", chunk_pe[c - 2])
                pl.run("act", lambda E, c=c: E.activation(x2[c % 2], X[c][:, :], AF.Square),
                       inc=[("act", 1)])
                pl.wait("pe", "act", pl.val("act"))
                for qt in range(2):
                    pl.run("pe", lambda E, c=c, qt=qt: E.matmul(
                        PS[qt][0:1, :], ONEB[:, :], X[c][:, qt * 512:(qt + 1) * 512],
                        start=(c == 0), stop=(c == HC - 1)))
                for qt in range(2):
                    pl.run("pe", lambda E, c=c, qt=qt, t=x2[c % 2]: E.matmul(
                        PS[2 + qt][0:1, :], ONEB[:, :], t[:, qt * 512:(qt + 1) * 512],
                        start=(c == 0), stop=(c == HC - 1)),
                        inc=([("pe", 1)] if qt == 1 else None))
                chunk_pe.append(pl.val("pe"))
            stats_pe = pl.val("pe")
            MR = SROWS[0:1, 0:1024]
            R2 = SROWS[0:1, 1024:2048]
            pl.wait("dve", "pe", stats_pe)
            for qt in range(2):
                pl.run("dve", lambda E, qt=qt: E.tensor_scalar_mul(
                    MR[:, qt * 512:(qt + 1) * 512], PS[qt][0:1, :], 1.0 / H))
            for qt in range(2):
                pl.run("dve", lambda E, qt=qt: E.tensor_mul(
                    R2[:, qt * 512:(qt + 1) * 512], MR[:, qt * 512:(qt + 1) * 512],
                    MR[:, qt * 512:(qt + 1) * 512]))
            for qt in range(2):
                pl.run("dve", lambda E, qt=qt: E.scalar_tensor_tensor(
                    R2[:, qt * 512:(qt + 1) * 512], PS[2 + qt][0:1, :], 1.0 / H,
                    R2[:, qt * 512:(qt + 1) * 512], ALU.mult, ALU.subtract),
                    inc=[("dve", 1)])
            for bk in range(4):
                pl.bank_set(bk, "dve")
            pl.run("dve", lambda E: E.tensor_scalar_add(R2[:, :], R2[:, :], 1e-5),
                   inc=[("dve", 1)])
            pl.wait("act", "dve", pl.val("dve"))
            pl.run("act", lambda E: E.activation(R2[:, :], R2[:, :], AF.Sqrt),
                   inc=[("act", 1)])
            pl.wait("dve", "act", pl.val("act"))
            pl.run("dve", lambda E: E.reciprocal(R2[:, :], R2[:, :]))
            pl.run("dve", lambda E: E.tensor_mul(MR[:, :], MR[:, :], R2[:, :]))
            pl.run("dve", lambda E: E.tensor_scalar_mul(MR[:, :], MR[:, :], -1.0),
                   inc=[("dve", 1)])
            rows = pl.val("dve")
            RBCb = ROPS[:, 0:1024]
            B2Cb = ROPS[:, 1024:2048]
            pl.wait("pe", "dve", rows)
            pl.bank_wait(4)
            pl.bank_wait(5)
            for qt in range(2):
                pl.run("pe", lambda E, qt=qt: E.matmul(
                    PS[4][:, :], ONEF[:, :], R2[:, qt * 512:(qt + 1) * 512],
                    start=True, stop=True))
                pl.run("pe", lambda E, qt=qt: E.matmul(
                    PS[5][:, :], ONEF[:, :], MR[:, qt * 512:(qt + 1) * 512],
                    start=True, stop=True), inc=[("pe", 1)])
                pl.wait("dve", "pe", pl.val("pe"))
                pl.run("dve", lambda E, qt=qt: E.tensor_copy(
                    RBCb[:, qt * 512:(qt + 1) * 512], PS[4][:, :]))
                pl.run("dve", lambda E, qt=qt: E.tensor_copy(
                    B2Cb[:, qt * 512:(qt + 1) * 512], PS[5][:, :]), inc=[("dve", 1)])
                pl.wait("pe", "dve", pl.val("dve"))
            pl.bank_set(4, "dve")
            pl.bank_set(5, "dve")
            for c in range(HC):
                pl.run("dve", lambda E, c=c: E.tensor_mul(LNX[c][:, :], X[c][:, :], RBCb))
                pl.run("dve", lambda E, c=c: E.tensor_add(LNX[c][:, :], LNX[c][:, :], B2Cb),
                       inc=[("dve", 1)])
            return pl.val("dve")

        dsc_hist = []

        def add_to_x(och, qt, psum, mmv, bias_idx=None):
            dcol = (len(dsc_hist) % 2) * 512
            pl.wait("act", "pe", mmv)
            if len(dsc_hist) >= 2:
                pl.wait("act", "dve", dsc_hist[-2])
            if bias_idx is None:
                pl.run("act", lambda E, psum=psum, dcol=dcol: E.activation(
                    DSC[:, dcol:dcol + 512], psum[:, :], AF.Copy), inc=[("act", 1)])
            else:
                pl.run("act", lambda E, psum=psum, dcol=dcol, b=bias_idx: E.activation(
                    DSC[:, dcol:dcol + 512], psum[:, :], AF.Identity, bias=bias_col(b)),
                    inc=[("act", 1)])
            pl.wait("dve", "act", pl.val("act"))
            pl.run("dve", lambda E, och=och, qt=qt, dcol=dcol: E.tensor_add(
                X[och][:, qt * 512:(qt + 1) * 512], X[och][:, qt * 512:(qt + 1) * 512],
                DSC[:, dcol:dcol + 512]), inc=[("dve", 1)])
            dsc_hist.append(pl.val("dve"))
            return "dve"

        def proj_add(blob_fn, kc, src_fn, bias_base=None, pre_pe=None):
            fd = Feeder(ring, [(blob_fn(o), kc * 128) for o in range(16)], 2)
            for och in range(16):
                s = fd.get(och)
                ring.wait_ready("pe", s)
                if pre_pe:
                    for se, v in pre_pe:
                        pl.wait("pe", se, v)
                for qt in range(2):
                    bank = (och % 2) * 2 + qt
                    pl.bank_wait(bank)
                    psum = PS[bank]
                    for k in range(kc):
                        pl.run("pe", lambda E, s=s, k=k, qt=qt, psum=psum: E.matmul(
                            psum[:, :], ring.ap(s, k * 128, 128),
                            src_fn(k, qt), start=(k == 0), stop=(k == kc - 1)),
                            inc=([("pe", 1)] if k == kc - 1 else None))
                    add_to_x(och, qt, psum, pl.val("pe"),
                             bias_idx=(None if bias_base is None else bias_base + och))
                    pl.bank_set(bank, "dve")
                ring.consumed(s)

        def attn_inner(h, scores_ops, vblk_ap, probs_of, dbg_head=False):
            for g in range(2):
                njmax = 4 * g + 4
                for qi in range(4):
                    qt = g * 4 + qi
                    tiles = []
                    bank = 0
                    for (j0, side) in ((0, 0), (8, 1)):
                        off = 0
                        nj = qt + 1
                        while off < nj:
                            n = min(4, nj - off)
                            tiles.append((bank, j0 + off, n,
                                          side if off + n == nj else None))
                            bank += 1
                            off += n
                    if SYNCDBG:
                        pl.barrier()
                    for (bk, j0, n, side) in tiles:
                        pl.bank_wait(bk)
                        scores_ops(qt, j0, n, PS[bk])
                    mmv = pl.val("pe")
                    pl.wait("dve", "pe", mmv)
                    for (bk, j0, n, side) in tiles:
                        if side is not None:
                            c0 = (n - 1) * 128
                            pl.run("dve", lambda E, bk=bk, c0=c0, qt=qt, side=side:
                                   E.tensor_add(PS[bk][:, c0:c0 + 128],
                                                PS[bk][:, c0:c0 + 128],
                                                am_ap(qt, side)))
                    nt = len(tiles)
                    for i, (bk, j0, n, side) in enumerate(tiles):
                        last = (i == nt - 1) and nt == 1
                        pl.run("dve", lambda E, bk=bk, n=n, i=i: E.reduce_max(
                            SCOL[:, i:i + 1], PS[bk][:, 0:n * 128], axis=AX.X,
                            negate=True), inc=([("dve", 1)] if last else None),
                            guard=(i == nt - 1))
                    for i in range(1, nt):
                        pl.run("dve", lambda E, i=i: E.tensor_tensor(
                            SCOL[:, 0:1], SCOL[:, 0:1], SCOL[:, i:i + 1], ALU.min),
                            inc=([("dve", 1)] if i == nt - 1 else None), guard=True)
                    dv = pl.val("dve")
                    if SYNCDBG:
                        pl.barrier()
                    pl.wait("act", "dve", dv)
                    for i, (bk, j0, n, side) in enumerate(tiles):
                        pl.run("act", lambda E, bk=bk, j0=j0, n=n, i=i, qi=qi, qt=qt:
                               E.activation(probs_of(qi, j0, n),
                                            PS[bk][:, 0:n * 128], AF.Exp,
                                            bias=SCOL[:, 0:1],
                                            accum_out=SCOL[:, 8 + i:9 + i]))
                    pl.run("act", lambda E: E.drain(), inc=[("act", 1)])
                    for (bk, j0, n, side) in tiles:
                        pl.bank_set(bk, "act")
                    av = pl.val("act")
                    pl.wait("dve", "act", av)
                    for i in range(1, nt):
                        pl.run("dve", lambda E, i=i: E.tensor_add(
                            SCOL[:, 8:9], SCOL[:, 8:9], SCOL[:, 8 + i:9 + i]),
                            guard=True)
                    pl.run("dve", lambda E, qi=qi: E.reciprocal(
                        SCOL[:, 16 + qi:17 + qi], SCOL[:, 8:9]), inc=[("dve", 1)])
                    if dbg_head and g == 0 and qi == 1:
                        dbgdump(SCOL[:, 0:32])
                if dbg_head and g == 0:
                    dbgdump(LNX[0][:, :])
                    dbgdump(LNX[1][:, :])
                    dbgdump(LNX[2][:, :])
                    dbgdump(LNX[3][:, :])
                av_started = [False] * 4
                probs_dve = pl.val("dve")
                for qi in range(4):
                    pl.bank_wait(qi)
                for jj in range(2 * njmax):
                    j = (jj // 2) if jj % 2 == 0 else 8 + (jj - 1) // 2
                    jq = j % 8
                    if SYNCDBG:
                        pl.barrier()
                    pl.wait("pe", "act", pl.val("act"))
                    trs = [qi for qi in range(4) if jq <= g * 4 + qi]
                    pl.bank_wait(6 + jj % 2)
                    for idx, qi in enumerate(trs):
                        pl.run("pe", lambda E, qi=qi, j=j, jj=jj: E.transpose(
                            PT[jj % 2][:, qi * 128:(qi + 1) * 128],
                            probs_of(qi, j, 1), IDB[:, :]))
                    pl.run("pe", lambda E: E.drain(), inc=[("pe", 1)])
                    trv = pl.val("pe")
                    pl.wait("act", "pe", trv)
                    pl.run("act", lambda E, jj=jj: E.activation(
                        pts_ap(jj % 2), PT[jj % 2][:, 0:512], AF.Copy),
                        inc=[("act", 1)])
                    pl.bank_set(6 + jj % 2, "act")
                    if dbg_head and g == 0 and jj in (0, 1):
                        dbgdump(pts_ap(jj % 2))
                    ev = pl.val("act")
                    if SYNCDBG:
                        pl.barrier()
                    pl.wait("pe", "act", ev)
                    for qi in trs:
                        qt = g * 4 + qi
                        st_ = not av_started[qi]
                        sp_ = (j == 8 + qt)
                        pl.run("pe", lambda E, qi=qi, j=j, jj=jj, st_=st_, sp_=sp_:
                               E.matmul(
                                   PS[qi][:, 0:128],
                                   pts_ap(jj % 2)[:, qi * 128:qi * 128 + 128],
                                   vblk_ap(j),
                                   start=st_, stop=sp_),
                               inc=([("pe", 1)] if sp_ else None))
                        av_started[qi] = True
                avv = pl.val("pe")
                if SYNCDBG:
                    pl.barrier()
                pl.wait("dve", "pe", avv)
                if dbg_head and g == 0:
                    for qi2 in range(4):
                        pl.run("dve", lambda E, qi2=qi2: E.tensor_copy(
                            ROPS[:, 2048 + qi2 * 128:2048 + qi2 * 128 + 128]
                            if False else GSC[:, 1024 + qi2 * 128:1024 + qi2 * 128 + 128],
                            PS[qi2][:, 0:128]), inc=[("dve", 1)])
                    dbgdump(GSC[:, 1024:1536])
                    dbgdump(SCOL[:, 0:32])
                for qi in range(4):
                    pl.run("dve", lambda E, qi=qi: E.tensor_scalar_mul(
                        GSC[:, qi * 128:(qi + 1) * 128], PS[qi][:, 0:128],
                        SCOL[:, 16 + qi:17 + qi]), inc=[("dve", 1)])
                    pl.bank_set(qi, "dve")
                odv = pl.val("dve")
                if SYNCDBG:
                    pl.barrier()
                if dbg_head and g == 0:
                    dbgdump(GSC[:, 0:512])
                pl.wait("pe", "dve", odv)
                pl.bank_wait(6)
                for qi in range(4):
                    pl.run("pe", lambda E, qi=qi: E.transpose(
                        PT[0][:, qi * 128:(qi + 1) * 128],
                        GSC[:, qi * 128:(qi + 1) * 128], IDB[:, :]))
                pl.run("pe", lambda E: E.drain(), inc=[("pe", 1)])
                pl.wait("act", "pe", pl.val("pe"))
                pl.run("act", lambda E, h=h, g=g: E.activation(
                    QN[h][:, g * 512:(g + 1) * 512], PT[0][:, 0:512], AF.Copy),
                    inc=[("act", 1)])
                pl.bank_set(6, "act")

        def attn_a():
            lnx_dve = layernorm()
            pl.phase("attn_a_proj")
            dbgdump(LNX[0][:, :])
            q2scr = [QN[4][:, 0:1024], QN[5][:, 0:1024]]
            ckv_own = [QN[0], QN[1], QN[2], QN[3]]

            pl.bank_wait(4)
            fd = Feeder(ring, [(wkvac[oi], 2048) for oi in range(4)], 2)
            for oi in range(4):
                s = fd.get(oi)
                ring.wait_ready("pe", s)
                pl.wait("pe", "dve", lnx_dve)
                for qt in range(2):
                    bank = (oi % 2) * 2 + qt
                    pl.bank_wait(bank)
                    psum = PS[bank]
                    for k in range(HC):
                        pl.run("pe", lambda E, s=s, k=k, qt=qt, psum=psum: E.matmul(
                            psum[:, :], ring.ap(s, k * 128, 128),
                            LNX[k][:, qt * 512:(qt + 1) * 512],
                            start=(k == 0), stop=(k == HC - 1)),
                            inc=([("pe", 1)] if k == HC - 1 else None))
                    mmv = pl.val("pe")
                    pl.wait("act", "pe", mmv)
                    pl.run("act", lambda E, oi=oi, qt=qt, psum=psum: E.activation(
                        ckv_own[oi][:, qt * 512:(qt + 1) * 512], psum[:, :], AF.Copy,
                        scale=bias_col(12 + oi)))
                    pl.run("act", lambda E, oi=oi, qt=qt, psum=psum: E.activation(
                        q2scr[qt][:, (oi % 2) * 512:(oi % 2) * 512 + 512], psum[:, :],
                        AF.Square), inc=[("act", 1)])
                    pl.bank_set(bank, "act")
                    pl.wait("pe", "act", pl.val("act"))
                    pl.run("pe", lambda E, oi=oi, qt=qt: E.matmul(
                        PS[4 + qt][0:1, :], ONEB[:, :],
                        q2scr[qt][:, (oi % 2) * 512:(oi % 2) * 512 + 512],
                        start=(oi == 0), stop=(oi == 3)),
                        inc=([("pe", 1)] if oi == 3 else None))
                ring.consumed(s)
            ckv_stats_pe = pl.val("pe")

            CKR = SROWS[0:1, 0:1024]
            pl.wait("dve", "pe", ckv_stats_pe)
            for qt in range(2):
                pl.run("dve", lambda E, qt=qt: E.tensor_scalar(
                    CKR[:, qt * 512:(qt + 1) * 512], PS[4 + qt][0:1, :],
                    1.0 / KVL, 1e-6, ALU.mult, ALU.add),
                    inc=([("dve", 1)] if qt == 1 else None))
            pl.bank_set(4, "dve")
            pl.bank_set(5, "dve")
            pl.wait("act", "dve", pl.val("dve"))
            pl.run("act", lambda E: E.activation(CKR[:, :], CKR[:, :], AF.Sqrt),
                   inc=[("act", 1)])
            pl.wait("dve", "act", pl.val("act"))
            pl.run("dve", lambda E: E.reciprocal(CKR[:, :], CKR[:, :]), inc=[("dve", 1)])
            pl.bank_set(4, "dve")
            pl.wait("pe", "dve", pl.val("dve"))
            CKB = ROPS[:, 0:1024]
            pl.bank_wait(4)
            for qt in range(2):
                pl.run("pe", lambda E, qt=qt: E.matmul(
                    PS[4][:, :], ONEF[:, :], CKR[:, qt * 512:(qt + 1) * 512],
                    start=True, stop=True), inc=[("pe", 1)])
                pl.wait("dve", "pe", pl.val("pe"))
                pl.run("dve", lambda E, qt=qt: E.tensor_copy(
                    CKB[:, qt * 512:(qt + 1) * 512], PS[4][:, :]), inc=[("dve", 1)])
                pl.wait("pe", "dve", pl.val("dve"))
            pl.bank_set(4, "dve")
            for oi in range(4):
                pl.run("dve", lambda E, oi=oi: E.tensor_mul(
                    ckv_own[oi][:, :], ckv_own[oi][:, :], CKB[:, :]), inc=[("dve", 1)])
            ckv_rms_dve = pl.val("dve")
            dbgdump(QN[0][:, :])

            s = ring.load(wkvap[:, :], cols=1024)
            ring.wait_ready("pe", s)
            for qt in range(2):
                pl.bank_wait(5)
                for k in range(HC):
                    pl.run("pe", lambda E, s=s, k=k, qt=qt: E.matmul(
                        PS[5][0:64, :], ring.ap(s, k * 64, 64),
                        LNX[k][:, qt * 512:(qt + 1) * 512],
                        start=(k == 0), stop=(k == HC - 1)),
                        inc=([("pe", 1)] if k == HC - 1 else None))
                pl.wait("act", "pe", pl.val("pe"))
                pl.run("act", lambda E, qt=qt: E.activation(
                    QN[6][0:64, qt * 512:(qt + 1) * 512], PS[5][0:64, :], AF.Copy),
                    inc=[("act", 1)])
                pl.bank_set(5, "act")
            ring.consumed(s)
            kpe_act = pl.val("act")

            pl.wait("sp", "dve", ckv_rms_dve)
            for oi in range(4):
                pl.run("sp", lambda E, oi=oi: E.dma_start(
                    out=ag_a_in[oi * 128:(oi + 1) * 128, :], in_=ckv_own[oi][:, :]),
                    inc=[("io", 16)])
            pl.wait("sp", "act", kpe_act)
            pl.run("sp", lambda E: E.dma_start(
                out=ag_a_in[512:576, :], in_=QN[6][0:64, 0:1024]), inc=[("io", 16)])
            a_store_io = pl.val("io")
            pl.wait("pool", "io", a_store_io)
            pl.run("pool", lambda E: E.collective_compute(
                "AllGather", ALU.bypass,
                replica_groups=[[0, 1], [2, 3], [4, 5], [6, 7]],
                ins=[ag_a_in.ap()], outs=[ag_a_out.ap()]), inc=[("cc0", 1)])

            def ql_tile(i):
                if i < 8:
                    return QP[i]
                return KVH[(i - 8) // 2][:, ((i - 8) % 2) * 1024:((i - 8) % 2 + 1) * 1024]
            pl.bank_wait(4)
            pl.bank_wait(5)
            fd = Feeder(ring, [(wqa[oi], 2048) for oi in range(12)], 2)
            for oi in range(12):
                s = fd.get(oi)
                ring.wait_ready("pe", s)
                for qt in range(2):
                    bank = (oi % 2) * 2 + qt
                    pl.bank_wait(bank)
                    psum = PS[bank]
                    for k in range(HC):
                        pl.run("pe", lambda E, s=s, k=k, qt=qt, psum=psum: E.matmul(
                            psum[:, :], ring.ap(s, k * 128, 128),
                            LNX[k][:, qt * 512:(qt + 1) * 512],
                            start=(k == 0), stop=(k == HC - 1)),
                            inc=([("pe", 1)] if k == HC - 1 else None))
                    mmv = pl.val("pe")
                    pl.wait("act", "pe", mmv)
                    pl.run("act", lambda E, oi=oi, qt=qt, psum=psum: E.activation(
                        ql_tile(oi)[:, qt * 512:(qt + 1) * 512], psum[:, :], AF.Copy,
                        scale=bias_col(oi)))
                    pl.run("act", lambda E, oi=oi, qt=qt, psum=psum: E.activation(
                        q2scr[qt][:, (oi % 2) * 512:(oi % 2) * 512 + 512], psum[:, :],
                        AF.Square), inc=[("act", 1)])
                    pl.bank_set(bank, "act")
                    pl.wait("pe", "act", pl.val("act"))
                    pl.run("pe", lambda E, oi=oi, qt=qt: E.matmul(
                        PS[4 + qt][0:1, :], ONEB[:, :],
                        q2scr[qt][:, (oi % 2) * 512:(oi % 2) * 512 + 512],
                        start=(oi == 0), stop=(oi == 11)),
                        inc=([("pe", 1)] if oi == 11 else None))
                ring.consumed(s)
            ql_stats_pe = pl.val("pe")

            QR = SROWS[0:1, 1024:2048]
            pl.wait("dve", "pe", ql_stats_pe)
            for qt in range(2):
                pl.run("dve", lambda E, qt=qt: E.tensor_scalar(
                    QR[:, qt * 512:(qt + 1) * 512], PS[4 + qt][0:1, :],
                    1.0 / QLORA, 1e-6, ALU.mult, ALU.add),
                    inc=([("dve", 1)] if qt == 1 else None))
            pl.bank_set(4, "dve")
            pl.bank_set(5, "dve")
            pl.wait("act", "dve", pl.val("dve"))
            pl.run("act", lambda E: E.activation(QR[:, :], QR[:, :], AF.Sqrt),
                   inc=[("act", 1)])
            pl.wait("dve", "act", pl.val("dve") if False else pl.val("act"))
            pl.run("dve", lambda E: E.reciprocal(QR[:, :], QR[:, :]), inc=[("dve", 1)])
            pl.bank_set(5, "dve")
            pl.wait("pe", "dve", pl.val("dve"))
            QRB = F32S[:, 0:1024]
            pl.bank_wait(4)
            for qt in range(2):
                pl.run("pe", lambda E, qt=qt: E.matmul(
                    PS[4][:, :], ONEF[:, :], QR[:, qt * 512:(qt + 1) * 512],
                    start=True, stop=True), inc=[("pe", 1)])
                pl.wait("dve", "pe", pl.val("pe"))
                pl.run("dve", lambda E, qt=qt: E.tensor_copy(
                    QRB[:, qt * 512:(qt + 1) * 512], PS[4][:, :]), inc=[("dve", 1)])
                pl.wait("pe", "dve", pl.val("dve"))
            pl.bank_set(4, "dve")

            RQC = wtab(0)
            RQS = wtab(1)
            pl.run("pool", lambda E: E.dma_start(out=RQC, in_=ra_q[0]), inc=[("wt", 16)])
            pl.run("pool", lambda E: E.dma_start(out=RQS, in_=ra_q[1]), inc=[("wt", 16)])
            rq_w = pl.val("wt")
            pl.wait("dve", "io", a_store_io)
            fd = Feeder(ring, [(wqb[oi], 1536) for oi in range(24)], 2)
            for oi in range(24):
                s = fd.get(oi)
                ring.wait_ready("pe", s)
                for qt in range(2):
                    bank = (oi % 2) * 2 + qt
                    pl.bank_wait(bank)
                    psum = PS[bank]
                    for k in range(12):
                        pl.run("pe", lambda E, s=s, k=k, qt=qt, psum=psum: E.matmul(
                            psum[:, :], ring.ap(s, k * 128, 128),
                            ql_tile(k)[:, qt * 512:(qt + 1) * 512],
                            start=(k == 0), stop=(k == 11)),
                            inc=([("pe", 1)] if k == 11 else None))
                    mmv = pl.val("pe")
                    pl.wait("dve", "pe", mmv)
                    if oi < 16:
                        pl.run("dve", lambda E, oi=oi, qt=qt, psum=psum:
                               E.scalar_tensor_tensor(
                                   QN[oi][:, qt * 512:(qt + 1) * 512], psum[:, :], SA,
                                   QRB[:, qt * 512:(qt + 1) * 512], ALU.mult, ALU.mult),
                               inc=[("dve", 1)])
                    else:
                        pl.run("dve", lambda E, qt=qt, psum=psum: E.tensor_mul(
                            ROPS[:, 1024 + qt * 512:1024 + qt * 512 + 512],
                            psum[:, :], QRB[:, qt * 512:(qt + 1) * 512]),
                            inc=[("dve", 1)])
                    pl.bank_set(bank, "dve")
                if oi >= 16:
                    g = oi - 16
                    dst = LNX[8 + g]
                    src = ROPS[:, 1024:2048]
                    rot = ROPS[:, 0:1024]
                    pl.wait("dve", "wt", rq_w)
                    pl.run("dve", lambda E: E.tensor_scalar_mul(rot[0:32, :], src[32:64, :], -1.0))
                    pl.run("dve", lambda E: E.tensor_copy(rot[32:64, :], src[0:32, :]))
                    pl.run("dve", lambda E: E.tensor_scalar_mul(rot[64:96, :], src[96:128, :], -1.0))
                    pl.run("dve", lambda E: E.tensor_copy(rot[96:128, :], src[64:96, :]))
                    pl.run("dve", lambda E, dst=dst: E.tensor_mul(dst[:, :], src[:, :], RQC))
                    pl.run("dve", lambda E: E.tensor_mul(rot[:, :], rot[:, :], RQS))
                    pl.run("dve", lambda E, dst=dst: E.tensor_add(dst[:, :], dst[:, :], rot[:, :]),
                           inc=[("dve", 1)])
                ring.consumed(s)
            q_dve = pl.val("dve")
            dbgdump(QN[0][:, :])
            dbgdump(LNX[8][:, :])
            dbgdump(ql_tile(0)[:, :])
            dbgdump(QRB)

            pl.wait("sp", "cc0", 1)
            pl.wait("sp", "pe", pl.val("pe"))
            for oi in range(4):
                for r in range(2):
                    pl.run("sp", lambda E, oi=oi, r=r: E.dma_start(
                        out=QP[2 * oi + r][:, :],
                        in_=ag_a_out[r * 576 + oi * 128:r * 576 + (oi + 1) * 128, :]),
                        inc=[("io", 16)])
            pl.wait("pool", "dve", q_dve)
            for i in range(4):
                pl.run("pool", lambda E, i=i: E.dma_start(out=wtab(i), in_=ra_k[i]),
                       inc=[("wt", 16)])
            rk_w = pl.val("wt")
            kw = ROPS[:, 2048:3072]
            rot = ROPS[:, 0:1024]
            t1 = ROPS[:, 1024:2048]
            for r in range(2):
                pl.wait("sp", "dve", pl.val("dve"))
                pl.run("sp", lambda E, r=r: E.dma_start(
                    out=kw[0:64, 0:1024], in_=ag_a_out[r * 576 + 512:r * 576 + 576, :]),
                    inc=[("io", 16)])
                kio = pl.val("io")
                pl.wait("dve", "io", kio)
                pl.wait("dve", "wt", rk_w)
                ctab = wtab(0 + r)
                stab = wtab(2 + r)
                pl.run("dve", lambda E: E.tensor_scalar_mul(rot[0:32, :], kw[32:64, 0:1024], -1.0))
                pl.run("dve", lambda E: E.tensor_copy(rot[32:64, :], kw[0:32, 0:1024]))
                pl.run("dve", lambda E, ctab=ctab: E.tensor_mul(
                    t1[0:64, :], kw[0:64, 0:1024], ctab[0:64, :]))
                pl.run("dve", lambda E, stab=stab: E.tensor_mul(
                    rot[0:64, :], rot[0:64, :], stab[0:64, :]))
                pl.run("dve", lambda E, r=r: E.tensor_add(
                    KVH[0][0:64, r * 1024:(r + 1) * 1024], t1[0:64, :], rot[0:64, :]))
                pl.run("dve", lambda E, r=r: E.tensor_copy(
                    KVH[0][64:128, r * 1024:(r + 1) * 1024],
                    KVH[0][0:64, r * 1024:(r + 1) * 1024]), inc=[("dve", 1)])
            kpe_dve = pl.val("dve")
            ckv_io = pl.val("io")
            dbgdump(KVH[0][0:64, :])
            dbgdump(QP[0][:, :])

            pl.phase("attn_a_heads")
            srcs = []
            for h in range(NH):
                srcs.append((wkvb[h], 512))
                srcs.append((wkvb[16 + h], 512))
            fdm = Feeder(ringm, srcs, 2)
            for h in range(NH):
                sk_ = fdm.get(2 * h)
                sv_ = fdm.get(2 * h + 1)
                ringm.wait_ready("pe", sv_)
                ringm.wait_ready("pe", sk_)
                pl.wait("pe", "io", ckv_io)
                pl.wait("pe", "dve", kpe_dve)
                for tk in range(4):
                    pl.bank_wait(4)
                    for k in range(4):
                        pl.run("pe", lambda E, sk_=sk_, k=k, tk=tk: E.matmul(
                            PS[4][:, :], ringm.ap(sk_, k * 128, 128),
                            QP[2 * k + tk // 2][:, (tk % 2) * 512:(tk % 2) * 512 + 512],
                            start=(k == 0), stop=(k == 3)),
                            inc=([("pe", 1)] if k == 3 else None))
                    pl.wait("act", "pe", pl.val("pe"))
                    pl.run("act", lambda E, tk=tk: E.activation(
                        KVH[1][:, tk * 512:(tk + 1) * 512], PS[4][:, :], AF.Copy),
                        inc=[("act", 1)])
                    pl.bank_set(4, "act")
                for jq in range(4):
                    pl.bank_wait(5)
                    for jj4 in range(4):
                        j = jq * 4 + jj4
                        for k in range(4):
                            pl.run("pe", lambda E, sv_=sv_, k=k, j=j, jj4=jj4: E.matmul(
                                PS[5][:, jj4 * 128:(jj4 + 1) * 128],
                                QP[2 * k + j // 8][:, (j % 8) * 128:(j % 8) * 128 + 128],
                                ringm.ap(sv_, k * 128, 128),
                                start=(k == 0), stop=(k == 3)),
                                inc=([("pe", 1)] if (k == 3 and jj4 == 3) else None))
                    pl.wait("act", "pe", pl.val("pe"))
                    pl.run("act", lambda E, jq=jq: E.activation(
                        ROPS[:, jq * 512:(jq + 1) * 512], PS[5][:, :], AF.Copy),
                        inc=[("act", 1)])
                    pl.bank_set(5, "act")
                kv_act = pl.val("act")
                ringm.consumed(sk_)
                ringm.consumed(sv_)
                pl.wait("pe", "act", kv_act)

                def scores_ops(qt, j0, nj, psum, h=h):
                    pl.run("pe", lambda E, qt=qt, j0=j0, nj=nj, psum=psum, h=h: E.matmul(
                        psum[:, 0:nj * 128], QN[h][:, qt * 128:(qt + 1) * 128],
                        KVH[1][:, j0 * 128:(j0 + nj) * 128],
                        start=True, stop=False))
                    hb = 64 * (h % 2)
                    pl.run("pe", lambda E, qt=qt, j0=j0, nj=nj, psum=psum, h=h, hb=hb:
                           E.matmul(
                               psum[:, 0:nj * 128],
                               LNX[8 + h // 2][hb:hb + 64, qt * 128:(qt + 1) * 128],
                               KVH[0][hb:hb + 64, j0 * 128:(j0 + nj) * 128],
                               start=False, stop=True), inc=[("pe", 1)])

                def vblk_ap(j):
                    return ROPS[:, j * 128:(j + 1) * 128]

                def probs_of(qi, j0, nj):
                    t = LNX[2 * qi] if j0 < 8 else LNX[2 * qi + 1]
                    c0 = (j0 % 8) * 128
                    return t[:, c0:c0 + nj * 128]

                if h == 0:
                    dbgdump(KVH[1][:, :])
                    dbgdump(ROPS[:, 0:2048])
                attn_inner(h, scores_ops, vblk_ap, probs_of,
                           dbg_head=(h == 0 and NDBG > 0))
                if h == 0:
                    dbgdump(QN[0][:, :])

            pl.phase("attn_a_wo")
            proj_add(lambda o: wo[o], 16,
                     lambda k, qt: QN[k][:, qt * 512:(qt + 1) * 512],
                     pre_pe=[("act", pl.val("act"))])

        def ffn():
            lnx_dve = layernorm()

            s = ring.load(rw[:, :], cols=128)
            ring.wait_ready("pe", s)
            pl.wait("pe", "dve", lnx_dve)
            LGF = F32S[0:8, 0:1024]
            for qt in range(2):
                pl.bank_wait(4)
                for k in range(HC):
                    pl.run("pe", lambda E, s=s, k=k, qt=qt: E.matmul(
                        PS[4][0:8, :], ring.ap(s, k * 8, 8),
                        LNX[k][:, qt * 512:(qt + 1) * 512],
                        start=(k == 0), stop=(k == HC - 1)),
                        inc=([("pe", 1)] if k == HC - 1 else None))
                pl.wait("act", "pe", pl.val("pe"))
                pl.run("act", lambda E, qt=qt: E.activation(
                    LGF[:, qt * 512:(qt + 1) * 512], PS[4][0:8, :], AF.Copy),
                    inc=[("act", 1)])
                pl.bank_set(4, "act")
            ring.consumed(s)
            lg_act = pl.val("act")
            pl.wait("pe", "act", lg_act)
            pl.bank_wait(5)
            for half in range(2):
                for b4 in range(4):
                    blk = half * 4 + b4
                    pl.run("pe", lambda E, blk=blk, b4=b4: E.transpose(
                        PS[5][:, b4 * 8:b4 * 8 + 8],
                        LGF[:, blk * 128:(blk + 1) * 128], IDF[0:8, 0:8]))
                pl.run("pe", lambda E: E.drain(), inc=[("pe", 1)])
                pl.wait("dve", "pe", pl.val("pe"))
                for b4 in range(4):
                    blk = half * 4 + b4
                    pl.run("dve", lambda E, blk=blk, b4=b4: E.tensor_copy(
                        RTR[:, blk * 8:blk * 8 + 8], PS[5][:, b4 * 8:b4 * 8 + 8]),
                        inc=([("dve", 1)] if b4 == 3 else None))
                pl.wait("pe", "dve", pl.val("dve"))
            pl.bank_set(5, "dve")
            for blk in range(8):
                P8 = RTR[:, blk * 8:blk * 8 + 8]
                CMB = RTR[:, 64 + blk * 8:64 + blk * 8 + 8]
                T8 = RTR[:, 128 + blk * 8:128 + blk * 8 + 8]
                pl.run("dve", lambda E, P8=P8: E.reduce_max(
                    SCOL[:, 0:1], P8, axis=AX.X, negate=True), inc=[("dve", 1)], guard=True)
                pl.wait("act", "dve", pl.val("dve"))
                pl.run("act", lambda E, P8=P8: E.activation(
                    P8, P8, AF.Exp, bias=SCOL[:, 0:1], accum_out=SCOL[:, 1:2]))
                pl.run("act", lambda E: E.drain(), inc=[("act", 1)])
                pl.wait("dve", "act", pl.val("act"))
                pl.run("dve", lambda E: E.reciprocal(SCOL[:, 2:3], SCOL[:, 1:2]), guard=True)
                pl.run("dve", lambda E, P8=P8: E.tensor_scalar_mul(P8, P8, SCOL[:, 2:3]), guard=True)
                pl.run("dve", lambda E, P8=P8: E.reduce_max(SCOL[:, 3:4], P8, axis=AX.X), guard=True)
                pl.run("dve", lambda E, P8=P8, CMB=CMB: E.tensor_scalar(
                    CMB, P8, SCOL[:, 3:4], None, ALU.is_ge), guard=True)
                pl.run("dve", lambda E, CMB=CMB, T8=T8: E.tensor_scalar(
                    T8, CMB, -1.0, 1.0, ALU.mult, ALU.add), guard=True)
                pl.run("dve", lambda E, P8=P8, T8=T8: E.tensor_mul(T8, T8, P8), guard=True)
                pl.run("dve", lambda E, T8=T8: E.reduce_max(SCOL[:, 4:5], T8, axis=AX.X), guard=True)
                pl.run("dve", lambda E, T8=T8: E.tensor_scalar(
                    T8, T8, SCOL[:, 4:5], None, ALU.is_ge), guard=True)
                pl.run("dve", lambda E: E.tensor_add(SCOL[:, 5:6], SCOL[:, 3:4], SCOL[:, 4:5]), guard=True)
                pl.run("dve", lambda E: E.reciprocal(SCOL[:, 5:6], SCOL[:, 5:6]), guard=True)
                pl.run("dve", lambda E: E.tensor_mul(SCOL[:, 6:7], SCOL[:, 3:4], SCOL[:, 5:6]), guard=True)
                pl.run("dve", lambda E: E.tensor_mul(SCOL[:, 7:8], SCOL[:, 4:5], SCOL[:, 5:6]), guard=True)
                pl.run("dve", lambda E, CMB=CMB: E.tensor_scalar_mul(CMB, CMB, SCOL[:, 6:7]), guard=True)
                pl.run("dve", lambda E, CMB=CMB, T8=T8: E.scalar_tensor_tensor(
                    CMB, T8, SCOL[:, 7:8], CMB, ALU.mult, ALU.add), inc=[("dve", 1)])
            cmb_dve = pl.val("dve")

            CBCF = F32S[:, 0:1024]
            CROW = SROWS[0:1, 0:1024]

            def compute_cbc(e):
                pl.wait("pe", "dve", cmb_dve)
                for half in range(2):
                    pl.bank_wait(4)
                    for b4 in range(4):
                        blk = half * 4 + b4
                        pl.run("pe", lambda E, blk=blk, b4=b4, e=e: E.transpose(
                            PS[4][0:1, b4 * 128:(b4 + 1) * 128],
                            RTR[:, 64 + blk * 8 + e:64 + blk * 8 + e + 1], IDF[:, :]))
                    pl.run("pe", lambda E: E.drain(), inc=[("pe", 1)])
                    pl.wait("dve", "pe", pl.val("pe"))
                    pl.run("dve", lambda E, half=half: E.tensor_copy(
                        CROW[:, half * 512:(half + 1) * 512], PS[4][0:1, 0:512]),
                        inc=[("dve", 1)])
                    pl.bank_set(4, "dve")
                pl.wait("pe", "dve", pl.val("dve"))
                for qt in range(2):
                    pl.bank_wait(4)
                    pl.run("pe", lambda E, qt=qt: E.matmul(
                        PS[4][:, :], ONEF[:, :], CROW[:, qt * 512:(qt + 1) * 512],
                        start=True, stop=True), inc=[("pe", 1)])
                    pl.wait("dve", "pe", pl.val("pe"))
                    pl.run("dve", lambda E, qt=qt: E.tensor_copy(
                        CBCF[:, qt * 512:(qt + 1) * 512], PS[4][:, :]), inc=[("dve", 1)])
                    pl.bank_set(4, "dve")
                    pl.wait("pe", "dve", pl.val("dve"))

            def gu_tile(i, ei):
                if ei == 0:
                    return QN[i]
                return QN[11 + i] if i < 5 else QP[i - 5]

            def gate_up(e, ei, with_bias, dst_fn):
                noch = FIC if not with_bias else FSC
                srcs = []
                for oi in range(noch):
                    srcs.append(((gw[e, oi] if not with_bias else sgw[oi]), 2048))
                    srcs.append(((uw[e, oi] if not with_bias else suw[oi]), 2048))
                fd = Feeder(ring, srcs, 2)
                for oi in range(noch):
                    sg_ = fd.get(2 * oi)
                    su_ = fd.get(2 * oi + 1)
                    ring.wait_ready("pe", su_)
                    ring.wait_ready("pe", sg_)
                    pl.wait("pe", "dve", lnx_dve)
                    for qt in range(2):
                        bank = qt
                        bank2 = 2 + qt
                        pl.bank_wait(bank)
                        for k in range(HC):
                            pl.run("pe", lambda E, sg_=sg_, k=k, qt=qt, bank=bank: E.matmul(
                                PS[bank][:, :], ring.ap(sg_, k * 128, 128),
                                LNX[k][:, qt * 512:(qt + 1) * 512],
                                start=(k == 0), stop=(k == HC - 1)),
                                inc=([("pe", 1)] if k == HC - 1 else None))
                        gate_pe = pl.val("pe")
                        pl.bank_wait(bank2)
                        for k in range(HC):
                            pl.run("pe", lambda E, su_=su_, k=k, qt=qt, bank2=bank2: E.matmul(
                                PS[bank2][:, :], ring.ap(su_, k * 128, 128),
                                LNX[k][:, qt * 512:(qt + 1) * 512],
                                start=(k == 0), stop=(k == HC - 1)),
                                inc=([("pe", 1)] if k == HC - 1 else None))
                        up_pe = pl.val("pe")
                        gcol = qt * 512
                        pl.wait("act", "pe", gate_pe)
                        if with_bias:
                            pl.run("act", lambda E, bank=bank, gcol=gcol, oi=oi: E.activation(
                                GSC[:, gcol:gcol + 512], PS[bank][:, :], AF.Silu,
                                bias=bias_col(80 + oi)), inc=[("act", 1)])
                        else:
                            pl.run("act", lambda E, bank=bank, gcol=gcol: E.activation(
                                GSC[:, gcol:gcol + 512], PS[bank][:, :], AF.Silu),
                                inc=[("act", 1)])
                        pl.bank_set(bank, "act")
                        pl.wait("dve", "act", pl.val("act"))
                        pl.wait("dve", "pe", up_pe)
                        dst_fn(oi, qt, gcol, bank2)
                        pl.bank_set(bank2, "dve")
                    ring.consumed(sg_)
                    ring.consumed(su_)

            def expert_pair(p):
                for ei in range(2):
                    e = 2 * p + ei
                    compute_cbc(e)

                    def dst_fn(oi, qt, gcol, bank2, ei=ei):
                        pl.run("dve", lambda E, gcol=gcol, qt=qt, bank2=bank2:
                               E.scalar_tensor_tensor(
                                   GSC[:, 1024 + gcol:1024 + gcol + 512],
                                   PS[bank2][:, :], 1.0,
                                   CBCF[:, qt * 512:(qt + 1) * 512],
                                   ALU.mult, ALU.mult))
                        pl.run("dve", lambda E, oi=oi, qt=qt, gcol=gcol, ei=ei: E.tensor_mul(
                            gu_tile(oi, ei)[:, qt * 512:(qt + 1) * 512],
                            GSC[:, 1024 + gcol:1024 + gcol + 512],
                            GSC[:, gcol:gcol + 512]), inc=[("dve", 1)])
                    gate_up(e, ei, False, dst_fn)
                gu_dve = pl.val("dve")

                srcs = []
                for och in range(16):
                    srcs.append((dwt[2 * p, och], FI))
                    srcs.append((dwt[2 * p + 1, och], FI))
                fd = Feeder(ring, srcs, 2)
                for och in range(16):
                    sd0 = fd.get(2 * och)
                    sd1 = fd.get(2 * och + 1)
                    ring.wait_ready("pe", sd1)
                    ring.wait_ready("pe", sd0)
                    pl.wait("pe", "dve", gu_dve)
                    for qt in range(2):
                        bank = (och % 2) * 2 + qt
                        pl.bank_wait(bank)
                        psum = PS[bank]
                        for ei in range(2):
                            sd = sd0 if ei == 0 else sd1
                            for k in range(FIC):
                                pl.run("pe", lambda E, sd=sd, k=k, qt=qt, psum=psum, ei=ei:
                                       E.matmul(psum[:, :], ring.ap(sd, k * 128, 128),
                                                gu_tile(k, ei)[:, qt * 512:(qt + 1) * 512],
                                                start=(ei == 0 and k == 0),
                                                stop=(ei == 1 and k == FIC - 1)),
                                       inc=([("pe", 1)] if (ei == 1 and k == FIC - 1)
                                            else None))
                        add_to_x(och, qt, psum, pl.val("pe"))
                        pl.bank_set(bank, "dve")
                    ring.consumed(sd0)
                    ring.consumed(sd1)

            for p in range(4):
                pl.phase(f"ffn_pair")
                expert_pair(p)

            pl.phase("ffn_shared")
            def sgu_tile(i):
                return QN[i] if i < 16 else QP[i - 16]

            def sdst_fn(oi, qt, gcol, bank2):
                pl.run("dve", lambda E, oi=oi, qt=qt, gcol=gcol, bank2=bank2:
                       E.scalar_tensor_tensor(
                           sgu_tile(oi)[:, qt * 512:(qt + 1) * 512],
                           PS[bank2][:, :], bias_col(102 + oi),
                           GSC[:, gcol:gcol + 512], ALU.add, ALU.mult),
                       inc=[("dve", 1)])
            gate_up(0, 0, True, sdst_fn)
            sgu_dve = pl.val("dve")
            srcs = []
            for och in range(16):
                srcs.append((sdw[och][:, 0:2048], 2048))
                srcs.append((sdw[och][:, 2048:2816], 768))
            fd = Feeder(ring, srcs, 2)
            for och in range(16):
                sda = fd.get(2 * och)
                sdb = fd.get(2 * och + 1)
                ring.wait_ready("pe", sdb)
                ring.wait_ready("pe", sda)
                pl.wait("pe", "dve", sgu_dve)
                for qt in range(2):
                    bank = (och % 2) * 2 + qt
                    pl.bank_wait(bank)
                    psum = PS[bank]
                    for k in range(FSC):
                        sd = sda if k < 16 else sdb
                        kk = k if k < 16 else k - 16
                        pl.run("pe", lambda E, sd=sd, kk=kk, k=k, qt=qt, psum=psum: E.matmul(
                            psum[:, :], ring.ap(sd, kk * 128, 128),
                            sgu_tile(k)[:, qt * 512:(qt + 1) * 512],
                            start=(k == 0), stop=(k == FSC - 1)),
                            inc=([("pe", 1)] if k == FSC - 1 else None))
                    add_to_x(och, qt, psum, pl.val("pe"), bias_idx=124 + och)
                    pl.bank_set(bank, "dve")
                ring.consumed(sda)
                ring.consumed(sdb)

        def attn_b():
            lnx_dve = layernorm()
            RQC = wtab(0)
            RQS = wtab(1)
            RKC = wtab(2)
            RKS = wtab(3)
            pl.run("pool", lambda E: E.dma_start(out=RQC, in_=rb_q[0]), inc=[("wt", 16)])
            pl.run("pool", lambda E: E.dma_start(out=RQS, in_=rb_q[1]), inc=[("wt", 16)])
            pl.run("pool", lambda E: E.dma_start(out=RKC, in_=rb_k[0]), inc=[("wt", 16)])
            pl.run("pool", lambda E: E.dma_start(out=RKS, in_=rb_k[1]), inc=[("wt", 16)])
            rb_w = pl.val("wt")

            kv_store_io = [0]
            fd = Feeder(ring, [(wqkv[oi], 2048) for oi in range(48)], 2)
            for oi in range(48):
                s = fd.get(oi)
                ring.wait_ready("pe", s)
                pl.wait("pe", "dve", lnx_dve)
                for qt in range(2):
                    bank = qt
                    pl.bank_wait(bank)
                    for k in range(HC):
                        pl.run("pe", lambda E, s=s, k=k, qt=qt, bank=bank: E.matmul(
                            PS[bank][:, :], ring.ap(s, k * 128, 128),
                            LNX[k][:, qt * 512:(qt + 1) * 512],
                            start=(k == 0), stop=(k == HC - 1)),
                            inc=([("pe", 1)] if k == HC - 1 else None))
                    mmv = pl.val("pe")
                    pl.wait("act", "pe", mmv)
                    pl.run("act", lambda E, oi=oi, qt=qt, bank=bank: E.activation(
                        ROPS[:, qt * 512:(qt + 1) * 512], PS[bank][:, :], AF.Identity,
                        bias=bias_col(16 + oi)), inc=[("act", 1)])
                    pl.bank_set(bank, "act")
                pj_act = pl.val("act")
                h = oi % 16
                if oi < 32:
                    is_q = oi < 16
                    dst = QN[h] if is_q else QP[h % 2]
                    src = ROPS[:, 0:1024]
                    rot = ROPS[:, 1024:2048]
                    ctab = RQC if is_q else RKC
                    stab = RQS if is_q else RKS
                    pl.wait("dve", "act", pj_act)
                    pl.wait("dve", "wt", rb_w)
                    if not is_q:
                        pl.wait("dve", "io", kv_store_io[0])
                    pl.run("dve", lambda E: E.tensor_scalar_mul(rot[0:16, :], src[32:48, :], -1.0))
                    pl.run("dve", lambda E: E.tensor_copy(rot[32:48, :], src[0:16, :]))
                    pl.run("dve", lambda E, dst=dst, ctab=ctab: E.tensor_mul(
                        dst[:, :], src[:, :], ctab))
                    pl.run("dve", lambda E, stab=stab: E.tensor_mul(
                        rot[0:48, :], rot[0:48, :], stab[0:48, :]))
                    pl.run("dve", lambda E, dst=dst: E.tensor_add(
                        dst[0:48, :], dst[0:48, :], rot[0:48, :]), inc=[("dve", 1)])
                    if not is_q:
                        pl.wait("sp", "dve", pl.val("dve"))
                        pl.run("sp", lambda E, h=h, dst=dst: E.dma_start(
                            out=ag_bk_in[h * 128:(h + 1) * 128, :], in_=dst[:, :]),
                            inc=[("io", 16)])
                        kv_store_io[0] = pl.val("io")
                else:
                    pl.wait("pe", "act", pj_act)
                    pl.wait("pe", "io", kv_store_io[0])
                    for blk in range(8):
                        pb = 6 + blk % 2
                        pl.bank_wait(pb)
                        pl.run("pe", lambda E, blk=blk: E.transpose(
                            PT[blk % 2][:, 0:128], ROPS[:, blk * 128:(blk + 1) * 128],
                            IDB[:, :]))
                        pl.run("pe", lambda E: E.drain(), inc=[("pe", 1)])
                        pl.wait("dve", "pe", pl.val("pe"))
                        pl.run("dve", lambda E, blk=blk: E.tensor_copy(
                            QP[2 + blk // 4][:, (blk % 4) * 128:(blk % 4) * 128 + 128],
                            PT[blk % 2][:, 0:128]), inc=[("dve", 1)])
                        pl.bank_set(pb, "dve")
                    pl.wait("sp", "dve", pl.val("dve"))
                    for half in range(2):
                        pl.run("sp", lambda E, h=h, half=half: E.dma_start(
                            out=ag_bv_in[h * 1024 + half * 512:h * 1024 + (half + 1) * 512, :]
                            .rearrange("(a p) d -> p a d", p=128),
                            in_=QP[2 + half][:, 0:512]), inc=[("io", 16)])
                    kv_store_io[0] = pl.val("io")
                ring.consumed(s)

            pl.wait("pool", "io", pl.val("io"))
            pl.run("pool", lambda E: E.collective_compute(
                "AllGather", ALU.bypass,
                replica_groups=[[0, 1], [2, 3], [4, 5], [6, 7]],
                ins=[ag_bk_in.ap()], outs=[ag_bk_out.ap()]), inc=[("cc1", 1)])
            pl.run("pool", lambda E: E.collective_compute(
                "AllGather", ALU.bypass,
                replica_groups=[[0, 1], [2, 3], [4, 5], [6, 7]],
                ins=[ag_bv_in.ap()], outs=[ag_bv_out.ap()]), inc=[("cc2", 1)])

            pl.phase("attn_b_heads")
            pl.wait("sp", "cc1", 1)
            pl.wait("sp", "cc2", 1)
            for h in range(NH):
                pl.wait("sp", "pe", pl.val("pe"))
                pl.wait("sp", "act", pl.val("act"))
                for r in range(2):
                    pl.run("sp", lambda E, h=h, r=r: E.dma_start(
                        out=KVH[0][:, r * 1024:(r + 1) * 1024],
                        in_=ag_bk_out[r * 2048 + h * 128:r * 2048 + (h + 1) * 128, :]),
                        inc=[("io", 16)])
                    pl.run("sp", lambda E, h=h, r=r: E.dma_start(
                        out=KVH[1][:, r * 1024:(r + 1) * 1024],
                        in_=ag_bv_out[r * NH * 1024 + h * 1024:
                                      r * NH * 1024 + (h + 1) * 1024, :]
                        .rearrange("(a p) d -> p a d", p=128)), inc=[("io", 16)])
                kv_io = pl.val("io")
                pl.wait("pe", "io", kv_io)

                def scores_ops(qt, j0, nj, psum, h=h):
                    pl.run("pe", lambda E, qt=qt, j0=j0, nj=nj, psum=psum, h=h: E.matmul(
                        psum[:, 0:nj * 128], QN[h][:, qt * 128:(qt + 1) * 128],
                        KVH[0][:, j0 * 128:(j0 + nj) * 128],
                        start=True, stop=True), inc=[("pe", 1)])

                def vblk_ap(j):
                    return KVH[1][:, j * 128:(j + 1) * 128]

                def probs_of(qi, j0, nj):
                    t = LNX[2 * qi] if j0 < 8 else LNX[2 * qi + 1]
                    c0 = (j0 % 8) * 128
                    return t[:, c0:c0 + nj * 128]

                attn_inner(h, scores_ops, vblk_ap, probs_of)

            proj_add(lambda o: wdense[o], 16,
                     lambda k, qt: QN[k][:, qt * 512:(qt + 1) * 512],
                     bias_base=64, pre_pe=[("act", pl.val("act"))])

        def tap_store(i):
            if not TAPS:
                return
            pl.wait("pool", "dve", pl.val("dve"))
            pl.wait("pool", "act", pl.val("act"))
            pl.wait("pool", "pe", pl.val("pe"))
            for c in range(HC):
                pl.run("pool", lambda E, c=c, i=i: E.dma_start(
                    out=taps[i][c, :, :], in_=X[c][:, :]), inc=[("out", 16)])

        if N_SUB >= 1:
            pl.phase("attn_a")
            attn_a()
            tap_store(0)
        if N_SUB >= 2:
            pl.phase("ffn1")
            ffn()
            tap_store(1)
        if N_SUB >= 3:
            pl.phase("attn_b")
            attn_b()
            tap_store(2)
        if N_SUB >= 4:
            pl.phase("ffn2")
            ffn()
            tap_store(3)
        pl.phase("fin")

        pl.wait("pool", "dve", pl.val("dve"))
        pl.wait("pool", "act", pl.val("act"))
        pl.wait("pool", "pe", pl.val("pe"))
        for c in range(HC):
            pl.run("pool", lambda E, c=c: E.dma_start(out=yout[c, :, :], in_=X[c][:, :]),
                   inc=[("out", 16)])
        pl.wait("pool", "out", pl.val("out"))
        pl.wait("sp", "out", pl.val("out"))

        blk = E(nc.Block())

        def walk(Eng, lst):
            import contextlib as _ctx
            cur = None
            ctx = _ctx.nullcontext()
            ctx.__enter__()
            for f in lst:
                if isinstance(f, str):
                    ctx.__exit__(None, None, None)
                    ctx = nc.named_scope(f)
                    ctx.__enter__()
                else:
                    f(Eng)
            ctx.__exit__(None, None, None)

        @blk.tensor
        def _(Eng):
            walk(Eng, pl.q["pe"])

        @blk.vector
        def _(Eng):
            walk(Eng, pl.q["dve"])

        @blk.scalar
        def _(Eng):
            walk(Eng, pl.q["act"])

        @blk.sync
        def _(Eng):
            walk(Eng, pl.q["sp"])

        @blk.gpsimd
        def _(Eng):
            walk(Eng, pl.q["pool"])

    return nc


def kernel(**inputs):
    nc = build_nc()
    in_maps = [build_host_inputs(c, inputs) for c in range(NCORES)]
    res = run_bass_kernel_spmd(nc, in_maps, core_ids=list(range(NCORES)))
    out = np.zeros((B, S, H), dtype=np.float32)
    for c in range(NCORES):
        par, bidx = c % 2, c // 2
        y = np.asarray(res.results[c]["yout"])
        xt = y.reshape(H, TC).T.reshape(NBLK, 128, H)
        for i in range(NBLK):
            p = 2 * i + par
            out[bidx, p * 128:(p + 1) * 128, :] = xt[i]
    return out


# version 03
# speedup vs baseline: 1.0551x; 1.0120x over previous
import os
import contextlib
import numpy as np
import concourse.bass as bass
import concourse.mybir as mybir
from concourse.bass_utils import run_bass_kernel_spmd

F32 = mybir.dt.float32
BF16 = mybir.dt.bfloat16
AF = mybir.ActivationFunctionType
ALU = mybir.AluOpType
AX = mybir.AxisListType

NCORES = 8
B, S, H = 4, 2048, 2048
HC = H // 128
TC = 1024
NBLK = 8
NH = 16
QLORA = 1536
KVL = 512
DR = 64
DN = 128
SA = float((DN + DR) ** (-0.5))
NE = 8
FIC = 11
FI = 1408
FSC = 22
FSI = 2816
ROT = 32
SB_SC = float(128 ** (-0.5))

N_SUB = int(os.environ.get("BASS_MK_SUBLAYERS", "4"))
TAPS = os.environ.get("BASS_MK_TAPS", "0") == "1"
NDBG = int(os.environ.get("BASS_MK_DBG", "0"))
SYNCDBG = os.environ.get("BASS_MK_SYNC", "0") == "1"

POS_OF_BUF = [2 * j for j in range(8)] + [2 * j + 1 for j in range(8)]



def _tile_weight(w, och_i=128):
    K, O = w.shape
    kc = K // 128
    oc = O // och_i
    b = w.reshape(kc, 128, oc, och_i).transpose(2, 1, 0, 3).reshape(oc, 128, kc * och_i)
    return np.ascontiguousarray(b.astype(np.float32))


def _perm_head_b():
    x1 = list(range(0, 16))
    x2 = list(range(16, 32))
    rest = list(range(32, 128))
    return np.array(x1 + rest[:16] + x2 + rest[16:32] + rest[32:], dtype=np.int64)


def _cos_sin(positions, dim):
    inv = 1.0 / (10000.0 ** (np.arange(0, dim, 2, dtype=np.float64) / dim))
    fr = np.outer(positions.astype(np.float64), inv)
    emb = np.concatenate([fr, fr], axis=-1)
    return np.cos(emb), np.sin(emb)


def build_host_inputs(core, ins):
    par = core % 2
    bidx = core // 2
    own_pos = np.array([2 * i + par for i in range(NBLK)])
    d = {}

    xc = np.asarray(ins["L0_attn_a_x"], dtype=np.float32)[bidx]
    xc = xc.reshape(16, 128, H)[own_pos]
    d["xin"] = np.ascontiguousarray(xc.reshape(TC, H).T.reshape(HC, 128, TC))

    d["wqa"] = _tile_weight(np.asarray(ins["L0_attn_a_wq_a"]))
    wqb = np.asarray(ins["L0_attn_a_wq_b"])
    cols = []
    for h in range(NH):
        cols.extend(range(h * 192, h * 192 + DN))
    for g in range(NH // 2):
        cols.extend(range((2 * g) * 192 + DN, (2 * g) * 192 + 192))
        cols.extend(range((2 * g + 1) * 192 + DN, (2 * g + 1) * 192 + 192))
    d["wqb"] = _tile_weight(wqb[:, np.array(cols)])
    wkva = np.asarray(ins["L0_attn_a_wkv_a"])
    d["wkvac"] = _tile_weight(wkva[:, :KVL])
    d["wkvap"] = _tile_weight(wkva[:, KVL:], och_i=64)[0]
    wkvb = np.asarray(ins["L0_attn_a_wkv_b"])
    cols = []
    for h in range(NH):
        cols.extend(range(h * 256, h * 256 + DN))
    for h in range(NH):
        cols.extend(range(h * 256 + DN, h * 256 + 256))
    d["wkvb"] = _tile_weight(wkvb[:, np.array(cols)])
    d["wo"] = _tile_weight(np.asarray(ins["L0_attn_a_wo"]))

    d["rw"] = _tile_weight(np.ascontiguousarray(np.asarray(ins["L0_ffn_router_w"]).T),
                           och_i=8)[0]
    d["gw"] = np.stack([_tile_weight(np.asarray(ins["L0_ffn_gate_w"])[e]) for e in range(NE)])
    d["uw"] = np.stack([_tile_weight(np.asarray(ins["L0_ffn_up_w"])[e]) for e in range(NE)])
    d["dw"] = np.stack([_tile_weight(np.asarray(ins["L0_ffn_down_w"])[e]) for e in range(NE)])
    d["sgw"] = _tile_weight(np.asarray(ins["L0_ffn_s_gate_w"]))
    d["suw"] = _tile_weight(np.asarray(ins["L0_ffn_s_up_w"]))
    d["sdw"] = _tile_weight(np.asarray(ins["L0_ffn_s_down_w"]))

    wqkv = np.asarray(ins["L0_attn_b_W_qkv"])
    bqkv = np.asarray(ins["L0_attn_b_b_qkv"])
    perm = _perm_head_b()
    cols = []
    for h in range(NH):
        cols.extend((h * 384 + perm).tolist())
    for h in range(NH):
        cols.extend((h * 384 + 128 + perm).tolist())
    for h in range(NH):
        cols.extend(range(h * 384 + 256, h * 384 + 256 + 128))
    cols = np.array(cols)
    d["wqkv"] = _tile_weight(wqkv[:, cols])
    bq = bqkv[cols].astype(np.float32)
    d["wdense"] = _tile_weight(np.asarray(ins["L0_attn_b_W_dense"]))

    vec_list = []
    def addvec(v):
        v = np.asarray(v, dtype=np.float32).reshape(-1)
        vec_list.append(v.reshape(-1, 128).T)
    addvec(ins["L0_attn_a_q_a_ln_w"])
    addvec(ins["L0_attn_a_kv_a_ln_w"])
    addvec(bq)
    addvec(ins["L0_attn_b_b_dense"])
    addvec(ins["L0_ffn_s_gate_b"])
    addvec(ins["L0_ffn_s_up_b"])
    addvec(ins["L0_ffn_s_down_b"])
    d["vecs"] = np.ascontiguousarray(np.concatenate(vec_list, axis=1))

    d["identf"] = np.eye(128, dtype=np.float32)
    d["ones1"] = np.ones((1, 128), dtype=np.float32)
    d["onesc"] = np.ones((128, 1), dtype=np.float32)

    tri = np.where(np.arange(128)[None, :] <= np.arange(128)[:, None], 0.0, -1e30
                   ).astype(np.float32)
    zer = np.zeros((128, 128), np.float32)
    neg = np.full((128, 128), -1e30, np.float32)
    am = np.zeros((8, 2, 128, 128), np.float32)
    for qt in range(8):
        am[qt, 0] = tri if par == 0 else zer
        am[qt, 1] = neg if par == 0 else tri
    d["amask"] = np.ascontiguousarray(am.transpose(2, 0, 1, 3).reshape(128, 2048))

    own_tok = (own_pos[:, None] * 128 + np.arange(128)[None, :]).reshape(-1)
    buf_tok = (np.array(POS_OF_BUF)[:, None] * 128 + np.arange(128)[None, :]).reshape(-1)

    ca, sa_ = _cos_sin(own_tok, DR)
    t = np.zeros((2, 128, TC), np.float32)
    t[0, 0:64] = ca.T * SA
    t[0, 64:128] = ca.T * SA
    t[1, 0:64] = sa_.T * SA
    t[1, 64:128] = sa_.T * SA
    d["ra_q"] = t
    ck, sk = _cos_sin(buf_tok, DR)
    t = np.zeros((4, 128, 1024), np.float32)
    t[0, 0:64] = ck.T[:, 0:1024]
    t[1, 0:64] = ck.T[:, 1024:2048]
    t[2, 0:64] = sk.T[:, 0:1024]
    t[3, 0:64] = sk.T[:, 1024:2048]
    d["ra_k"] = t
    cb, sb = _cos_sin(own_tok, ROT)
    def tb(scale):
        cos_t = np.full((128, TC), scale, np.float32)
        sin_t = np.zeros((128, TC), np.float32)
        cos_t[0:16] = cb[:, 0:16].T * scale
        cos_t[32:48] = cb[:, 16:32].T * scale
        sin_t[0:16] = sb[:, 0:16].T * scale
        sin_t[32:48] = sb[:, 16:32].T * scale
        return np.stack([cos_t, sin_t])
    d["rb_q"] = tb(SB_SC)
    d["rb_k"] = tb(1.0)
    return d



ENGINES = ("pe", "dve", "act", "sp", "pool")


class Plan:
    def __init__(self, sems):
        self.sems = sems
        self.q = {e: [] for e in ENGINES}
        self.c = {n: 0 for n in sems}
        self._lastwait = {e: {} for e in ENGINES}
        self.bank = {}

    def wait(self, eng, sem, val=None):
        v = self.c[sem] if val is None else val
        if v <= 0:
            return
        if self._lastwait[eng].get(sem, -1) >= v:
            return
        self._lastwait[eng][sem] = v
        h = self.sems[sem]
        self.q[eng].append(lambda E, h=h, v=v: E.wait_ge(h, v))

    def run(self, eng, fn, inc=None, guard=False):
        incs = []
        if inc:
            for s_, d_ in inc:
                self.c[s_] += d_
                incs.append((self.sems[s_], d_))
        need_drain = (bool(incs) and eng in ("dve", "act")) or guard

        def cl(E, fn=fn, incs=incs, need_drain=need_drain):
            inst = fn(E)
            if need_drain:
                inst = E.drain(fusable=False)
            for h, d_ in incs:
                inst.then_inc(h, d_)
        self.q[eng].append(cl)

    def val(self, sem):
        return self.c[sem]

    def phase(self, name):
        for e in ENGINES:
            self.q[e].append(name)

    def barrier(self):
        for eng in ENGINES:
            for sem in ("pe", "dve", "act", "w", "io"):
                self.wait(eng, sem, self.c[sem])

    def bank_wait(self, bank):
        g = self.bank.get(bank)
        if g:
            self.wait("pe", g[0], g[1])

    def bank_set(self, bank, eng):
        self.bank[bank] = (eng, self.c[eng])


class Ring:
    def __init__(self, pl, arena, col0, cols_per_slot, nslots, sempfx):
        self.pl = pl
        self.arena = arena
        self.base = [col0 + i * cols_per_slot for i in range(nslots)]
        self.maxcols = cols_per_slot
        self.free_at = [0] * nslots
        self.semname = [f"{sempfx}{i}" for i in range(nslots)]
        self.nloads = [0] * nslots
        self.i = 0

    def load(self, src_ap, cols):
        assert cols <= self.maxcols
        pl = self.pl
        s = self.i % len(self.base)
        self.i += 1
        pl.wait("pool", "pe", self.free_at[s])
        ap = self.arena[:, self.base[s]:self.base[s] + cols]
        pl.run("pool", lambda E, ap=ap, src=src_ap: E.dma_start(out=ap, in_=src),
               inc=[(self.semname[s], 16)])
        self.nloads[s] += 1
        return s

    def wait_ready(self, eng, s):
        self.pl.wait(eng, self.semname[s], self.nloads[s] * 16)

    def ap(self, s, c0, cols):
        return self.arena[:, self.base[s] + c0:self.base[s] + c0 + cols]

    def consumed(self, s):
        self.free_at[s] = self.pl.val("pe")


class Feeder:
    def __init__(self, rng, srcs, ahead):
        self.rng = rng
        self.srcs = srcs
        self.ahead = ahead
        self.slots = []
        self.next = 0
        self._fill(ahead)

    def _fill(self, upto_extra):
        while self.next < len(self.srcs) and self.next < upto_extra:
            self.slots.append(self.rng.load(*self.srcs[self.next]))
            self.next += 1

    def get(self, i):
        while self.next < len(self.srcs) and self.next <= i + self.ahead:
            self.slots.append(self.rng.load(*self.srcs[self.next]))
            self.next += 1
        return self.slots[i]



def build_nc():
    nc = bass.Bass("TRN2", target_bir_lowering=False)

    DI = lambda name, shape: nc.dram_tensor(name, shape, F32, kind="ExternalInput")
    xin = DI("xin", [HC, 128, TC])
    wqa = DI("wqa", [12, 128, 2048])
    wqb = DI("wqb", [24, 128, 1536])
    wkvac = DI("wkvac", [4, 128, 2048])
    wkvap = DI("wkvap", [128, 1024])
    wkvb = DI("wkvb", [32, 128, 512])
    wo = DI("wo", [16, 128, 2048])
    rw = DI("rw", [128, 128])
    gw = DI("gw", [NE, FIC, 128, 2048])
    uw = DI("uw", [NE, FIC, 128, 2048])
    dwt = DI("dw", [NE, 16, 128, FI])
    sgw = DI("sgw", [FSC, 128, 2048])
    suw = DI("suw", [FSC, 128, 2048])
    sdw = DI("sdw", [16, 128, FSI])
    wqkv = DI("wqkv", [48, 128, 2048])
    wdense = DI("wdense", [16, 128, 2048])
    vecs = DI("vecs", [128, 140])
    identf = DI("identf", [128, 128])
    ones1 = DI("ones1", [1, 128])
    onesc = DI("onesc", [128, 1])
    amask = DI("amask", [128, 2048])
    ra_q = DI("ra_q", [2, 128, TC])
    ra_k = DI("ra_k", [4, 128, 1024])
    rb_q = DI("rb_q", [2, 128, TC])
    rb_k = DI("rb_k", [2, 128, TC])

    yout = nc.dram_tensor("yout", [HC, 128, TC], F32, kind="ExternalOutput")
    dbg = (nc.dram_tensor("dbg", [NDBG, 128, 2048], F32, kind="ExternalOutput")
           if NDBG else None)
    taps = [nc.dram_tensor(f"tap{i+1}", [HC, 128, TC], F32, kind="ExternalOutput")
            for i in range(4)] if TAPS else []

    ag_a_in = nc.dram_tensor("ag_a_in", [576, TC], BF16)
    ag_a_out = nc.dram_tensor("ag_a_out", [1152, TC], BF16)
    ag_bk_in = nc.dram_tensor("ag_bk_in", [2048, TC], BF16)
    ag_bk_out = nc.dram_tensor("ag_bk_out", [4096, TC], BF16)
    ag_bv_in = nc.dram_tensor("ag_bv_in", [NH * TC, 128], BF16)
    ag_bv_out = nc.dram_tensor("ag_bv_out", [2 * NH * TC, 128], BF16)

    with contextlib.ExitStack() as stack:
        E = stack.enter_context

        X = [E(nc.sbuf_tensor(f"x{c}", [128, TC], BF16)) for c in range(HC)]
        LNX = [E(nc.sbuf_tensor(f"lnx{c}", [128, TC], BF16)) for c in range(HC)]
        QN = [E(nc.sbuf_tensor(f"qn{h}", [128, TC], BF16)) for h in range(NH)]
        QP = [E(nc.sbuf_tensor(f"qp{g}", [128, TC], BF16)) for g in range(8)]
        KVH = [E(nc.sbuf_tensor(f"kvh{j}", [128, S], BF16)) for j in range(2)]
        WRB = E(nc.sbuf_tensor("wrb", [128, 16384], BF16))
        ROPS = E(nc.sbuf_tensor("rops", [128, 3072], BF16))
        GSC = E(nc.sbuf_tensor("gsc", [128, 2048], BF16))
        DSC = E(nc.sbuf_tensor("dsc", [128, 1024], BF16))
        F32S = E(nc.sbuf_tensor("f32s", [128, 1024], F32))
        SROWS = E(nc.sbuf_tensor("srows", [1, 2048], F32))
        SCOL = E(nc.sbuf_tensor("scol", [128, 32], F32))
        RTR = E(nc.sbuf_tensor("rtr", [128, 192], F32))
        AM = E(nc.sbuf_tensor("am", [128, 2048], F32))
        IDF = E(nc.sbuf_tensor("idf", [128, 128], F32))
        IDB = E(nc.sbuf_tensor("idb", [128, 128], BF16))
        ONEF = E(nc.sbuf_tensor("onef", [1, 128], F32))
        ONEB = E(nc.sbuf_tensor("oneb", [128, 1], BF16))
        VEC = E(nc.sbuf_tensor("vec", [128, 140], F32))

        PS = [E(nc.psum_tensor(f"ps{i}", [128, 512], F32)) for i in range(6)]
        PT = [E(nc.psum_tensor(f"pt{i}", [128, 1024], BF16)) for i in range(2)]

        sems = {}
        for name in ("w", "wi", "wt", "io", "pe", "dve", "act", "out",
                     "cc0", "cc1", "cc2",
                     "rg0", "rg1", "rg2", "rg3", "rm0", "rm1", "rm2", "rm3"):
            sems[name] = E(nc.semaphore(name))
        pl = Plan(sems)

        RING0 = 0
        RINGM = 4 * 2048
        WTAB = RINGM + 4 * 512
        PTS0 = WTAB + 4 * 1024
        ring = Ring(pl, WRB, RING0, 2048, 4, "rg")
        ringm = Ring(pl, WRB, RINGM, 512, 4, "rm")

        def wtab(i):
            return WRB[:, WTAB + i * 1024:WTAB + (i + 1) * 1024]

        def pts_ap(i, cols=512):
            return WRB[:, PTS0 + i * 512:PTS0 + i * 512 + cols]

        def bias_col(i):
            return VEC[:, i:i + 1]

        pl.run("sp", lambda E: E.dma_start(out=IDF[:, :], in_=identf[:, :]), inc=[("io", 16)])
        pl.run("pool", lambda E: E.dma_start(out=IDB[:, :], in_=identf[:, :]), inc=[("wi", 16)])
        pl.run("sp", lambda E: E.dma_start(out=ONEF[:, :], in_=ones1[:, :]), inc=[("io", 16)])
        pl.run("pool", lambda E: E.dma_start(out=ONEB[:, :], in_=onesc[:, :]), inc=[("wi", 16)])
        pl.run("sp", lambda E: E.dma_start(out=VEC[:, :], in_=vecs[:, :]), inc=[("io", 16)])
        pl.run("sp", lambda E: E.dma_start(
            out=AM[:, :], in_=amask[:, :]), inc=[("io", 16)])
        for c in range(HC):
            pl.run("pool", lambda E, c=c: E.dma_start(out=X[c][:, :], in_=xin[c, :, :]),
                   inc=[("wi", 16)])
        init_io = pl.val("io")
        init_w = pl.val("wi")

        def am_ap(qt, side):
            c0 = (qt * 2 + side) * 128
            return AM[:, c0:c0 + 128]

        dbg_n = [0]

        def dbgdump(ap):
            if not NDBG or dbg_n[0] >= NDBG:
                return
            i = dbg_n[0]
            dbg_n[0] += 1
            for eng_s in ("dve", "act", "pe", "io"):
                pl.wait("pool", eng_s, pl.val(eng_s))
            pl.run("pool", lambda E, ap=ap, i=i: E.dma_start(
                out=dbg[i, 0:ap.shape[0], 0:ap.shape[1]], in_=ap), inc=[("out", 16)])
            for eng_s in ("dve", "act", "pe", "sp", "pool"):
                pl.wait(eng_s, "out", pl.val("out"))

        def layernorm():
            pl.wait("act", "wi", init_w)
            pl.wait("pe", "wi", init_w)
            pl.wait("pe", "io", init_io)
            x2 = [ROPS[:, 0:1024], ROPS[:, 1024:2048]]
            for bk in range(4):
                pl.bank_wait(bk)
            chunk_pe = []
            for c in range(HC):
                if c >= 2:
                    pl.wait("act", "pe", chunk_pe[c - 2])
                pl.run("act", lambda E, c=c: E.activation(x2[c % 2], X[c][:, :], AF.Square),
                       inc=[("act", 1)])
                pl.wait("pe", "act", pl.val("act"))
                for qt in range(2):
                    pl.run("pe", lambda E, c=c, qt=qt: E.matmul(
                        PS[qt][0:1, :], ONEB[:, :], X[c][:, qt * 512:(qt + 1) * 512],
                        start=(c == 0), stop=(c == HC - 1)))
                for qt in range(2):
                    pl.run("pe", lambda E, c=c, qt=qt, t=x2[c % 2]: E.matmul(
                        PS[2 + qt][0:1, :], ONEB[:, :], t[:, qt * 512:(qt + 1) * 512],
                        start=(c == 0), stop=(c == HC - 1)),
                        inc=([("pe", 1)] if qt == 1 else None))
                chunk_pe.append(pl.val("pe"))
            stats_pe = pl.val("pe")
            MR = SROWS[0:1, 0:1024]
            R2 = SROWS[0:1, 1024:2048]
            pl.wait("dve", "pe", stats_pe)
            for qt in range(2):
                pl.run("dve", lambda E, qt=qt: E.tensor_scalar_mul(
                    MR[:, qt * 512:(qt + 1) * 512], PS[qt][0:1, :], 1.0 / H))
            for qt in range(2):
                pl.run("dve", lambda E, qt=qt: E.tensor_mul(
                    R2[:, qt * 512:(qt + 1) * 512], MR[:, qt * 512:(qt + 1) * 512],
                    MR[:, qt * 512:(qt + 1) * 512]))
            for qt in range(2):
                pl.run("dve", lambda E, qt=qt: E.scalar_tensor_tensor(
                    R2[:, qt * 512:(qt + 1) * 512], PS[2 + qt][0:1, :], 1.0 / H,
                    R2[:, qt * 512:(qt + 1) * 512], ALU.mult, ALU.subtract),
                    inc=[("dve", 1)])
            for bk in range(4):
                pl.bank_set(bk, "dve")
            pl.run("dve", lambda E: E.tensor_scalar_add(R2[:, :], R2[:, :], 1e-5),
                   inc=[("dve", 1)])
            pl.wait("act", "dve", pl.val("dve"))
            pl.run("act", lambda E: E.activation(R2[:, :], R2[:, :], AF.Sqrt),
                   inc=[("act", 1)])
            pl.wait("dve", "act", pl.val("act"))
            pl.run("dve", lambda E: E.reciprocal(R2[:, :], R2[:, :]))
            pl.run("dve", lambda E: E.tensor_mul(MR[:, :], MR[:, :], R2[:, :]))
            pl.run("dve", lambda E: E.tensor_scalar_mul(MR[:, :], MR[:, :], -1.0),
                   inc=[("dve", 1)])
            rows = pl.val("dve")
            RBCb = ROPS[:, 0:1024]
            B2Cb = ROPS[:, 1024:2048]
            pl.wait("pe", "dve", rows)
            pl.bank_wait(4)
            pl.bank_wait(5)
            for qt in range(2):
                pl.run("pe", lambda E, qt=qt: E.matmul(
                    PS[4][:, :], ONEF[:, :], R2[:, qt * 512:(qt + 1) * 512],
                    start=True, stop=True))
                pl.run("pe", lambda E, qt=qt: E.matmul(
                    PS[5][:, :], ONEF[:, :], MR[:, qt * 512:(qt + 1) * 512],
                    start=True, stop=True), inc=[("pe", 1)])
                pl.wait("dve", "pe", pl.val("pe"))
                pl.run("dve", lambda E, qt=qt: E.tensor_copy(
                    RBCb[:, qt * 512:(qt + 1) * 512], PS[4][:, :]))
                pl.run("dve", lambda E, qt=qt: E.tensor_copy(
                    B2Cb[:, qt * 512:(qt + 1) * 512], PS[5][:, :]), inc=[("dve", 1)])
                pl.wait("pe", "dve", pl.val("dve"))
            pl.bank_set(4, "dve")
            pl.bank_set(5, "dve")
            for c in range(HC):
                pl.run("dve", lambda E, c=c: E.tensor_mul(LNX[c][:, :], X[c][:, :], RBCb))
                pl.run("dve", lambda E, c=c: E.tensor_add(LNX[c][:, :], LNX[c][:, :], B2Cb),
                       inc=[("dve", 1)])
            return pl.val("dve")

        dsc_hist = []

        def add_to_x(och, qt, psum, mmv, bias_idx=None):
            dcol = (len(dsc_hist) % 2) * 512
            pl.wait("act", "pe", mmv)
            if len(dsc_hist) >= 2:
                pl.wait("act", "dve", dsc_hist[-2])
            if bias_idx is None:
                pl.run("act", lambda E, psum=psum, dcol=dcol: E.activation(
                    DSC[:, dcol:dcol + 512], psum[:, :], AF.Copy), inc=[("act", 1)])
            else:
                pl.run("act", lambda E, psum=psum, dcol=dcol, b=bias_idx: E.activation(
                    DSC[:, dcol:dcol + 512], psum[:, :], AF.Identity, bias=bias_col(b)),
                    inc=[("act", 1)])
            pl.wait("dve", "act", pl.val("act"))
            pl.run("dve", lambda E, och=och, qt=qt, dcol=dcol: E.tensor_add(
                X[och][:, qt * 512:(qt + 1) * 512], X[och][:, qt * 512:(qt + 1) * 512],
                DSC[:, dcol:dcol + 512]), inc=[("dve", 1)])
            dsc_hist.append(pl.val("dve"))
            return "dve"

        def proj_add(blob_fn, kc, src_fn, bias_base=None, pre_pe=None):
            fd = Feeder(ring, [(blob_fn(o), kc * 128) for o in range(16)], 2)
            for och in range(16):
                s = fd.get(och)
                ring.wait_ready("pe", s)
                if pre_pe:
                    for se, v in pre_pe:
                        pl.wait("pe", se, v)
                for qt in range(2):
                    bank = (och % 2) * 2 + qt
                    pl.bank_wait(bank)
                    psum = PS[bank]
                    for k in range(kc):
                        pl.run("pe", lambda E, s=s, k=k, qt=qt, psum=psum: E.matmul(
                            psum[:, :], ring.ap(s, k * 128, 128),
                            src_fn(k, qt), start=(k == 0), stop=(k == kc - 1)),
                            inc=([("pe", 1)] if k == kc - 1 else None))
                    add_to_x(och, qt, psum, pl.val("pe"),
                             bias_idx=(None if bias_base is None else bias_base + och))
                    pl.bank_set(bank, "dve")
                ring.consumed(s)

        def attn_inner(h, scores_ops, vblk_ap, probs_of, dbg_head=False):
            for g in range(2):
                njmax = 4 * g + 4
                for qi in range(4):
                    qt = g * 4 + qi
                    nt_need = 2 * ((qt + 1 + 3) // 4)
                    base = 4 if (qi % 2 == 1 and nt_need <= 2) else 0
                    tiles = []
                    bank = base
                    for (j0, side) in ((0, 0), (8, 1)):
                        off = 0
                        nj = qt + 1
                        while off < nj:
                            n = min(4, nj - off)
                            tiles.append((bank, j0 + off, n,
                                          side if off + n == nj else None))
                            bank += 1
                            off += n
                    if SYNCDBG:
                        pl.barrier()
                    for (bk, j0, n, side) in tiles:
                        pl.bank_wait(bk)
                        scores_ops(qt, j0, n, PS[bk])
                    mmv = pl.val("pe")
                    pl.wait("dve", "pe", mmv)
                    for (bk, j0, n, side) in tiles:
                        if side is not None:
                            c0 = (n - 1) * 128
                            pl.run("dve", lambda E, bk=bk, c0=c0, qt=qt, side=side:
                                   E.tensor_add(PS[bk][:, c0:c0 + 128],
                                                PS[bk][:, c0:c0 + 128],
                                                am_ap(qt, side)))
                    nt = len(tiles)
                    for i, (bk, j0, n, side) in enumerate(tiles):
                        last = (i == nt - 1) and nt == 1
                        pl.run("dve", lambda E, bk=bk, n=n, i=i: E.reduce_max(
                            SCOL[:, i:i + 1], PS[bk][:, 0:n * 128], axis=AX.X,
                            negate=True), inc=([("dve", 1)] if last else None),
                            guard=(i == nt - 1))
                    for i in range(1, nt):
                        pl.run("dve", lambda E, i=i: E.tensor_tensor(
                            SCOL[:, 0:1], SCOL[:, 0:1], SCOL[:, i:i + 1], ALU.min),
                            inc=([("dve", 1)] if i == nt - 1 else None), guard=True)
                    dv = pl.val("dve")
                    if SYNCDBG:
                        pl.barrier()
                    pl.wait("act", "dve", dv)
                    for i, (bk, j0, n, side) in enumerate(tiles):
                        pl.run("act", lambda E, bk=bk, j0=j0, n=n, i=i, qi=qi, qt=qt:
                               E.activation(probs_of(qi, j0, n),
                                            PS[bk][:, 0:n * 128], AF.Exp,
                                            bias=SCOL[:, 0:1],
                                            accum_out=SCOL[:, 8 + i:9 + i]))
                    pl.run("act", lambda E: E.drain(), inc=[("act", 1)])
                    for (bk, j0, n, side) in tiles:
                        pl.bank_set(bk, "act")
                    av = pl.val("act")
                    pl.wait("dve", "act", av)
                    for i in range(1, nt):
                        pl.run("dve", lambda E, i=i: E.tensor_add(
                            SCOL[:, 8:9], SCOL[:, 8:9], SCOL[:, 8 + i:9 + i]),
                            guard=True)
                    pl.run("dve", lambda E, qi=qi: E.reciprocal(
                        SCOL[:, 16 + qi:17 + qi], SCOL[:, 8:9]), inc=[("dve", 1)])
                    if dbg_head and g == 0 and qi == 1:
                        dbgdump(SCOL[:, 0:32])
                if dbg_head and g == 0:
                    dbgdump(LNX[0][:, :])
                    dbgdump(LNX[1][:, :])
                    dbgdump(LNX[2][:, :])
                    dbgdump(LNX[3][:, :])
                av_started = [False] * 4
                probs_dve = pl.val("dve")
                for qi in range(4):
                    pl.bank_wait(qi)
                for jj in range(2 * njmax):
                    j = (jj // 2) if jj % 2 == 0 else 8 + (jj - 1) // 2
                    jq = j % 8
                    if SYNCDBG:
                        pl.barrier()
                    pl.wait("pe", "act", pl.val("act"))
                    trs = [qi for qi in range(4) if jq <= g * 4 + qi]
                    pl.bank_wait(6 + jj % 2)
                    for idx, qi in enumerate(trs):
                        pl.run("pe", lambda E, qi=qi, j=j, jj=jj: E.transpose(
                            PT[jj % 2][:, qi * 128:(qi + 1) * 128],
                            probs_of(qi, j, 1), IDB[:, :]))
                    pl.run("pe", lambda E: E.drain(), inc=[("pe", 1)])
                    trv = pl.val("pe")
                    pl.wait("act", "pe", trv)
                    pl.run("act", lambda E, jj=jj: E.activation(
                        pts_ap(jj % 2), PT[jj % 2][:, 0:512], AF.Copy),
                        inc=[("act", 1)])
                    pl.bank_set(6 + jj % 2, "act")
                    if dbg_head and g == 0 and jj in (0, 1):
                        dbgdump(pts_ap(jj % 2))
                    ev = pl.val("act")
                    if SYNCDBG:
                        pl.barrier()
                    pl.wait("pe", "act", ev)
                    for qi in trs:
                        qt = g * 4 + qi
                        st_ = not av_started[qi]
                        sp_ = (j == 8 + qt)
                        pl.run("pe", lambda E, qi=qi, j=j, jj=jj, st_=st_, sp_=sp_:
                               E.matmul(
                                   PS[qi][:, 0:128],
                                   pts_ap(jj % 2)[:, qi * 128:qi * 128 + 128],
                                   vblk_ap(j),
                                   start=st_, stop=sp_),
                               inc=([("pe", 1)] if sp_ else None))
                        av_started[qi] = True
                avv = pl.val("pe")
                if SYNCDBG:
                    pl.barrier()
                pl.wait("dve", "pe", avv)
                if dbg_head and g == 0:
                    for qi2 in range(4):
                        pl.run("dve", lambda E, qi2=qi2: E.tensor_copy(
                            ROPS[:, 2048 + qi2 * 128:2048 + qi2 * 128 + 128]
                            if False else GSC[:, 1024 + qi2 * 128:1024 + qi2 * 128 + 128],
                            PS[qi2][:, 0:128]), inc=[("dve", 1)])
                    dbgdump(GSC[:, 1024:1536])
                    dbgdump(SCOL[:, 0:32])
                for qi in range(4):
                    pl.run("dve", lambda E, qi=qi: E.tensor_scalar_mul(
                        GSC[:, qi * 128:(qi + 1) * 128], PS[qi][:, 0:128],
                        SCOL[:, 16 + qi:17 + qi]), inc=[("dve", 1)])
                    pl.bank_set(qi, "dve")
                odv = pl.val("dve")
                if SYNCDBG:
                    pl.barrier()
                if dbg_head and g == 0:
                    dbgdump(GSC[:, 0:512])
                pl.wait("pe", "dve", odv)
                pl.bank_wait(6)
                for qi in range(4):
                    pl.run("pe", lambda E, qi=qi: E.transpose(
                        PT[0][:, qi * 128:(qi + 1) * 128],
                        GSC[:, qi * 128:(qi + 1) * 128], IDB[:, :]))
                pl.run("pe", lambda E: E.drain(), inc=[("pe", 1)])
                pl.wait("act", "pe", pl.val("pe"))
                pl.run("act", lambda E, h=h, g=g: E.activation(
                    QN[h][:, g * 512:(g + 1) * 512], PT[0][:, 0:512], AF.Copy),
                    inc=[("act", 1)])
                pl.bank_set(6, "act")

        def attn_a():
            lnx_dve = layernorm()
            pl.phase("attn_a_proj")
            dbgdump(LNX[0][:, :])
            q2scr = [QN[4][:, 0:1024], QN[5][:, 0:1024]]
            ckv_own = [QN[0], QN[1], QN[2], QN[3]]

            pl.bank_wait(4)
            fd = Feeder(ring, [(wkvac[oi], 2048) for oi in range(4)], 2)
            for oi in range(4):
                s = fd.get(oi)
                ring.wait_ready("pe", s)
                pl.wait("pe", "dve", lnx_dve)
                for qt in range(2):
                    bank = (oi % 2) * 2 + qt
                    pl.bank_wait(bank)
                    psum = PS[bank]
                    for k in range(HC):
                        pl.run("pe", lambda E, s=s, k=k, qt=qt, psum=psum: E.matmul(
                            psum[:, :], ring.ap(s, k * 128, 128),
                            LNX[k][:, qt * 512:(qt + 1) * 512],
                            start=(k == 0), stop=(k == HC - 1)),
                            inc=([("pe", 1)] if k == HC - 1 else None))
                    mmv = pl.val("pe")
                    pl.wait("act", "pe", mmv)
                    pl.run("act", lambda E, oi=oi, qt=qt, psum=psum: E.activation(
                        ckv_own[oi][:, qt * 512:(qt + 1) * 512], psum[:, :], AF.Copy,
                        scale=bias_col(12 + oi)))
                    pl.run("act", lambda E, oi=oi, qt=qt, psum=psum: E.activation(
                        q2scr[qt][:, (oi % 2) * 512:(oi % 2) * 512 + 512], psum[:, :],
                        AF.Square), inc=[("act", 1)])
                    pl.bank_set(bank, "act")
                    pl.wait("pe", "act", pl.val("act"))
                    pl.run("pe", lambda E, oi=oi, qt=qt: E.matmul(
                        PS[4 + qt][0:1, :], ONEB[:, :],
                        q2scr[qt][:, (oi % 2) * 512:(oi % 2) * 512 + 512],
                        start=(oi == 0), stop=(oi == 3)),
                        inc=([("pe", 1)] if oi == 3 else None))
                ring.consumed(s)
            ckv_stats_pe = pl.val("pe")

            CKR = SROWS[0:1, 0:1024]
            pl.wait("dve", "pe", ckv_stats_pe)
            for qt in range(2):
                pl.run("dve", lambda E, qt=qt: E.tensor_scalar(
                    CKR[:, qt * 512:(qt + 1) * 512], PS[4 + qt][0:1, :],
                    1.0 / KVL, 1e-6, ALU.mult, ALU.add),
                    inc=([("dve", 1)] if qt == 1 else None))
            pl.bank_set(4, "dve")
            pl.bank_set(5, "dve")
            pl.wait("act", "dve", pl.val("dve"))
            pl.run("act", lambda E: E.activation(CKR[:, :], CKR[:, :], AF.Sqrt),
                   inc=[("act", 1)])
            pl.wait("dve", "act", pl.val("act"))
            pl.run("dve", lambda E: E.reciprocal(CKR[:, :], CKR[:, :]), inc=[("dve", 1)])
            pl.bank_set(4, "dve")
            pl.wait("pe", "dve", pl.val("dve"))
            CKB = ROPS[:, 0:1024]
            pl.bank_wait(4)
            for qt in range(2):
                pl.run("pe", lambda E, qt=qt: E.matmul(
                    PS[4][:, :], ONEF[:, :], CKR[:, qt * 512:(qt + 1) * 512],
                    start=True, stop=True), inc=[("pe", 1)])
                pl.wait("dve", "pe", pl.val("pe"))
                pl.run("dve", lambda E, qt=qt: E.tensor_copy(
                    CKB[:, qt * 512:(qt + 1) * 512], PS[4][:, :]), inc=[("dve", 1)])
                pl.wait("pe", "dve", pl.val("dve"))
            pl.bank_set(4, "dve")
            for oi in range(4):
                pl.run("dve", lambda E, oi=oi: E.tensor_mul(
                    ckv_own[oi][:, :], ckv_own[oi][:, :], CKB[:, :]), inc=[("dve", 1)])
            ckv_rms_dve = pl.val("dve")
            dbgdump(QN[0][:, :])

            s = ring.load(wkvap[:, :], cols=1024)
            ring.wait_ready("pe", s)
            for qt in range(2):
                pl.bank_wait(5)
                for k in range(HC):
                    pl.run("pe", lambda E, s=s, k=k, qt=qt: E.matmul(
                        PS[5][0:64, :], ring.ap(s, k * 64, 64),
                        LNX[k][:, qt * 512:(qt + 1) * 512],
                        start=(k == 0), stop=(k == HC - 1)),
                        inc=([("pe", 1)] if k == HC - 1 else None))
                pl.wait("act", "pe", pl.val("pe"))
                pl.run("act", lambda E, qt=qt: E.activation(
                    QN[6][0:64, qt * 512:(qt + 1) * 512], PS[5][0:64, :], AF.Copy),
                    inc=[("act", 1)])
                pl.bank_set(5, "act")
            ring.consumed(s)
            kpe_act = pl.val("act")

            pl.wait("sp", "dve", ckv_rms_dve)
            for oi in range(4):
                pl.run("sp", lambda E, oi=oi: E.dma_start(
                    out=ag_a_in[oi * 128:(oi + 1) * 128, :], in_=ckv_own[oi][:, :]),
                    inc=[("io", 16)])
            pl.wait("sp", "act", kpe_act)
            pl.run("sp", lambda E: E.dma_start(
                out=ag_a_in[512:576, :], in_=QN[6][0:64, 0:1024]), inc=[("io", 16)])
            a_store_io = pl.val("io")
            pl.wait("pool", "io", a_store_io)
            pl.run("pool", lambda E: E.collective_compute(
                "AllGather", ALU.bypass,
                replica_groups=[[0, 1], [2, 3], [4, 5], [6, 7]],
                ins=[ag_a_in.ap()], outs=[ag_a_out.ap()]), inc=[("cc0", 1)])

            def ql_tile(i):
                if i < 8:
                    return QP[i]
                return KVH[(i - 8) // 2][:, ((i - 8) % 2) * 1024:((i - 8) % 2 + 1) * 1024]
            pl.bank_wait(4)
            pl.bank_wait(5)
            fd = Feeder(ring, [(wqa[oi], 2048) for oi in range(12)], 2)
            for oi in range(12):
                s = fd.get(oi)
                ring.wait_ready("pe", s)
                for qt in range(2):
                    bank = (oi % 2) * 2 + qt
                    pl.bank_wait(bank)
                    psum = PS[bank]
                    for k in range(HC):
                        pl.run("pe", lambda E, s=s, k=k, qt=qt, psum=psum: E.matmul(
                            psum[:, :], ring.ap(s, k * 128, 128),
                            LNX[k][:, qt * 512:(qt + 1) * 512],
                            start=(k == 0), stop=(k == HC - 1)),
                            inc=([("pe", 1)] if k == HC - 1 else None))
                    mmv = pl.val("pe")
                    pl.wait("act", "pe", mmv)
                    pl.run("act", lambda E, oi=oi, qt=qt, psum=psum: E.activation(
                        ql_tile(oi)[:, qt * 512:(qt + 1) * 512], psum[:, :], AF.Copy,
                        scale=bias_col(oi)))
                    pl.run("act", lambda E, oi=oi, qt=qt, psum=psum: E.activation(
                        q2scr[qt][:, (oi % 2) * 512:(oi % 2) * 512 + 512], psum[:, :],
                        AF.Square), inc=[("act", 1)])
                    pl.bank_set(bank, "act")
                    pl.wait("pe", "act", pl.val("act"))
                    pl.run("pe", lambda E, oi=oi, qt=qt: E.matmul(
                        PS[4 + qt][0:1, :], ONEB[:, :],
                        q2scr[qt][:, (oi % 2) * 512:(oi % 2) * 512 + 512],
                        start=(oi == 0), stop=(oi == 11)),
                        inc=([("pe", 1)] if oi == 11 else None))
                ring.consumed(s)
            ql_stats_pe = pl.val("pe")

            QR = SROWS[0:1, 1024:2048]
            pl.wait("dve", "pe", ql_stats_pe)
            for qt in range(2):
                pl.run("dve", lambda E, qt=qt: E.tensor_scalar(
                    QR[:, qt * 512:(qt + 1) * 512], PS[4 + qt][0:1, :],
                    1.0 / QLORA, 1e-6, ALU.mult, ALU.add),
                    inc=([("dve", 1)] if qt == 1 else None))
            pl.bank_set(4, "dve")
            pl.bank_set(5, "dve")
            pl.wait("act", "dve", pl.val("dve"))
            pl.run("act", lambda E: E.activation(QR[:, :], QR[:, :], AF.Sqrt),
                   inc=[("act", 1)])
            pl.wait("dve", "act", pl.val("dve") if False else pl.val("act"))
            pl.run("dve", lambda E: E.reciprocal(QR[:, :], QR[:, :]), inc=[("dve", 1)])
            pl.bank_set(5, "dve")
            pl.wait("pe", "dve", pl.val("dve"))
            QRB = F32S[:, 0:1024]
            pl.bank_wait(4)
            for qt in range(2):
                pl.run("pe", lambda E, qt=qt: E.matmul(
                    PS[4][:, :], ONEF[:, :], QR[:, qt * 512:(qt + 1) * 512],
                    start=True, stop=True), inc=[("pe", 1)])
                pl.wait("dve", "pe", pl.val("pe"))
                pl.run("dve", lambda E, qt=qt: E.tensor_copy(
                    QRB[:, qt * 512:(qt + 1) * 512], PS[4][:, :]), inc=[("dve", 1)])
                pl.wait("pe", "dve", pl.val("dve"))
            pl.bank_set(4, "dve")

            RQC = wtab(0)
            RQS = wtab(1)
            pl.run("pool", lambda E: E.dma_start(out=RQC, in_=ra_q[0]), inc=[("wt", 16)])
            pl.run("pool", lambda E: E.dma_start(out=RQS, in_=ra_q[1]), inc=[("wt", 16)])
            rq_w = pl.val("wt")
            pl.wait("dve", "io", a_store_io)
            fd = Feeder(ring, [(wqb[oi], 1536) for oi in range(24)], 2)
            for oi in range(24):
                s = fd.get(oi)
                ring.wait_ready("pe", s)
                for qt in range(2):
                    bank = (oi % 2) * 2 + qt
                    pl.bank_wait(bank)
                    psum = PS[bank]
                    for k in range(12):
                        pl.run("pe", lambda E, s=s, k=k, qt=qt, psum=psum: E.matmul(
                            psum[:, :], ring.ap(s, k * 128, 128),
                            ql_tile(k)[:, qt * 512:(qt + 1) * 512],
                            start=(k == 0), stop=(k == 11)),
                            inc=([("pe", 1)] if k == 11 else None))
                    mmv = pl.val("pe")
                    pl.wait("dve", "pe", mmv)
                    if oi < 16:
                        pl.run("dve", lambda E, oi=oi, qt=qt, psum=psum:
                               E.scalar_tensor_tensor(
                                   QN[oi][:, qt * 512:(qt + 1) * 512], psum[:, :], SA,
                                   QRB[:, qt * 512:(qt + 1) * 512], ALU.mult, ALU.mult),
                               inc=[("dve", 1)])
                    else:
                        pl.run("dve", lambda E, qt=qt, psum=psum: E.tensor_mul(
                            ROPS[:, 1024 + qt * 512:1024 + qt * 512 + 512],
                            psum[:, :], QRB[:, qt * 512:(qt + 1) * 512]),
                            inc=[("dve", 1)])
                    pl.bank_set(bank, "dve")
                if oi >= 16:
                    g = oi - 16
                    dst = LNX[8 + g]
                    src = ROPS[:, 1024:2048]
                    rot = ROPS[:, 0:1024]
                    pl.wait("dve", "wt", rq_w)
                    pl.run("dve", lambda E: E.tensor_scalar_mul(rot[0:32, :], src[32:64, :], -1.0))
                    pl.run("dve", lambda E: E.tensor_copy(rot[32:64, :], src[0:32, :]))
                    pl.run("dve", lambda E: E.tensor_scalar_mul(rot[64:96, :], src[96:128, :], -1.0))
                    pl.run("dve", lambda E: E.tensor_copy(rot[96:128, :], src[64:96, :]))
                    pl.run("dve", lambda E, dst=dst: E.tensor_mul(dst[:, :], src[:, :], RQC))
                    pl.run("dve", lambda E: E.tensor_mul(rot[:, :], rot[:, :], RQS))
                    pl.run("dve", lambda E, dst=dst: E.tensor_add(dst[:, :], dst[:, :], rot[:, :]),
                           inc=[("dve", 1)])
                ring.consumed(s)
            q_dve = pl.val("dve")
            dbgdump(QN[0][:, :])
            dbgdump(LNX[8][:, :])
            dbgdump(ql_tile(0)[:, :])
            dbgdump(QRB)

            pl.wait("sp", "cc0", 1)
            pl.wait("sp", "pe", pl.val("pe"))
            for oi in range(4):
                for r in range(2):
                    pl.run("sp", lambda E, oi=oi, r=r: E.dma_start(
                        out=QP[2 * oi + r][:, :],
                        in_=ag_a_out[r * 576 + oi * 128:r * 576 + (oi + 1) * 128, :]),
                        inc=[("io", 16)])
            pl.wait("pool", "dve", q_dve)
            for i in range(4):
                pl.run("pool", lambda E, i=i: E.dma_start(out=wtab(i), in_=ra_k[i]),
                       inc=[("wt", 16)])
            rk_w = pl.val("wt")
            kw = ROPS[:, 2048:3072]
            rot = ROPS[:, 0:1024]
            t1 = ROPS[:, 1024:2048]
            for r in range(2):
                pl.wait("sp", "dve", pl.val("dve"))
                pl.run("sp", lambda E, r=r: E.dma_start(
                    out=kw[0:64, 0:1024], in_=ag_a_out[r * 576 + 512:r * 576 + 576, :]),
                    inc=[("io", 16)])
                kio = pl.val("io")
                pl.wait("dve", "io", kio)
                pl.wait("dve", "wt", rk_w)
                ctab = wtab(0 + r)
                stab = wtab(2 + r)
                pl.run("dve", lambda E: E.tensor_scalar_mul(rot[0:32, :], kw[32:64, 0:1024], -1.0))
                pl.run("dve", lambda E: E.tensor_copy(rot[32:64, :], kw[0:32, 0:1024]))
                pl.run("dve", lambda E, ctab=ctab: E.tensor_mul(
                    t1[0:64, :], kw[0:64, 0:1024], ctab[0:64, :]))
                pl.run("dve", lambda E, stab=stab: E.tensor_mul(
                    rot[0:64, :], rot[0:64, :], stab[0:64, :]))
                pl.run("dve", lambda E, r=r: E.tensor_add(
                    KVH[0][0:64, r * 1024:(r + 1) * 1024], t1[0:64, :], rot[0:64, :]))
                pl.run("dve", lambda E, r=r: E.tensor_copy(
                    KVH[0][64:128, r * 1024:(r + 1) * 1024],
                    KVH[0][0:64, r * 1024:(r + 1) * 1024]), inc=[("dve", 1)])
            kpe_dve = pl.val("dve")
            ckv_io = pl.val("io")
            dbgdump(KVH[0][0:64, :])
            dbgdump(QP[0][:, :])

            pl.phase("attn_a_heads")
            srcs = []
            for h in range(NH):
                srcs.append((wkvb[h], 512))
                srcs.append((wkvb[16 + h], 512))
            fdm = Feeder(ringm, srcs, 2)
            for h in range(NH):
                sk_ = fdm.get(2 * h)
                sv_ = fdm.get(2 * h + 1)
                ringm.wait_ready("pe", sv_)
                ringm.wait_ready("pe", sk_)
                pl.wait("pe", "io", ckv_io)
                pl.wait("pe", "dve", kpe_dve)
                for tk in range(4):
                    pl.bank_wait(4)
                    for k in range(4):
                        pl.run("pe", lambda E, sk_=sk_, k=k, tk=tk: E.matmul(
                            PS[4][:, :], ringm.ap(sk_, k * 128, 128),
                            QP[2 * k + tk // 2][:, (tk % 2) * 512:(tk % 2) * 512 + 512],
                            start=(k == 0), stop=(k == 3)),
                            inc=([("pe", 1)] if k == 3 else None))
                    pl.wait("act", "pe", pl.val("pe"))
                    pl.run("act", lambda E, tk=tk: E.activation(
                        KVH[1][:, tk * 512:(tk + 1) * 512], PS[4][:, :], AF.Copy),
                        inc=[("act", 1)])
                    pl.bank_set(4, "act")
                for jq in range(4):
                    pl.bank_wait(5)
                    for jj4 in range(4):
                        j = jq * 4 + jj4
                        for k in range(4):
                            pl.run("pe", lambda E, sv_=sv_, k=k, j=j, jj4=jj4: E.matmul(
                                PS[5][:, jj4 * 128:(jj4 + 1) * 128],
                                QP[2 * k + j // 8][:, (j % 8) * 128:(j % 8) * 128 + 128],
                                ringm.ap(sv_, k * 128, 128),
                                start=(k == 0), stop=(k == 3)),
                                inc=([("pe", 1)] if (k == 3 and jj4 == 3) else None))
                    pl.wait("act", "pe", pl.val("pe"))
                    pl.run("act", lambda E, jq=jq: E.activation(
                        ROPS[:, jq * 512:(jq + 1) * 512], PS[5][:, :], AF.Copy),
                        inc=[("act", 1)])
                    pl.bank_set(5, "act")
                kv_act = pl.val("act")
                ringm.consumed(sk_)
                ringm.consumed(sv_)
                pl.wait("pe", "act", kv_act)

                def scores_ops(qt, j0, nj, psum, h=h):
                    pl.run("pe", lambda E, qt=qt, j0=j0, nj=nj, psum=psum, h=h: E.matmul(
                        psum[:, 0:nj * 128], QN[h][:, qt * 128:(qt + 1) * 128],
                        KVH[1][:, j0 * 128:(j0 + nj) * 128],
                        start=True, stop=False))
                    hb = 64 * (h % 2)
                    pl.run("pe", lambda E, qt=qt, j0=j0, nj=nj, psum=psum, h=h, hb=hb:
                           E.matmul(
                               psum[:, 0:nj * 128],
                               LNX[8 + h // 2][hb:hb + 64, qt * 128:(qt + 1) * 128],
                               KVH[0][hb:hb + 64, j0 * 128:(j0 + nj) * 128],
                               start=False, stop=True), inc=[("pe", 1)])

                def vblk_ap(j):
                    return ROPS[:, j * 128:(j + 1) * 128]

                def probs_of(qi, j0, nj):
                    t = LNX[2 * qi] if j0 < 8 else LNX[2 * qi + 1]
                    c0 = (j0 % 8) * 128
                    return t[:, c0:c0 + nj * 128]

                if h == 0:
                    dbgdump(KVH[1][:, :])
                    dbgdump(ROPS[:, 0:2048])
                attn_inner(h, scores_ops, vblk_ap, probs_of,
                           dbg_head=(h == 0 and NDBG > 0))
                if h == 0:
                    dbgdump(QN[0][:, :])

            pl.phase("attn_a_wo")
            proj_add(lambda o: wo[o], 16,
                     lambda k, qt: QN[k][:, qt * 512:(qt + 1) * 512],
                     pre_pe=[("act", pl.val("act"))])

        def ffn():
            lnx_dve = layernorm()

            s = ring.load(rw[:, :], cols=128)
            ring.wait_ready("pe", s)
            pl.wait("pe", "dve", lnx_dve)
            LGF = F32S[0:8, 0:1024]
            for qt in range(2):
                pl.bank_wait(4)
                for k in range(HC):
                    pl.run("pe", lambda E, s=s, k=k, qt=qt: E.matmul(
                        PS[4][0:8, :], ring.ap(s, k * 8, 8),
                        LNX[k][:, qt * 512:(qt + 1) * 512],
                        start=(k == 0), stop=(k == HC - 1)),
                        inc=([("pe", 1)] if k == HC - 1 else None))
                pl.wait("act", "pe", pl.val("pe"))
                pl.run("act", lambda E, qt=qt: E.activation(
                    LGF[:, qt * 512:(qt + 1) * 512], PS[4][0:8, :], AF.Copy),
                    inc=[("act", 1)])
                pl.bank_set(4, "act")
            ring.consumed(s)
            lg_act = pl.val("act")
            pl.wait("pe", "act", lg_act)
            pl.bank_wait(5)
            for half in range(2):
                for b4 in range(4):
                    blk = half * 4 + b4
                    pl.run("pe", lambda E, blk=blk, b4=b4: E.transpose(
                        PS[5][:, b4 * 8:b4 * 8 + 8],
                        LGF[:, blk * 128:(blk + 1) * 128], IDF[0:8, 0:8]))
                pl.run("pe", lambda E: E.drain(), inc=[("pe", 1)])
                pl.wait("dve", "pe", pl.val("pe"))
                for b4 in range(4):
                    blk = half * 4 + b4
                    pl.run("dve", lambda E, blk=blk, b4=b4: E.tensor_copy(
                        RTR[:, blk * 8:blk * 8 + 8], PS[5][:, b4 * 8:b4 * 8 + 8]),
                        inc=([("dve", 1)] if b4 == 3 else None))
                pl.wait("pe", "dve", pl.val("dve"))
            pl.bank_set(5, "dve")
            for blk in range(8):
                P8 = RTR[:, blk * 8:blk * 8 + 8]
                CMB = RTR[:, 64 + blk * 8:64 + blk * 8 + 8]
                T8 = RTR[:, 128 + blk * 8:128 + blk * 8 + 8]
                pl.run("dve", lambda E, P8=P8: E.reduce_max(
                    SCOL[:, 0:1], P8, axis=AX.X, negate=True), inc=[("dve", 1)], guard=True)
                pl.wait("act", "dve", pl.val("dve"))
                pl.run("act", lambda E, P8=P8: E.activation(
                    P8, P8, AF.Exp, bias=SCOL[:, 0:1], accum_out=SCOL[:, 1:2]))
                pl.run("act", lambda E: E.drain(), inc=[("act", 1)])
                pl.wait("dve", "act", pl.val("act"))
                pl.run("dve", lambda E: E.reciprocal(SCOL[:, 2:3], SCOL[:, 1:2]), guard=True)
                pl.run("dve", lambda E, P8=P8: E.tensor_scalar_mul(P8, P8, SCOL[:, 2:3]), guard=True)
                pl.run("dve", lambda E, P8=P8: E.reduce_max(SCOL[:, 3:4], P8, axis=AX.X), guard=True)
                pl.run("dve", lambda E, P8=P8, CMB=CMB: E.tensor_scalar(
                    CMB, P8, SCOL[:, 3:4], None, ALU.is_ge), guard=True)
                pl.run("dve", lambda E, CMB=CMB, T8=T8: E.tensor_scalar(
                    T8, CMB, -1.0, 1.0, ALU.mult, ALU.add), guard=True)
                pl.run("dve", lambda E, P8=P8, T8=T8: E.tensor_mul(T8, T8, P8), guard=True)
                pl.run("dve", lambda E, T8=T8: E.reduce_max(SCOL[:, 4:5], T8, axis=AX.X), guard=True)
                pl.run("dve", lambda E, T8=T8: E.tensor_scalar(
                    T8, T8, SCOL[:, 4:5], None, ALU.is_ge), guard=True)
                pl.run("dve", lambda E: E.tensor_add(SCOL[:, 5:6], SCOL[:, 3:4], SCOL[:, 4:5]), guard=True)
                pl.run("dve", lambda E: E.reciprocal(SCOL[:, 5:6], SCOL[:, 5:6]), guard=True)
                pl.run("dve", lambda E: E.tensor_mul(SCOL[:, 6:7], SCOL[:, 3:4], SCOL[:, 5:6]), guard=True)
                pl.run("dve", lambda E: E.tensor_mul(SCOL[:, 7:8], SCOL[:, 4:5], SCOL[:, 5:6]), guard=True)
                pl.run("dve", lambda E, CMB=CMB: E.tensor_scalar_mul(CMB, CMB, SCOL[:, 6:7]), guard=True)
                pl.run("dve", lambda E, CMB=CMB, T8=T8: E.scalar_tensor_tensor(
                    CMB, T8, SCOL[:, 7:8], CMB, ALU.mult, ALU.add), inc=[("dve", 1)])
            cmb_dve = pl.val("dve")

            CBCF = F32S[:, 0:1024]
            CROW = SROWS[0:1, 0:1024]

            def compute_cbc(e):
                pl.wait("pe", "dve", cmb_dve)
                for half in range(2):
                    pl.bank_wait(4)
                    for b4 in range(4):
                        blk = half * 4 + b4
                        pl.run("pe", lambda E, blk=blk, b4=b4, e=e: E.transpose(
                            PS[4][0:1, b4 * 128:(b4 + 1) * 128],
                            RTR[:, 64 + blk * 8 + e:64 + blk * 8 + e + 1], IDF[:, :]))
                    pl.run("pe", lambda E: E.drain(), inc=[("pe", 1)])
                    pl.wait("dve", "pe", pl.val("pe"))
                    pl.run("dve", lambda E, half=half: E.tensor_copy(
                        CROW[:, half * 512:(half + 1) * 512], PS[4][0:1, 0:512]),
                        inc=[("dve", 1)])
                    pl.bank_set(4, "dve")
                pl.wait("pe", "dve", pl.val("dve"))
                for qt in range(2):
                    pl.bank_wait(4)
                    pl.run("pe", lambda E, qt=qt: E.matmul(
                        PS[4][:, :], ONEF[:, :], CROW[:, qt * 512:(qt + 1) * 512],
                        start=True, stop=True), inc=[("pe", 1)])
                    pl.wait("dve", "pe", pl.val("pe"))
                    pl.run("dve", lambda E, qt=qt: E.tensor_copy(
                        CBCF[:, qt * 512:(qt + 1) * 512], PS[4][:, :]), inc=[("dve", 1)])
                    pl.bank_set(4, "dve")
                    pl.wait("pe", "dve", pl.val("dve"))

            def gu_tile(i, ei):
                if ei == 0:
                    return QN[i]
                return QN[11 + i] if i < 5 else QP[i - 5]

            def gate_up(e, ei, with_bias, dst_fn):
                noch = FIC if not with_bias else FSC
                srcs = []
                for oi in range(noch):
                    srcs.append(((gw[e, oi] if not with_bias else sgw[oi]), 2048))
                    srcs.append(((uw[e, oi] if not with_bias else suw[oi]), 2048))
                fd = Feeder(ring, srcs, 2)
                for oi in range(noch):
                    sg_ = fd.get(2 * oi)
                    su_ = fd.get(2 * oi + 1)
                    ring.wait_ready("pe", su_)
                    ring.wait_ready("pe", sg_)
                    pl.wait("pe", "dve", lnx_dve)
                    for qt in range(2):
                        bank = qt
                        bank2 = 2 + qt
                        pl.bank_wait(bank)
                        for k in range(HC):
                            pl.run("pe", lambda E, sg_=sg_, k=k, qt=qt, bank=bank: E.matmul(
                                PS[bank][:, :], ring.ap(sg_, k * 128, 128),
                                LNX[k][:, qt * 512:(qt + 1) * 512],
                                start=(k == 0), stop=(k == HC - 1)),
                                inc=([("pe", 1)] if k == HC - 1 else None))
                        gate_pe = pl.val("pe")
                        pl.bank_wait(bank2)
                        for k in range(HC):
                            pl.run("pe", lambda E, su_=su_, k=k, qt=qt, bank2=bank2: E.matmul(
                                PS[bank2][:, :], ring.ap(su_, k * 128, 128),
                                LNX[k][:, qt * 512:(qt + 1) * 512],
                                start=(k == 0), stop=(k == HC - 1)),
                                inc=([("pe", 1)] if k == HC - 1 else None))
                        up_pe = pl.val("pe")
                        gcol = qt * 512
                        pl.wait("act", "pe", gate_pe)
                        if with_bias:
                            pl.run("act", lambda E, bank=bank, gcol=gcol, oi=oi: E.activation(
                                GSC[:, gcol:gcol + 512], PS[bank][:, :], AF.Silu,
                                bias=bias_col(80 + oi)), inc=[("act", 1)])
                        else:
                            pl.run("act", lambda E, bank=bank, gcol=gcol: E.activation(
                                GSC[:, gcol:gcol + 512], PS[bank][:, :], AF.Silu),
                                inc=[("act", 1)])
                        pl.bank_set(bank, "act")
                        pl.wait("dve", "act", pl.val("act"))
                        pl.wait("dve", "pe", up_pe)
                        dst_fn(oi, qt, gcol, bank2)
                        pl.bank_set(bank2, "dve")
                    ring.consumed(sg_)
                    ring.consumed(su_)

            def expert_pair(p):
                for ei in range(2):
                    e = 2 * p + ei
                    compute_cbc(e)

                    def dst_fn(oi, qt, gcol, bank2, ei=ei):
                        pl.run("dve", lambda E, gcol=gcol, qt=qt, bank2=bank2:
                               E.scalar_tensor_tensor(
                                   GSC[:, 1024 + gcol:1024 + gcol + 512],
                                   PS[bank2][:, :], 1.0,
                                   CBCF[:, qt * 512:(qt + 1) * 512],
                                   ALU.mult, ALU.mult))
                        pl.run("dve", lambda E, oi=oi, qt=qt, gcol=gcol, ei=ei: E.tensor_mul(
                            gu_tile(oi, ei)[:, qt * 512:(qt + 1) * 512],
                            GSC[:, 1024 + gcol:1024 + gcol + 512],
                            GSC[:, gcol:gcol + 512]), inc=[("dve", 1)])
                    gate_up(e, ei, False, dst_fn)
                gu_dve = pl.val("dve")

                srcs = []
                for och in range(16):
                    srcs.append((dwt[2 * p, och], FI))
                    srcs.append((dwt[2 * p + 1, och], FI))
                fd = Feeder(ring, srcs, 2)
                for och in range(16):
                    sd0 = fd.get(2 * och)
                    sd1 = fd.get(2 * och + 1)
                    ring.wait_ready("pe", sd1)
                    ring.wait_ready("pe", sd0)
                    pl.wait("pe", "dve", gu_dve)
                    for qt in range(2):
                        bank = (och % 2) * 2 + qt
                        pl.bank_wait(bank)
                        psum = PS[bank]
                        for ei in range(2):
                            sd = sd0 if ei == 0 else sd1
                            for k in range(FIC):
                                pl.run("pe", lambda E, sd=sd, k=k, qt=qt, psum=psum, ei=ei:
                                       E.matmul(psum[:, :], ring.ap(sd, k * 128, 128),
                                                gu_tile(k, ei)[:, qt * 512:(qt + 1) * 512],
                                                start=(ei == 0 and k == 0),
                                                stop=(ei == 1 and k == FIC - 1)),
                                       inc=([("pe", 1)] if (ei == 1 and k == FIC - 1)
                                            else None))
                        add_to_x(och, qt, psum, pl.val("pe"))
                        pl.bank_set(bank, "dve")
                    ring.consumed(sd0)
                    ring.consumed(sd1)

            for p in range(4):
                pl.phase(f"ffn_pair")
                expert_pair(p)

            pl.phase("ffn_shared")
            def sgu_tile(i):
                return QN[i] if i < 16 else QP[i - 16]

            def sdst_fn(oi, qt, gcol, bank2):
                pl.run("dve", lambda E, oi=oi, qt=qt, gcol=gcol, bank2=bank2:
                       E.scalar_tensor_tensor(
                           sgu_tile(oi)[:, qt * 512:(qt + 1) * 512],
                           PS[bank2][:, :], bias_col(102 + oi),
                           GSC[:, gcol:gcol + 512], ALU.add, ALU.mult),
                       inc=[("dve", 1)])
            gate_up(0, 0, True, sdst_fn)
            sgu_dve = pl.val("dve")
            srcs = []
            for och in range(16):
                srcs.append((sdw[och][:, 0:2048], 2048))
                srcs.append((sdw[och][:, 2048:2816], 768))
            fd = Feeder(ring, srcs, 2)
            for och in range(16):
                sda = fd.get(2 * och)
                sdb = fd.get(2 * och + 1)
                ring.wait_ready("pe", sdb)
                ring.wait_ready("pe", sda)
                pl.wait("pe", "dve", sgu_dve)
                for qt in range(2):
                    bank = (och % 2) * 2 + qt
                    pl.bank_wait(bank)
                    psum = PS[bank]
                    for k in range(FSC):
                        sd = sda if k < 16 else sdb
                        kk = k if k < 16 else k - 16
                        pl.run("pe", lambda E, sd=sd, kk=kk, k=k, qt=qt, psum=psum: E.matmul(
                            psum[:, :], ring.ap(sd, kk * 128, 128),
                            sgu_tile(k)[:, qt * 512:(qt + 1) * 512],
                            start=(k == 0), stop=(k == FSC - 1)),
                            inc=([("pe", 1)] if k == FSC - 1 else None))
                    add_to_x(och, qt, psum, pl.val("pe"), bias_idx=124 + och)
                    pl.bank_set(bank, "dve")
                ring.consumed(sda)
                ring.consumed(sdb)

        def attn_b():
            lnx_dve = layernorm()
            RQC = wtab(0)
            RQS = wtab(1)
            RKC = wtab(2)
            RKS = wtab(3)
            pl.run("pool", lambda E: E.dma_start(out=RQC, in_=rb_q[0]), inc=[("wt", 16)])
            pl.run("pool", lambda E: E.dma_start(out=RQS, in_=rb_q[1]), inc=[("wt", 16)])
            pl.run("pool", lambda E: E.dma_start(out=RKC, in_=rb_k[0]), inc=[("wt", 16)])
            pl.run("pool", lambda E: E.dma_start(out=RKS, in_=rb_k[1]), inc=[("wt", 16)])
            rb_w = pl.val("wt")

            kv_store_io = [0]
            fd = Feeder(ring, [(wqkv[oi], 2048) for oi in range(48)], 2)
            for oi in range(48):
                s = fd.get(oi)
                ring.wait_ready("pe", s)
                pl.wait("pe", "dve", lnx_dve)
                for qt in range(2):
                    bank = qt
                    pl.bank_wait(bank)
                    for k in range(HC):
                        pl.run("pe", lambda E, s=s, k=k, qt=qt, bank=bank: E.matmul(
                            PS[bank][:, :], ring.ap(s, k * 128, 128),
                            LNX[k][:, qt * 512:(qt + 1) * 512],
                            start=(k == 0), stop=(k == HC - 1)),
                            inc=([("pe", 1)] if k == HC - 1 else None))
                    mmv = pl.val("pe")
                    pl.wait("act", "pe", mmv)
                    pl.run("act", lambda E, oi=oi, qt=qt, bank=bank: E.activation(
                        ROPS[:, qt * 512:(qt + 1) * 512], PS[bank][:, :], AF.Identity,
                        bias=bias_col(16 + oi)), inc=[("act", 1)])
                    pl.bank_set(bank, "act")
                pj_act = pl.val("act")
                h = oi % 16
                if oi < 32:
                    is_q = oi < 16
                    dst = QN[h] if is_q else QP[h % 2]
                    src = ROPS[:, 0:1024]
                    rot = ROPS[:, 1024:2048]
                    ctab = RQC if is_q else RKC
                    stab = RQS if is_q else RKS
                    pl.wait("dve", "act", pj_act)
                    pl.wait("dve", "wt", rb_w)
                    if not is_q:
                        pl.wait("dve", "io", kv_store_io[0])
                    pl.run("dve", lambda E: E.tensor_scalar_mul(rot[0:16, :], src[32:48, :], -1.0))
                    pl.run("dve", lambda E: E.tensor_copy(rot[32:48, :], src[0:16, :]))
                    pl.run("dve", lambda E, dst=dst, ctab=ctab: E.tensor_mul(
                        dst[:, :], src[:, :], ctab))
                    pl.run("dve", lambda E, stab=stab: E.tensor_mul(
                        rot[0:48, :], rot[0:48, :], stab[0:48, :]))
                    pl.run("dve", lambda E, dst=dst: E.tensor_add(
                        dst[0:48, :], dst[0:48, :], rot[0:48, :]), inc=[("dve", 1)])
                    if not is_q:
                        pl.wait("sp", "dve", pl.val("dve"))
                        pl.run("sp", lambda E, h=h, dst=dst: E.dma_start(
                            out=ag_bk_in[h * 128:(h + 1) * 128, :], in_=dst[:, :]),
                            inc=[("io", 16)])
                        kv_store_io[0] = pl.val("io")
                else:
                    pl.wait("pe", "act", pj_act)
                    pl.wait("pe", "io", kv_store_io[0])
                    for blk in range(8):
                        pb = 6 + blk % 2
                        pl.bank_wait(pb)
                        pl.run("pe", lambda E, blk=blk: E.transpose(
                            PT[blk % 2][:, 0:128], ROPS[:, blk * 128:(blk + 1) * 128],
                            IDB[:, :]))
                        pl.run("pe", lambda E: E.drain(), inc=[("pe", 1)])
                        pl.wait("dve", "pe", pl.val("pe"))
                        pl.run("dve", lambda E, blk=blk: E.tensor_copy(
                            QP[2 + blk // 4][:, (blk % 4) * 128:(blk % 4) * 128 + 128],
                            PT[blk % 2][:, 0:128]), inc=[("dve", 1)])
                        pl.bank_set(pb, "dve")
                    pl.wait("sp", "dve", pl.val("dve"))
                    for half in range(2):
                        pl.run("sp", lambda E, h=h, half=half: E.dma_start(
                            out=ag_bv_in[h * 1024 + half * 512:h * 1024 + (half + 1) * 512, :]
                            .rearrange("(a p) d -> p a d", p=128),
                            in_=QP[2 + half][:, 0:512]), inc=[("io", 16)])
                    kv_store_io[0] = pl.val("io")
                ring.consumed(s)

            pl.wait("pool", "io", pl.val("io"))
            pl.run("pool", lambda E: E.collective_compute(
                "AllGather", ALU.bypass,
                replica_groups=[[0, 1], [2, 3], [4, 5], [6, 7]],
                ins=[ag_bk_in.ap()], outs=[ag_bk_out.ap()]), inc=[("cc1", 1)])
            pl.run("pool", lambda E: E.collective_compute(
                "AllGather", ALU.bypass,
                replica_groups=[[0, 1], [2, 3], [4, 5], [6, 7]],
                ins=[ag_bv_in.ap()], outs=[ag_bv_out.ap()]), inc=[("cc2", 1)])

            pl.phase("attn_b_heads")
            pl.wait("sp", "cc1", 1)
            pl.wait("sp", "cc2", 1)
            for h in range(NH):
                pl.wait("sp", "pe", pl.val("pe"))
                pl.wait("sp", "act", pl.val("act"))
                for r in range(2):
                    pl.run("sp", lambda E, h=h, r=r: E.dma_start(
                        out=KVH[0][:, r * 1024:(r + 1) * 1024],
                        in_=ag_bk_out[r * 2048 + h * 128:r * 2048 + (h + 1) * 128, :]),
                        inc=[("io", 16)])
                    pl.run("sp", lambda E, h=h, r=r: E.dma_start(
                        out=KVH[1][:, r * 1024:(r + 1) * 1024],
                        in_=ag_bv_out[r * NH * 1024 + h * 1024:
                                      r * NH * 1024 + (h + 1) * 1024, :]
                        .rearrange("(a p) d -> p a d", p=128)), inc=[("io", 16)])
                kv_io = pl.val("io")
                pl.wait("pe", "io", kv_io)

                def scores_ops(qt, j0, nj, psum, h=h):
                    pl.run("pe", lambda E, qt=qt, j0=j0, nj=nj, psum=psum, h=h: E.matmul(
                        psum[:, 0:nj * 128], QN[h][:, qt * 128:(qt + 1) * 128],
                        KVH[0][:, j0 * 128:(j0 + nj) * 128],
                        start=True, stop=True), inc=[("pe", 1)])

                def vblk_ap(j):
                    return KVH[1][:, j * 128:(j + 1) * 128]

                def probs_of(qi, j0, nj):
                    t = LNX[2 * qi] if j0 < 8 else LNX[2 * qi + 1]
                    c0 = (j0 % 8) * 128
                    return t[:, c0:c0 + nj * 128]

                attn_inner(h, scores_ops, vblk_ap, probs_of)

            proj_add(lambda o: wdense[o], 16,
                     lambda k, qt: QN[k][:, qt * 512:(qt + 1) * 512],
                     bias_base=64, pre_pe=[("act", pl.val("act"))])

        def tap_store(i):
            if not TAPS:
                return
            pl.wait("pool", "dve", pl.val("dve"))
            pl.wait("pool", "act", pl.val("act"))
            pl.wait("pool", "pe", pl.val("pe"))
            for c in range(HC):
                pl.run("pool", lambda E, c=c, i=i: E.dma_start(
                    out=taps[i][c, :, :], in_=X[c][:, :]), inc=[("out", 16)])

        if N_SUB >= 1:
            pl.phase("attn_a")
            attn_a()
            tap_store(0)
        if N_SUB >= 2:
            pl.phase("ffn1")
            ffn()
            tap_store(1)
        if N_SUB >= 3:
            pl.phase("attn_b")
            attn_b()
            tap_store(2)
        if N_SUB >= 4:
            pl.phase("ffn2")
            ffn()
            tap_store(3)
        pl.phase("fin")

        pl.wait("pool", "dve", pl.val("dve"))
        pl.wait("pool", "act", pl.val("act"))
        pl.wait("pool", "pe", pl.val("pe"))
        for c in range(HC):
            pl.run("pool", lambda E, c=c: E.dma_start(out=yout[c, :, :], in_=X[c][:, :]),
                   inc=[("out", 16)])
        pl.wait("pool", "out", pl.val("out"))
        pl.wait("sp", "out", pl.val("out"))

        blk = E(nc.Block())

        def walk(Eng, lst):
            import contextlib as _ctx
            cur = None
            ctx = _ctx.nullcontext()
            ctx.__enter__()
            for f in lst:
                if isinstance(f, str):
                    ctx.__exit__(None, None, None)
                    ctx = nc.named_scope(f)
                    ctx.__enter__()
                else:
                    f(Eng)
            ctx.__exit__(None, None, None)

        @blk.tensor
        def _(Eng):
            walk(Eng, pl.q["pe"])

        @blk.vector
        def _(Eng):
            walk(Eng, pl.q["dve"])

        @blk.scalar
        def _(Eng):
            walk(Eng, pl.q["act"])

        @blk.sync
        def _(Eng):
            walk(Eng, pl.q["sp"])

        @blk.gpsimd
        def _(Eng):
            walk(Eng, pl.q["pool"])

    return nc


def kernel(**inputs):
    nc = build_nc()
    in_maps = [build_host_inputs(c, inputs) for c in range(NCORES)]
    res = run_bass_kernel_spmd(nc, in_maps, core_ids=list(range(NCORES)))
    out = np.zeros((B, S, H), dtype=np.float32)
    for c in range(NCORES):
        par, bidx = c % 2, c // 2
        y = np.asarray(res.results[c]["yout"])
        xt = y.reshape(H, TC).T.reshape(NBLK, 128, H)
        for i in range(NBLK):
            p = 2 * i + par
            out[bidx, p * 128:(p + 1) * 128, :] = xt[i]
    return out


# version 04 (final)
# speedup vs baseline: 1.1383x; 1.0789x over previous
import os
import contextlib
import numpy as np
import concourse.bass as bass
import concourse.mybir as mybir
from concourse.bass_utils import run_bass_kernel_spmd

F32 = mybir.dt.float32
BF16 = mybir.dt.bfloat16
AF = mybir.ActivationFunctionType
ALU = mybir.AluOpType
AX = mybir.AxisListType

NCORES = 8
B, S, H = 4, 2048, 2048
HC = H // 128
TC = 1024
NBLK = 8
NH = 16
QLORA = 1536
KVL = 512
DR = 64
DN = 128
SA = float((DN + DR) ** (-0.5))
NE = 8
FIC = 11
FI = 1408
FSC = 22
FSI = 2816
ROT = 32
SB_SC = float(128 ** (-0.5))

N_SUB = int(os.environ.get("BASS_MK_SUBLAYERS", "4"))
TAPS = os.environ.get("BASS_MK_TAPS", "0") == "1"
NDBG = int(os.environ.get("BASS_MK_DBG", "0"))
SYNCDBG = os.environ.get("BASS_MK_SYNC", "0") == "1"

POS_OF_BUF = [2 * j for j in range(8)] + [2 * j + 1 for j in range(8)]



def _tile_weight(w, och_i=128):
    K, O = w.shape
    kc = K // 128
    oc = O // och_i
    b = w.reshape(kc, 128, oc, och_i).transpose(2, 1, 0, 3).reshape(oc, 128, kc * och_i)
    return np.ascontiguousarray(b.astype(np.float32))


def _perm_head_b():
    x1 = list(range(0, 16))
    x2 = list(range(16, 32))
    rest = list(range(32, 128))
    return np.array(x1 + rest[:16] + x2 + rest[16:32] + rest[32:], dtype=np.int64)


def _cos_sin(positions, dim):
    inv = 1.0 / (10000.0 ** (np.arange(0, dim, 2, dtype=np.float64) / dim))
    fr = np.outer(positions.astype(np.float64), inv)
    emb = np.concatenate([fr, fr], axis=-1)
    return np.cos(emb), np.sin(emb)


def build_host_inputs(core, ins):
    par = core % 2
    bidx = core // 2
    own_pos = np.array([2 * i + par for i in range(NBLK)])
    d = {}

    xc = np.asarray(ins["L0_attn_a_x"], dtype=np.float32)[bidx]
    xc = xc.reshape(16, 128, H)[own_pos]
    d["xin"] = np.ascontiguousarray(xc.reshape(TC, H).T.reshape(HC, 128, TC))

    d["wqa"] = _tile_weight(np.asarray(ins["L0_attn_a_wq_a"]))
    wqb = np.asarray(ins["L0_attn_a_wq_b"])
    cols = []
    for h in range(NH):
        cols.extend(range(h * 192, h * 192 + DN))
    for g in range(NH // 2):
        cols.extend(range((2 * g) * 192 + DN, (2 * g) * 192 + 192))
        cols.extend(range((2 * g + 1) * 192 + DN, (2 * g + 1) * 192 + 192))
    d["wqb"] = _tile_weight(wqb[:, np.array(cols)])
    wkva = np.asarray(ins["L0_attn_a_wkv_a"])
    d["wkvac"] = _tile_weight(wkva[:, :KVL])
    d["wkvap"] = _tile_weight(wkva[:, KVL:], och_i=64)[0]
    wkvb = np.asarray(ins["L0_attn_a_wkv_b"])
    cols = []
    for h in range(NH):
        cols.extend(range(h * 256, h * 256 + DN))
    for h in range(NH):
        cols.extend(range(h * 256 + DN, h * 256 + 256))
    d["wkvb"] = _tile_weight(wkvb[:, np.array(cols)])
    d["wo"] = _tile_weight(np.asarray(ins["L0_attn_a_wo"]))

    d["rw"] = _tile_weight(np.ascontiguousarray(np.asarray(ins["L0_ffn_router_w"]).T),
                           och_i=8)[0]
    d["gw"] = np.stack([_tile_weight(np.asarray(ins["L0_ffn_gate_w"])[e]) for e in range(NE)])
    d["uw"] = np.stack([_tile_weight(np.asarray(ins["L0_ffn_up_w"])[e]) for e in range(NE)])
    d["dw"] = np.stack([_tile_weight(np.asarray(ins["L0_ffn_down_w"])[e]) for e in range(NE)])
    d["sgw"] = _tile_weight(np.asarray(ins["L0_ffn_s_gate_w"]))
    d["suw"] = _tile_weight(np.asarray(ins["L0_ffn_s_up_w"]))
    d["sdw"] = _tile_weight(np.asarray(ins["L0_ffn_s_down_w"]))

    wqkv = np.asarray(ins["L0_attn_b_W_qkv"])
    bqkv = np.asarray(ins["L0_attn_b_b_qkv"])
    perm = _perm_head_b()
    cols = []
    for h in range(NH):
        cols.extend((h * 384 + perm).tolist())
    for h in range(NH):
        cols.extend((h * 384 + 128 + perm).tolist())
    for h in range(NH):
        cols.extend(range(h * 384 + 256, h * 384 + 256 + 128))
    cols = np.array(cols)
    d["wqkv"] = _tile_weight(wqkv[:, cols])
    bq = bqkv[cols].astype(np.float32)
    d["wdense"] = _tile_weight(np.asarray(ins["L0_attn_b_W_dense"]))

    vec_list = []
    def addvec(v):
        v = np.asarray(v, dtype=np.float32).reshape(-1)
        vec_list.append(v.reshape(-1, 128).T)
    addvec(ins["L0_attn_a_q_a_ln_w"])
    addvec(ins["L0_attn_a_kv_a_ln_w"])
    addvec(bq)
    addvec(ins["L0_attn_b_b_dense"])
    addvec(ins["L0_ffn_s_gate_b"])
    addvec(ins["L0_ffn_s_up_b"])
    addvec(ins["L0_ffn_s_down_b"])
    d["vecs"] = np.ascontiguousarray(np.concatenate(vec_list, axis=1))

    d["identf"] = np.eye(128, dtype=np.float32)
    d["ones1"] = np.ones((1, 128), dtype=np.float32)
    d["onesc"] = np.ones((128, 1), dtype=np.float32)

    tri = np.where(np.arange(128)[None, :] <= np.arange(128)[:, None], 0.0, -1e30
                   ).astype(np.float32)
    zer = np.zeros((128, 128), np.float32)
    neg = np.full((128, 128), -1e30, np.float32)
    am = np.zeros((8, 2, 128, 128), np.float32)
    for qt in range(8):
        am[qt, 0] = tri if par == 0 else zer
        am[qt, 1] = neg if par == 0 else tri
    d["amask"] = np.ascontiguousarray(am.transpose(2, 0, 1, 3).reshape(128, 2048))

    own_tok = (own_pos[:, None] * 128 + np.arange(128)[None, :]).reshape(-1)
    buf_tok = (np.array(POS_OF_BUF)[:, None] * 128 + np.arange(128)[None, :]).reshape(-1)

    ca, sa_ = _cos_sin(own_tok, DR)
    t = np.zeros((2, 128, TC), np.float32)
    t[0, 0:64] = ca.T * SA
    t[0, 64:128] = ca.T * SA
    t[1, 0:64] = sa_.T * SA
    t[1, 64:128] = sa_.T * SA
    d["ra_q"] = t
    ck, sk = _cos_sin(buf_tok, DR)
    t = np.zeros((4, 128, 1024), np.float32)
    t[0, 0:64] = ck.T[:, 0:1024]
    t[1, 0:64] = ck.T[:, 1024:2048]
    t[2, 0:64] = sk.T[:, 0:1024]
    t[3, 0:64] = sk.T[:, 1024:2048]
    d["ra_k"] = t
    cb, sb = _cos_sin(own_tok, ROT)
    def tb(scale):
        cos_t = np.full((128, TC), scale, np.float32)
        sin_t = np.zeros((128, TC), np.float32)
        cos_t[0:16] = cb[:, 0:16].T * scale
        cos_t[32:48] = cb[:, 16:32].T * scale
        sin_t[0:16] = sb[:, 0:16].T * scale
        sin_t[32:48] = sb[:, 16:32].T * scale
        return np.stack([cos_t, sin_t])
    d["rb_q"] = tb(SB_SC)
    d["rb_k"] = tb(1.0)
    return d



ENGINES = ("pe", "dve", "act", "sp", "pool")


class Plan:
    def __init__(self, sems):
        self.sems = sems
        self.q = {e: [] for e in ENGINES}
        self.c = {n: 0 for n in sems}
        self._lastwait = {e: {} for e in ENGINES}
        self.bank = {}

    def wait(self, eng, sem, val=None):
        v = self.c[sem] if val is None else val
        if v <= 0:
            return
        if self._lastwait[eng].get(sem, -1) >= v:
            return
        self._lastwait[eng][sem] = v
        h = self.sems[sem]
        self.q[eng].append(lambda E, h=h, v=v: E.wait_ge(h, v))

    def run(self, eng, fn, inc=None, guard=False):
        incs = []
        if inc:
            for s_, d_ in inc:
                self.c[s_] += d_
                incs.append((self.sems[s_], d_))
        need_drain = (bool(incs) and eng in ("dve", "act")) or guard

        def cl(E, fn=fn, incs=incs, need_drain=need_drain):
            inst = fn(E)
            if need_drain:
                inst = E.drain(fusable=False)
            for h, d_ in incs:
                inst.then_inc(h, d_)
        self.q[eng].append(cl)

    def val(self, sem):
        return self.c[sem]

    def phase(self, name):
        for e in ENGINES:
            self.q[e].append(name)

    def barrier(self):
        for eng in ENGINES:
            for sem in ("pe", "dve", "act", "w", "io"):
                self.wait(eng, sem, self.c[sem])

    def bank_wait(self, bank):
        g = self.bank.get(bank)
        if g:
            self.wait("pe", g[0], g[1])

    def bank_set(self, bank, eng):
        self.bank[bank] = (eng, self.c[eng])


class Ring:
    def __init__(self, pl, arena, col0, cols_per_slot, nslots, sempfx):
        self.pl = pl
        self.arena = arena
        self.base = [col0 + i * cols_per_slot for i in range(nslots)]
        self.maxcols = cols_per_slot
        self.free_at = [0] * nslots
        self.semname = [f"{sempfx}{i}" for i in range(nslots)]
        self.nloads = [0] * nslots
        self.i = 0

    def load(self, src_ap, cols):
        assert cols <= self.maxcols
        pl = self.pl
        s = self.i % len(self.base)
        self.i += 1
        pl.wait("pool", "pe", self.free_at[s])
        ap = self.arena[:, self.base[s]:self.base[s] + cols]
        pl.run("pool", lambda E, ap=ap, src=src_ap: E.dma_start(out=ap, in_=src),
               inc=[(self.semname[s], 16)])
        self.nloads[s] += 1
        return s

    def wait_ready(self, eng, s):
        self.pl.wait(eng, self.semname[s], self.nloads[s] * 16)

    def ap(self, s, c0, cols):
        return self.arena[:, self.base[s] + c0:self.base[s] + c0 + cols]

    def consumed(self, s):
        self.free_at[s] = self.pl.val("pe")


class Feeder:
    def __init__(self, rng, srcs, ahead):
        self.rng = rng
        self.srcs = srcs
        self.ahead = ahead
        self.slots = []
        self.next = 0
        self._fill(ahead)

    def _fill(self, upto_extra):
        while self.next < len(self.srcs) and self.next < upto_extra:
            self.slots.append(self.rng.load(*self.srcs[self.next]))
            self.next += 1

    def get(self, i):
        while self.next < len(self.srcs) and self.next <= i + self.ahead:
            self.slots.append(self.rng.load(*self.srcs[self.next]))
            self.next += 1
        return self.slots[i]



def build_nc():
    nc = bass.Bass("TRN2", target_bir_lowering=False)

    DI = lambda name, shape: nc.dram_tensor(name, shape, F32, kind="ExternalInput")
    xin = DI("xin", [HC, 128, TC])
    wqa = DI("wqa", [12, 128, 2048])
    wqb = DI("wqb", [24, 128, 1536])
    wkvac = DI("wkvac", [4, 128, 2048])
    wkvap = DI("wkvap", [128, 1024])
    wkvb = DI("wkvb", [32, 128, 512])
    wo = DI("wo", [16, 128, 2048])
    rw = DI("rw", [128, 128])
    gw = DI("gw", [NE, FIC, 128, 2048])
    uw = DI("uw", [NE, FIC, 128, 2048])
    dwt = DI("dw", [NE, 16, 128, FI])
    sgw = DI("sgw", [FSC, 128, 2048])
    suw = DI("suw", [FSC, 128, 2048])
    sdw = DI("sdw", [16, 128, FSI])
    wqkv = DI("wqkv", [48, 128, 2048])
    wdense = DI("wdense", [16, 128, 2048])
    vecs = DI("vecs", [128, 140])
    identf = DI("identf", [128, 128])
    ones1 = DI("ones1", [1, 128])
    onesc = DI("onesc", [128, 1])
    amask = DI("amask", [128, 2048])
    ra_q = DI("ra_q", [2, 128, TC])
    ra_k = DI("ra_k", [4, 128, 1024])
    rb_q = DI("rb_q", [2, 128, TC])
    rb_k = DI("rb_k", [2, 128, TC])

    yout = nc.dram_tensor("yout", [HC, 128, TC], F32, kind="ExternalOutput")
    dbg = (nc.dram_tensor("dbg", [NDBG, 128, 2048], F32, kind="ExternalOutput")
           if NDBG else None)
    taps = [nc.dram_tensor(f"tap{i+1}", [HC, 128, TC], F32, kind="ExternalOutput")
            for i in range(4)] if TAPS else []

    ag_a_in = nc.dram_tensor("ag_a_in", [576, TC], BF16)
    ag_a_out = nc.dram_tensor("ag_a_out", [1152, TC], BF16)
    ag_bk_in = nc.dram_tensor("ag_bk_in", [2048, TC], BF16)
    ag_bk_out = nc.dram_tensor("ag_bk_out", [4096, TC], BF16)
    ag_bv_in = nc.dram_tensor("ag_bv_in", [NH * TC, 128], BF16)
    ag_bv_out = nc.dram_tensor("ag_bv_out", [2 * NH * TC, 128], BF16)

    with contextlib.ExitStack() as stack:
        E = stack.enter_context

        X = [E(nc.sbuf_tensor(f"x{c}", [128, TC], BF16)) for c in range(HC)]
        LNX = [E(nc.sbuf_tensor(f"lnx{c}", [128, TC], BF16)) for c in range(HC)]
        QN = [E(nc.sbuf_tensor(f"qn{h}", [128, TC], BF16)) for h in range(NH)]
        QP = [E(nc.sbuf_tensor(f"qp{g}", [128, TC], BF16)) for g in range(8)]
        KVH = [E(nc.sbuf_tensor(f"kvh{j}", [128, S], BF16)) for j in range(2)]
        WRB = E(nc.sbuf_tensor("wrb", [128, 16384], BF16))
        ROPS = E(nc.sbuf_tensor("rops", [128, 3072], BF16))
        GSC = E(nc.sbuf_tensor("gsc", [128, 2048], BF16))
        DSC = E(nc.sbuf_tensor("dsc", [128, 1024], BF16))
        F32S = E(nc.sbuf_tensor("f32s", [128, 1024], F32))
        SROWS = E(nc.sbuf_tensor("srows", [1, 2048], F32))
        SCOL = E(nc.sbuf_tensor("scol", [128, 32], F32))
        RTR = E(nc.sbuf_tensor("rtr", [128, 192], F32))
        AM = E(nc.sbuf_tensor("am", [128, 2048], F32))
        IDF = E(nc.sbuf_tensor("idf", [128, 128], F32))
        IDB = E(nc.sbuf_tensor("idb", [128, 128], BF16))
        ONEF = E(nc.sbuf_tensor("onef", [1, 128], F32))
        ONEB = E(nc.sbuf_tensor("oneb", [128, 1], BF16))
        VEC = E(nc.sbuf_tensor("vec", [128, 140], F32))

        PS = [E(nc.psum_tensor(f"ps{i}", [128, 512], F32)) for i in range(6)]
        PT = [E(nc.psum_tensor(f"pt{i}", [128, 1024], BF16)) for i in range(2)]

        sems = {}
        for name in ("w", "wi", "wt", "io", "pe", "dve", "act", "out",
                     "cc0", "cc1", "cc2",
                     "rg0", "rg1", "rg2", "rg3", "rm0", "rm1", "rm2", "rm3"):
            sems[name] = E(nc.semaphore(name))
        pl = Plan(sems)

        RING0 = 0
        RINGM = 4 * 2048
        WTAB = RINGM + 4 * 512
        PTS0 = WTAB + 4 * 1024
        ring = Ring(pl, WRB, RING0, 2048, 4, "rg")
        ringm = Ring(pl, WRB, RINGM, 512, 4, "rm")

        def wtab(i):
            return WRB[:, WTAB + i * 1024:WTAB + (i + 1) * 1024]

        def pts_ap(i, cols=512):
            return WRB[:, PTS0 + i * 512:PTS0 + i * 512 + cols]

        def bias_col(i):
            return VEC[:, i:i + 1]

        pl.run("sp", lambda E: E.dma_start(out=IDF[:, :], in_=identf[:, :]), inc=[("io", 16)])
        pl.run("pool", lambda E: E.dma_start(out=IDB[:, :], in_=identf[:, :]), inc=[("wi", 16)])
        pl.run("sp", lambda E: E.dma_start(out=ONEF[:, :], in_=ones1[:, :]), inc=[("io", 16)])
        pl.run("pool", lambda E: E.dma_start(out=ONEB[:, :], in_=onesc[:, :]), inc=[("wi", 16)])
        pl.run("sp", lambda E: E.dma_start(out=VEC[:, :], in_=vecs[:, :]), inc=[("io", 16)])
        pl.run("sp", lambda E: E.dma_start(
            out=AM[:, :], in_=amask[:, :]), inc=[("io", 16)])
        for c in range(HC):
            pl.run("pool", lambda E, c=c: E.dma_start(out=X[c][:, :], in_=xin[c, :, :]),
                   inc=[("wi", 16)])
        init_io = pl.val("io")
        init_w = pl.val("wi")

        def am_ap(qt, side):
            c0 = (qt * 2 + side) * 128
            return AM[:, c0:c0 + 128]

        dbg_n = [0]

        def dbgdump(ap):
            if not NDBG or dbg_n[0] >= NDBG:
                return
            i = dbg_n[0]
            dbg_n[0] += 1
            for eng_s in ("dve", "act", "pe", "io"):
                pl.wait("pool", eng_s, pl.val(eng_s))
            pl.run("pool", lambda E, ap=ap, i=i: E.dma_start(
                out=dbg[i, 0:ap.shape[0], 0:ap.shape[1]], in_=ap), inc=[("out", 16)])
            for eng_s in ("dve", "act", "pe", "sp", "pool"):
                pl.wait(eng_s, "out", pl.val("out"))

        def layernorm():
            pl.wait("act", "wi", init_w)
            pl.wait("pe", "wi", init_w)
            pl.wait("pe", "io", init_io)
            x2 = [ROPS[:, 0:1024], ROPS[:, 1024:2048]]
            for bk in range(4):
                pl.bank_wait(bk)
            chunk_pe = []
            for c in range(HC):
                if c >= 2:
                    pl.wait("act", "pe", chunk_pe[c - 2])
                pl.run("act", lambda E, c=c: E.activation(x2[c % 2], X[c][:, :], AF.Square),
                       inc=[("act", 1)])
                pl.wait("pe", "act", pl.val("act"))
                for qt in range(2):
                    pl.run("pe", lambda E, c=c, qt=qt: E.matmul(
                        PS[qt][0:1, :], ONEB[:, :], X[c][:, qt * 512:(qt + 1) * 512],
                        start=(c == 0), stop=(c == HC - 1)))
                for qt in range(2):
                    pl.run("pe", lambda E, c=c, qt=qt, t=x2[c % 2]: E.matmul(
                        PS[2 + qt][0:1, :], ONEB[:, :], t[:, qt * 512:(qt + 1) * 512],
                        start=(c == 0), stop=(c == HC - 1)),
                        inc=([("pe", 1)] if qt == 1 else None))
                chunk_pe.append(pl.val("pe"))
            stats_pe = pl.val("pe")
            MR = SROWS[0:1, 0:1024]
            R2 = SROWS[0:1, 1024:2048]
            pl.wait("dve", "pe", stats_pe)
            for qt in range(2):
                pl.run("dve", lambda E, qt=qt: E.tensor_scalar_mul(
                    MR[:, qt * 512:(qt + 1) * 512], PS[qt][0:1, :], 1.0 / H))
            for qt in range(2):
                pl.run("dve", lambda E, qt=qt: E.tensor_mul(
                    R2[:, qt * 512:(qt + 1) * 512], MR[:, qt * 512:(qt + 1) * 512],
                    MR[:, qt * 512:(qt + 1) * 512]))
            for qt in range(2):
                pl.run("dve", lambda E, qt=qt: E.scalar_tensor_tensor(
                    R2[:, qt * 512:(qt + 1) * 512], PS[2 + qt][0:1, :], 1.0 / H,
                    R2[:, qt * 512:(qt + 1) * 512], ALU.mult, ALU.subtract),
                    inc=[("dve", 1)])
            for bk in range(4):
                pl.bank_set(bk, "dve")
            pl.run("dve", lambda E: E.tensor_scalar_add(R2[:, :], R2[:, :], 1e-5),
                   inc=[("dve", 1)])
            pl.wait("act", "dve", pl.val("dve"))
            pl.run("act", lambda E: E.activation(R2[:, :], R2[:, :], AF.Sqrt),
                   inc=[("act", 1)])
            pl.wait("dve", "act", pl.val("act"))
            pl.run("dve", lambda E: E.reciprocal(R2[:, :], R2[:, :]))
            pl.run("dve", lambda E: E.tensor_mul(MR[:, :], MR[:, :], R2[:, :]))
            pl.run("dve", lambda E: E.tensor_scalar_mul(MR[:, :], MR[:, :], -1.0),
                   inc=[("dve", 1)])
            rows = pl.val("dve")
            RBCb = ROPS[:, 0:1024]
            B2Cb = ROPS[:, 1024:2048]
            pl.wait("pe", "dve", rows)
            pl.bank_wait(4)
            pl.bank_wait(5)
            for qt in range(2):
                pl.run("pe", lambda E, qt=qt: E.matmul(
                    PS[4][:, :], ONEF[:, :], R2[:, qt * 512:(qt + 1) * 512],
                    start=True, stop=True))
                pl.run("pe", lambda E, qt=qt: E.matmul(
                    PS[5][:, :], ONEF[:, :], MR[:, qt * 512:(qt + 1) * 512],
                    start=True, stop=True), inc=[("pe", 1)])
                pl.wait("dve", "pe", pl.val("pe"))
                pl.run("dve", lambda E, qt=qt: E.tensor_copy(
                    RBCb[:, qt * 512:(qt + 1) * 512], PS[4][:, :]))
                pl.run("dve", lambda E, qt=qt: E.tensor_copy(
                    B2Cb[:, qt * 512:(qt + 1) * 512], PS[5][:, :]), inc=[("dve", 1)])
                pl.wait("pe", "dve", pl.val("dve"))
            pl.bank_set(4, "dve")
            pl.bank_set(5, "dve")
            for c in range(HC):
                pl.run("dve", lambda E, c=c: E.tensor_mul(LNX[c][:, :], X[c][:, :], RBCb))
                pl.run("dve", lambda E, c=c: E.tensor_add(LNX[c][:, :], LNX[c][:, :], B2Cb),
                       inc=[("dve", 1)])
            return pl.val("dve")

        dsc_hist = []

        def add_to_x(och, qt, psum, mmv, bias_idx=None):
            dcol = (len(dsc_hist) % 2) * 512
            pl.wait("act", "pe", mmv)
            if len(dsc_hist) >= 2:
                pl.wait("act", "dve", dsc_hist[-2])
            if bias_idx is None:
                pl.run("act", lambda E, psum=psum, dcol=dcol: E.activation(
                    DSC[:, dcol:dcol + 512], psum[:, :], AF.Copy), inc=[("act", 1)])
            else:
                pl.run("act", lambda E, psum=psum, dcol=dcol, b=bias_idx: E.activation(
                    DSC[:, dcol:dcol + 512], psum[:, :], AF.Identity, bias=bias_col(b)),
                    inc=[("act", 1)])
            pl.wait("dve", "act", pl.val("act"))
            pl.run("dve", lambda E, och=och, qt=qt, dcol=dcol: E.tensor_add(
                X[och][:, qt * 512:(qt + 1) * 512], X[och][:, qt * 512:(qt + 1) * 512],
                DSC[:, dcol:dcol + 512]), inc=[("dve", 1)])
            dsc_hist.append(pl.val("dve"))
            return "dve"

        def proj_add(blob_fn, kc, src_fn, bias_base=None, pre_pe=None):
            fd = Feeder(ring, [(blob_fn(o), kc * 128) for o in range(16)], 2)
            for och in range(16):
                s = fd.get(och)
                ring.wait_ready("pe", s)
                if pre_pe:
                    for se, v in pre_pe:
                        pl.wait("pe", se, v)
                for qt in range(2):
                    bank = (och % 2) * 2 + qt
                    pl.bank_wait(bank)
                    psum = PS[bank]
                    for k in range(kc):
                        pl.run("pe", lambda E, s=s, k=k, qt=qt, psum=psum: E.matmul(
                            psum[:, :], ring.ap(s, k * 128, 128),
                            src_fn(k, qt), start=(k == 0), stop=(k == kc - 1)),
                            inc=([("pe", 1)] if k == kc - 1 else None))
                    add_to_x(och, qt, psum, pl.val("pe"),
                             bias_idx=(None if bias_base is None else bias_base + och))
                    pl.bank_set(bank, "dve")
                ring.consumed(s)

        def attn_inner(h, scores_ops, vblk_ap, probs_of, dbg_head=False):
            for g in range(2):
                njmax = 4 * g + 4
                for qi in range(4):
                    qt = g * 4 + qi
                    nt_need = 2 * ((qt + 1 + 3) // 4)
                    base = 4 if (qi % 2 == 1 and nt_need <= 2) else 0
                    tiles = []
                    bank = base
                    for (j0, side) in ((0, 0), (8, 1)):
                        off = 0
                        nj = qt + 1
                        while off < nj:
                            n = min(4, nj - off)
                            tiles.append((bank, j0 + off, n,
                                          side if off + n == nj else None))
                            bank += 1
                            off += n
                    if SYNCDBG:
                        pl.barrier()
                    for (bk, j0, n, side) in tiles:
                        pl.bank_wait(bk)
                        scores_ops(qt, j0, n, PS[bk])
                    mmv = pl.val("pe")
                    pl.wait("dve", "pe", mmv)
                    for (bk, j0, n, side) in tiles:
                        if side is not None:
                            c0 = (n - 1) * 128
                            pl.run("dve", lambda E, bk=bk, c0=c0, qt=qt, side=side:
                                   E.tensor_add(PS[bk][:, c0:c0 + 128],
                                                PS[bk][:, c0:c0 + 128],
                                                am_ap(qt, side)))
                    nt = len(tiles)
                    for i, (bk, j0, n, side) in enumerate(tiles):
                        last = (i == nt - 1) and nt == 1
                        pl.run("dve", lambda E, bk=bk, n=n, i=i: E.reduce_max(
                            SCOL[:, i:i + 1], PS[bk][:, 0:n * 128], axis=AX.X,
                            negate=True), inc=([("dve", 1)] if last else None),
                            guard=(i == nt - 1))
                    for i in range(1, nt):
                        pl.run("dve", lambda E, i=i: E.tensor_tensor(
                            SCOL[:, 0:1], SCOL[:, 0:1], SCOL[:, i:i + 1], ALU.min),
                            inc=([("dve", 1)] if i == nt - 1 else None), guard=True)
                    dv = pl.val("dve")
                    if SYNCDBG:
                        pl.barrier()
                    pl.wait("act", "dve", dv)
                    for i, (bk, j0, n, side) in enumerate(tiles):
                        pl.run("act", lambda E, bk=bk, j0=j0, n=n, i=i, qi=qi, qt=qt:
                               E.activation(probs_of(qi, j0, n),
                                            PS[bk][:, 0:n * 128], AF.Exp,
                                            bias=SCOL[:, 0:1],
                                            accum_out=SCOL[:, 8 + i:9 + i]))
                    pl.run("act", lambda E: E.drain(), inc=[("act", 1)])
                    for (bk, j0, n, side) in tiles:
                        pl.bank_set(bk, "act")
                    av = pl.val("act")
                    pl.wait("dve", "act", av)
                    for i in range(1, nt):
                        pl.run("dve", lambda E, i=i: E.tensor_add(
                            SCOL[:, 8:9], SCOL[:, 8:9], SCOL[:, 8 + i:9 + i]),
                            guard=True)
                    pl.run("dve", lambda E, qi=qi: E.reciprocal(
                        SCOL[:, 16 + qi:17 + qi], SCOL[:, 8:9]), inc=[("dve", 1)])
                    if dbg_head and g == 0 and qi == 1:
                        dbgdump(SCOL[:, 0:32])
                if dbg_head and g == 0:
                    dbgdump(LNX[0][:, :])
                    dbgdump(LNX[1][:, :])
                    dbgdump(LNX[2][:, :])
                    dbgdump(LNX[3][:, :])
                av_started = [False] * 4
                for qi in range(4):
                    pl.bank_wait(qi)
                pl.wait("pe", "act", pl.val("act"))

                def jmap(jj):
                    return (jj // 2) if jj % 2 == 0 else 8 + (jj - 1) // 2

                def emit_tr(jj):
                    j = jmap(jj)
                    jq = j % 8
                    trs_ = [qi for qi in range(4) if jq <= g * 4 + qi]
                    pl.bank_wait(6 + jj % 2)
                    for qi in trs_:
                        pl.run("pe", lambda E, qi=qi, j=j, jj=jj: E.transpose(
                            PT[jj % 2][:, qi * 128:(qi + 1) * 128],
                            probs_of(qi, j, 1), IDB[:, :]))
                    pl.run("pe", lambda E: E.drain(), inc=[("pe", 1)])
                    trv = pl.val("pe")
                    pl.wait("act", "pe", trv)
                    pl.run("act", lambda E, jj=jj: E.activation(
                        pts_ap(jj % 2), PT[jj % 2][:, 0:512], AF.Copy),
                        inc=[("act", 1)])
                    pl.bank_set(6 + jj % 2, "act")
                    return pl.val("act")

                evs = {0: emit_tr(0)}
                for jj in range(2 * njmax):
                    j = jmap(jj)
                    jq = j % 8
                    trs = [qi for qi in range(4) if jq <= g * 4 + qi]
                    if jj + 1 < 2 * njmax:
                        evs[jj + 1] = emit_tr(jj + 1)
                    if SYNCDBG:
                        pl.barrier()
                    pl.wait("pe", "act", evs[jj])
                    for qi in trs:
                        qt = g * 4 + qi
                        st_ = not av_started[qi]
                        sp_ = (j == 8 + qt)
                        pl.run("pe", lambda E, qi=qi, j=j, jj=jj, st_=st_, sp_=sp_:
                               E.matmul(
                                   PS[qi][:, 0:128],
                                   pts_ap(jj % 2)[:, qi * 128:qi * 128 + 128],
                                   vblk_ap(j),
                                   start=st_, stop=sp_),
                               inc=([("pe", 1)] if sp_ else None))
                        av_started[qi] = True
                avv = pl.val("pe")
                if SYNCDBG:
                    pl.barrier()
                pl.wait("dve", "pe", avv)
                if dbg_head and g == 0:
                    for qi2 in range(4):
                        pl.run("dve", lambda E, qi2=qi2: E.tensor_copy(
                            ROPS[:, 2048 + qi2 * 128:2048 + qi2 * 128 + 128]
                            if False else GSC[:, 1024 + qi2 * 128:1024 + qi2 * 128 + 128],
                            PS[qi2][:, 0:128]), inc=[("dve", 1)])
                    dbgdump(GSC[:, 1024:1536])
                    dbgdump(SCOL[:, 0:32])
                for qi in range(4):
                    pl.run("dve", lambda E, qi=qi: E.tensor_scalar_mul(
                        GSC[:, qi * 128:(qi + 1) * 128], PS[qi][:, 0:128],
                        SCOL[:, 16 + qi:17 + qi]), inc=[("dve", 1)])
                    pl.bank_set(qi, "dve")
                odv = pl.val("dve")
                if SYNCDBG:
                    pl.barrier()
                if dbg_head and g == 0:
                    dbgdump(GSC[:, 0:512])
                pl.wait("pe", "dve", odv)
                pl.bank_wait(6)
                for qi in range(4):
                    pl.run("pe", lambda E, qi=qi: E.transpose(
                        PT[0][:, qi * 128:(qi + 1) * 128],
                        GSC[:, qi * 128:(qi + 1) * 128], IDB[:, :]))
                pl.run("pe", lambda E: E.drain(), inc=[("pe", 1)])
                pl.wait("act", "pe", pl.val("pe"))
                pl.run("act", lambda E, h=h, g=g: E.activation(
                    QN[h][:, g * 512:(g + 1) * 512], PT[0][:, 0:512], AF.Copy),
                    inc=[("act", 1)])
                pl.bank_set(6, "act")

        def attn_a():
            lnx_dve = layernorm()
            pl.phase("attn_a_proj")
            dbgdump(LNX[0][:, :])
            q2scr = [QN[4][:, 0:1024], QN[5][:, 0:1024]]
            ckv_own = [QN[0], QN[1], QN[2], QN[3]]

            pl.bank_wait(4)
            fd = Feeder(ring, [(wkvac[oi], 2048) for oi in range(4)], 2)
            for oi in range(4):
                s = fd.get(oi)
                ring.wait_ready("pe", s)
                pl.wait("pe", "dve", lnx_dve)
                for qt in range(2):
                    bank = (oi % 2) * 2 + qt
                    pl.bank_wait(bank)
                    psum = PS[bank]
                    for k in range(HC):
                        pl.run("pe", lambda E, s=s, k=k, qt=qt, psum=psum: E.matmul(
                            psum[:, :], ring.ap(s, k * 128, 128),
                            LNX[k][:, qt * 512:(qt + 1) * 512],
                            start=(k == 0), stop=(k == HC - 1)),
                            inc=([("pe", 1)] if k == HC - 1 else None))
                    mmv = pl.val("pe")
                    pl.wait("act", "pe", mmv)
                    pl.run("act", lambda E, oi=oi, qt=qt, psum=psum: E.activation(
                        ckv_own[oi][:, qt * 512:(qt + 1) * 512], psum[:, :], AF.Copy,
                        scale=bias_col(12 + oi)))
                    pl.run("act", lambda E, oi=oi, qt=qt, psum=psum: E.activation(
                        q2scr[qt][:, (oi % 2) * 512:(oi % 2) * 512 + 512], psum[:, :],
                        AF.Square), inc=[("act", 1)])
                    pl.bank_set(bank, "act")
                    pl.wait("pe", "act", pl.val("act"))
                    pl.run("pe", lambda E, oi=oi, qt=qt: E.matmul(
                        PS[4 + qt][0:1, :], ONEB[:, :],
                        q2scr[qt][:, (oi % 2) * 512:(oi % 2) * 512 + 512],
                        start=(oi == 0), stop=(oi == 3)),
                        inc=([("pe", 1)] if oi == 3 else None))
                ring.consumed(s)
            ckv_stats_pe = pl.val("pe")

            CKR = SROWS[0:1, 0:1024]
            pl.wait("dve", "pe", ckv_stats_pe)
            for qt in range(2):
                pl.run("dve", lambda E, qt=qt: E.tensor_scalar(
                    CKR[:, qt * 512:(qt + 1) * 512], PS[4 + qt][0:1, :],
                    1.0 / KVL, 1e-6, ALU.mult, ALU.add),
                    inc=([("dve", 1)] if qt == 1 else None))
            pl.bank_set(4, "dve")
            pl.bank_set(5, "dve")
            pl.wait("act", "dve", pl.val("dve"))
            pl.run("act", lambda E: E.activation(CKR[:, :], CKR[:, :], AF.Sqrt),
                   inc=[("act", 1)])
            pl.wait("dve", "act", pl.val("act"))
            pl.run("dve", lambda E: E.reciprocal(CKR[:, :], CKR[:, :]), inc=[("dve", 1)])
            pl.bank_set(4, "dve")
            pl.wait("pe", "dve", pl.val("dve"))
            CKB = ROPS[:, 0:1024]
            pl.bank_wait(4)
            for qt in range(2):
                pl.run("pe", lambda E, qt=qt: E.matmul(
                    PS[4][:, :], ONEF[:, :], CKR[:, qt * 512:(qt + 1) * 512],
                    start=True, stop=True), inc=[("pe", 1)])
                pl.wait("dve", "pe", pl.val("pe"))
                pl.run("dve", lambda E, qt=qt: E.tensor_copy(
                    CKB[:, qt * 512:(qt + 1) * 512], PS[4][:, :]), inc=[("dve", 1)])
                pl.wait("pe", "dve", pl.val("dve"))
            pl.bank_set(4, "dve")
            for oi in range(4):
                pl.run("dve", lambda E, oi=oi: E.tensor_mul(
                    ckv_own[oi][:, :], ckv_own[oi][:, :], CKB[:, :]), inc=[("dve", 1)])
            ckv_rms_dve = pl.val("dve")
            dbgdump(QN[0][:, :])

            s = ring.load(wkvap[:, :], cols=1024)
            ring.wait_ready("pe", s)
            for qt in range(2):
                pl.bank_wait(5)
                for k in range(HC):
                    pl.run("pe", lambda E, s=s, k=k, qt=qt: E.matmul(
                        PS[5][0:64, :], ring.ap(s, k * 64, 64),
                        LNX[k][:, qt * 512:(qt + 1) * 512],
                        start=(k == 0), stop=(k == HC - 1)),
                        inc=([("pe", 1)] if k == HC - 1 else None))
                pl.wait("act", "pe", pl.val("pe"))
                pl.run("act", lambda E, qt=qt: E.activation(
                    QN[6][0:64, qt * 512:(qt + 1) * 512], PS[5][0:64, :], AF.Copy),
                    inc=[("act", 1)])
                pl.bank_set(5, "act")
            ring.consumed(s)
            kpe_act = pl.val("act")

            pl.wait("sp", "dve", ckv_rms_dve)
            for oi in range(4):
                pl.run("sp", lambda E, oi=oi: E.dma_start(
                    out=ag_a_in[oi * 128:(oi + 1) * 128, :], in_=ckv_own[oi][:, :]),
                    inc=[("io", 16)])
            pl.wait("sp", "act", kpe_act)
            pl.run("sp", lambda E: E.dma_start(
                out=ag_a_in[512:576, :], in_=QN[6][0:64, 0:1024]), inc=[("io", 16)])
            a_store_io = pl.val("io")
            pl.wait("pool", "io", a_store_io)
            pl.run("pool", lambda E: E.collective_compute(
                "AllGather", ALU.bypass,
                replica_groups=[[0, 1], [2, 3], [4, 5], [6, 7]],
                ins=[ag_a_in.ap()], outs=[ag_a_out.ap()]), inc=[("cc0", 1)])

            def ql_tile(i):
                if i < 8:
                    return QP[i]
                return KVH[(i - 8) // 2][:, ((i - 8) % 2) * 1024:((i - 8) % 2 + 1) * 1024]
            pl.bank_wait(4)
            pl.bank_wait(5)
            fd = Feeder(ring, [(wqa[oi], 2048) for oi in range(12)], 2)
            for oi in range(12):
                s = fd.get(oi)
                ring.wait_ready("pe", s)
                for qt in range(2):
                    bank = (oi % 2) * 2 + qt
                    pl.bank_wait(bank)
                    psum = PS[bank]
                    for k in range(HC):
                        pl.run("pe", lambda E, s=s, k=k, qt=qt, psum=psum: E.matmul(
                            psum[:, :], ring.ap(s, k * 128, 128),
                            LNX[k][:, qt * 512:(qt + 1) * 512],
                            start=(k == 0), stop=(k == HC - 1)),
                            inc=([("pe", 1)] if k == HC - 1 else None))
                    mmv = pl.val("pe")
                    pl.wait("act", "pe", mmv)
                    pl.run("act", lambda E, oi=oi, qt=qt, psum=psum: E.activation(
                        ql_tile(oi)[:, qt * 512:(qt + 1) * 512], psum[:, :], AF.Copy,
                        scale=bias_col(oi)))
                    pl.run("act", lambda E, oi=oi, qt=qt, psum=psum: E.activation(
                        q2scr[qt][:, (oi % 2) * 512:(oi % 2) * 512 + 512], psum[:, :],
                        AF.Square), inc=[("act", 1)])
                    pl.bank_set(bank, "act")
                    pl.wait("pe", "act", pl.val("act"))
                    pl.run("pe", lambda E, oi=oi, qt=qt: E.matmul(
                        PS[4 + qt][0:1, :], ONEB[:, :],
                        q2scr[qt][:, (oi % 2) * 512:(oi % 2) * 512 + 512],
                        start=(oi == 0), stop=(oi == 11)),
                        inc=([("pe", 1)] if oi == 11 else None))
                ring.consumed(s)
            ql_stats_pe = pl.val("pe")

            QR = SROWS[0:1, 1024:2048]
            pl.wait("dve", "pe", ql_stats_pe)
            for qt in range(2):
                pl.run("dve", lambda E, qt=qt: E.tensor_scalar(
                    QR[:, qt * 512:(qt + 1) * 512], PS[4 + qt][0:1, :],
                    1.0 / QLORA, 1e-6, ALU.mult, ALU.add),
                    inc=([("dve", 1)] if qt == 1 else None))
            pl.bank_set(4, "dve")
            pl.bank_set(5, "dve")
            pl.wait("act", "dve", pl.val("dve"))
            pl.run("act", lambda E: E.activation(QR[:, :], QR[:, :], AF.Sqrt),
                   inc=[("act", 1)])
            pl.wait("dve", "act", pl.val("dve") if False else pl.val("act"))
            pl.run("dve", lambda E: E.reciprocal(QR[:, :], QR[:, :]), inc=[("dve", 1)])
            pl.bank_set(5, "dve")
            pl.wait("pe", "dve", pl.val("dve"))
            QRB = F32S[:, 0:1024]
            pl.bank_wait(4)
            for qt in range(2):
                pl.run("pe", lambda E, qt=qt: E.matmul(
                    PS[4][:, :], ONEF[:, :], QR[:, qt * 512:(qt + 1) * 512],
                    start=True, stop=True), inc=[("pe", 1)])
                pl.wait("dve", "pe", pl.val("pe"))
                pl.run("dve", lambda E, qt=qt: E.tensor_copy(
                    QRB[:, qt * 512:(qt + 1) * 512], PS[4][:, :]), inc=[("dve", 1)])
                pl.wait("pe", "dve", pl.val("dve"))
            pl.bank_set(4, "dve")

            RQC = wtab(0)
            RQS = wtab(1)
            pl.run("pool", lambda E: E.dma_start(out=RQC, in_=ra_q[0]), inc=[("wt", 16)])
            pl.run("pool", lambda E: E.dma_start(out=RQS, in_=ra_q[1]), inc=[("wt", 16)])
            rq_w = pl.val("wt")
            pl.wait("dve", "io", a_store_io)
            fd = Feeder(ring, [(wqb[oi], 1536) for oi in range(24)], 2)
            for oi in range(24):
                s = fd.get(oi)
                ring.wait_ready("pe", s)
                for qt in range(2):
                    bank = (oi % 2) * 2 + qt
                    pl.bank_wait(bank)
                    psum = PS[bank]
                    for k in range(12):
                        pl.run("pe", lambda E, s=s, k=k, qt=qt, psum=psum: E.matmul(
                            psum[:, :], ring.ap(s, k * 128, 128),
                            ql_tile(k)[:, qt * 512:(qt + 1) * 512],
                            start=(k == 0), stop=(k == 11)),
                            inc=([("pe", 1)] if k == 11 else None))
                    mmv = pl.val("pe")
                    pl.wait("dve", "pe", mmv)
                    if oi < 16:
                        pl.run("dve", lambda E, oi=oi, qt=qt, psum=psum:
                               E.scalar_tensor_tensor(
                                   QN[oi][:, qt * 512:(qt + 1) * 512], psum[:, :], SA,
                                   QRB[:, qt * 512:(qt + 1) * 512], ALU.mult, ALU.mult),
                               inc=[("dve", 1)])
                    else:
                        pl.run("dve", lambda E, qt=qt, psum=psum: E.tensor_mul(
                            ROPS[:, 1024 + qt * 512:1024 + qt * 512 + 512],
                            psum[:, :], QRB[:, qt * 512:(qt + 1) * 512]),
                            inc=[("dve", 1)])
                    pl.bank_set(bank, "dve")
                if oi >= 16:
                    g = oi - 16
                    dst = LNX[8 + g]
                    src = ROPS[:, 1024:2048]
                    rot = ROPS[:, 0:1024]
                    pl.wait("dve", "wt", rq_w)
                    pl.run("dve", lambda E: E.tensor_scalar_mul(rot[0:32, :], src[32:64, :], -1.0))
                    pl.run("dve", lambda E: E.tensor_copy(rot[32:64, :], src[0:32, :]))
                    pl.run("dve", lambda E: E.tensor_scalar_mul(rot[64:96, :], src[96:128, :], -1.0))
                    pl.run("dve", lambda E: E.tensor_copy(rot[96:128, :], src[64:96, :]))
                    pl.run("dve", lambda E, dst=dst: E.tensor_mul(dst[:, :], src[:, :], RQC))
                    pl.run("dve", lambda E: E.tensor_mul(rot[:, :], rot[:, :], RQS))
                    pl.run("dve", lambda E, dst=dst: E.tensor_add(dst[:, :], dst[:, :], rot[:, :]),
                           inc=[("dve", 1)])
                ring.consumed(s)
            q_dve = pl.val("dve")
            dbgdump(QN[0][:, :])
            dbgdump(LNX[8][:, :])
            dbgdump(ql_tile(0)[:, :])
            dbgdump(QRB)

            pl.wait("sp", "cc0", 1)
            pl.wait("sp", "pe", pl.val("pe"))
            for oi in range(4):
                for r in range(2):
                    pl.run("sp", lambda E, oi=oi, r=r: E.dma_start(
                        out=QP[2 * oi + r][:, :],
                        in_=ag_a_out[r * 576 + oi * 128:r * 576 + (oi + 1) * 128, :]),
                        inc=[("io", 16)])
            pl.wait("pool", "dve", q_dve)
            for i in range(4):
                pl.run("pool", lambda E, i=i: E.dma_start(out=wtab(i), in_=ra_k[i]),
                       inc=[("wt", 16)])
            rk_w = pl.val("wt")
            kw = ROPS[:, 2048:3072]
            rot = ROPS[:, 0:1024]
            t1 = ROPS[:, 1024:2048]
            for r in range(2):
                pl.wait("sp", "dve", pl.val("dve"))
                pl.run("sp", lambda E, r=r: E.dma_start(
                    out=kw[0:64, 0:1024], in_=ag_a_out[r * 576 + 512:r * 576 + 576, :]),
                    inc=[("io", 16)])
                kio = pl.val("io")
                pl.wait("dve", "io", kio)
                pl.wait("dve", "wt", rk_w)
                ctab = wtab(0 + r)
                stab = wtab(2 + r)
                pl.run("dve", lambda E: E.tensor_scalar_mul(rot[0:32, :], kw[32:64, 0:1024], -1.0))
                pl.run("dve", lambda E: E.tensor_copy(rot[32:64, :], kw[0:32, 0:1024]))
                pl.run("dve", lambda E, ctab=ctab: E.tensor_mul(
                    t1[0:64, :], kw[0:64, 0:1024], ctab[0:64, :]))
                pl.run("dve", lambda E, stab=stab: E.tensor_mul(
                    rot[0:64, :], rot[0:64, :], stab[0:64, :]))
                pl.run("dve", lambda E, r=r: E.tensor_add(
                    KVH[0][0:64, r * 1024:(r + 1) * 1024], t1[0:64, :], rot[0:64, :]))
                pl.run("dve", lambda E, r=r: E.tensor_copy(
                    KVH[0][64:128, r * 1024:(r + 1) * 1024],
                    KVH[0][0:64, r * 1024:(r + 1) * 1024]), inc=[("dve", 1)])
            kpe_dve = pl.val("dve")
            ckv_io = pl.val("io")
            dbgdump(KVH[0][0:64, :])
            dbgdump(QP[0][:, :])

            pl.phase("attn_a_heads")
            srcs = []
            for h in range(NH):
                srcs.append((wkvb[h], 512))
                srcs.append((wkvb[16 + h], 512))
            fdm = Feeder(ringm, srcs, 2)
            for h in range(NH):
                sk_ = fdm.get(2 * h)
                sv_ = fdm.get(2 * h + 1)
                ringm.wait_ready("pe", sv_)
                ringm.wait_ready("pe", sk_)
                pl.wait("pe", "io", ckv_io)
                pl.wait("pe", "dve", kpe_dve)
                for tk in range(4):
                    pl.bank_wait(4)
                    for k in range(4):
                        pl.run("pe", lambda E, sk_=sk_, k=k, tk=tk: E.matmul(
                            PS[4][:, :], ringm.ap(sk_, k * 128, 128),
                            QP[2 * k + tk // 2][:, (tk % 2) * 512:(tk % 2) * 512 + 512],
                            start=(k == 0), stop=(k == 3)),
                            inc=([("pe", 1)] if k == 3 else None))
                    pl.wait("act", "pe", pl.val("pe"))
                    pl.run("act", lambda E, tk=tk: E.activation(
                        KVH[1][:, tk * 512:(tk + 1) * 512], PS[4][:, :], AF.Copy),
                        inc=[("act", 1)])
                    pl.bank_set(4, "act")
                for jq in range(4):
                    pl.bank_wait(5)
                    for jj4 in range(4):
                        j = jq * 4 + jj4
                        for k in range(4):
                            pl.run("pe", lambda E, sv_=sv_, k=k, j=j, jj4=jj4: E.matmul(
                                PS[5][:, jj4 * 128:(jj4 + 1) * 128],
                                QP[2 * k + j // 8][:, (j % 8) * 128:(j % 8) * 128 + 128],
                                ringm.ap(sv_, k * 128, 128),
                                start=(k == 0), stop=(k == 3)),
                                inc=([("pe", 1)] if (k == 3 and jj4 == 3) else None))
                    pl.wait("act", "pe", pl.val("pe"))
                    pl.run("act", lambda E, jq=jq: E.activation(
                        ROPS[:, jq * 512:(jq + 1) * 512], PS[5][:, :], AF.Copy),
                        inc=[("act", 1)])
                    pl.bank_set(5, "act")
                kv_act = pl.val("act")
                ringm.consumed(sk_)
                ringm.consumed(sv_)
                pl.wait("pe", "act", kv_act)

                def scores_ops(qt, j0, nj, psum, h=h):
                    pl.run("pe", lambda E, qt=qt, j0=j0, nj=nj, psum=psum, h=h: E.matmul(
                        psum[:, 0:nj * 128], QN[h][:, qt * 128:(qt + 1) * 128],
                        KVH[1][:, j0 * 128:(j0 + nj) * 128],
                        start=True, stop=False))
                    hb = 64 * (h % 2)
                    pl.run("pe", lambda E, qt=qt, j0=j0, nj=nj, psum=psum, h=h, hb=hb:
                           E.matmul(
                               psum[:, 0:nj * 128],
                               LNX[8 + h // 2][hb:hb + 64, qt * 128:(qt + 1) * 128],
                               KVH[0][hb:hb + 64, j0 * 128:(j0 + nj) * 128],
                               start=False, stop=True), inc=[("pe", 1)])

                def vblk_ap(j):
                    return ROPS[:, j * 128:(j + 1) * 128]

                def probs_of(qi, j0, nj):
                    t = LNX[2 * qi] if j0 < 8 else LNX[2 * qi + 1]
                    c0 = (j0 % 8) * 128
                    return t[:, c0:c0 + nj * 128]

                if h == 0:
                    dbgdump(KVH[1][:, :])
                    dbgdump(ROPS[:, 0:2048])
                attn_inner(h, scores_ops, vblk_ap, probs_of,
                           dbg_head=(h == 0 and NDBG > 0))
                if h == 0:
                    dbgdump(QN[0][:, :])

            pl.phase("attn_a_wo")
            proj_add(lambda o: wo[o], 16,
                     lambda k, qt: QN[k][:, qt * 512:(qt + 1) * 512],
                     pre_pe=[("act", pl.val("act"))])

        def ffn():
            lnx_dve = layernorm()

            s = ring.load(rw[:, :], cols=128)
            ring.wait_ready("pe", s)
            pl.wait("pe", "dve", lnx_dve)
            LGF = F32S[0:8, 0:1024]
            for qt in range(2):
                pl.bank_wait(4)
                for k in range(HC):
                    pl.run("pe", lambda E, s=s, k=k, qt=qt: E.matmul(
                        PS[4][0:8, :], ring.ap(s, k * 8, 8),
                        LNX[k][:, qt * 512:(qt + 1) * 512],
                        start=(k == 0), stop=(k == HC - 1)),
                        inc=([("pe", 1)] if k == HC - 1 else None))
                pl.wait("act", "pe", pl.val("pe"))
                pl.run("act", lambda E, qt=qt: E.activation(
                    LGF[:, qt * 512:(qt + 1) * 512], PS[4][0:8, :], AF.Copy),
                    inc=[("act", 1)])
                pl.bank_set(4, "act")
            ring.consumed(s)
            lg_act = pl.val("act")
            pl.wait("pe", "act", lg_act)
            pl.bank_wait(5)
            for half in range(2):
                for b4 in range(4):
                    blk = half * 4 + b4
                    pl.run("pe", lambda E, blk=blk, b4=b4: E.transpose(
                        PS[5][:, b4 * 8:b4 * 8 + 8],
                        LGF[:, blk * 128:(blk + 1) * 128], IDF[0:8, 0:8]))
                pl.run("pe", lambda E: E.drain(), inc=[("pe", 1)])
                pl.wait("dve", "pe", pl.val("pe"))
                for b4 in range(4):
                    blk = half * 4 + b4
                    pl.run("dve", lambda E, blk=blk, b4=b4: E.tensor_copy(
                        RTR[:, blk * 8:blk * 8 + 8], PS[5][:, b4 * 8:b4 * 8 + 8]),
                        inc=([("dve", 1)] if b4 == 3 else None))
                pl.wait("pe", "dve", pl.val("dve"))
            pl.bank_set(5, "dve")
            for blk in range(8):
                P8 = RTR[:, blk * 8:blk * 8 + 8]
                CMB = RTR[:, 64 + blk * 8:64 + blk * 8 + 8]
                T8 = RTR[:, 128 + blk * 8:128 + blk * 8 + 8]
                pl.run("dve", lambda E, P8=P8: E.reduce_max(
                    SCOL[:, 0:1], P8, axis=AX.X, negate=True), inc=[("dve", 1)], guard=True)
                pl.wait("act", "dve", pl.val("dve"))
                pl.run("act", lambda E, P8=P8: E.activation(
                    P8, P8, AF.Exp, bias=SCOL[:, 0:1], accum_out=SCOL[:, 1:2]))
                pl.run("act", lambda E: E.drain(), inc=[("act", 1)])
                pl.wait("dve", "act", pl.val("act"))
                pl.run("dve", lambda E: E.reciprocal(SCOL[:, 2:3], SCOL[:, 1:2]), guard=True)
                pl.run("dve", lambda E, P8=P8: E.tensor_scalar_mul(P8, P8, SCOL[:, 2:3]), guard=True)
                pl.run("dve", lambda E, P8=P8: E.reduce_max(SCOL[:, 3:4], P8, axis=AX.X), guard=True)
                pl.run("dve", lambda E, P8=P8, CMB=CMB: E.tensor_scalar(
                    CMB, P8, SCOL[:, 3:4], None, ALU.is_ge), guard=True)
                pl.run("dve", lambda E, CMB=CMB, T8=T8: E.tensor_scalar(
                    T8, CMB, -1.0, 1.0, ALU.mult, ALU.add), guard=True)
                pl.run("dve", lambda E, P8=P8, T8=T8: E.tensor_mul(T8, T8, P8), guard=True)
                pl.run("dve", lambda E, T8=T8: E.reduce_max(SCOL[:, 4:5], T8, axis=AX.X), guard=True)
                pl.run("dve", lambda E, T8=T8: E.tensor_scalar(
                    T8, T8, SCOL[:, 4:5], None, ALU.is_ge), guard=True)
                pl.run("dve", lambda E: E.tensor_add(SCOL[:, 5:6], SCOL[:, 3:4], SCOL[:, 4:5]), guard=True)
                pl.run("dve", lambda E: E.reciprocal(SCOL[:, 5:6], SCOL[:, 5:6]), guard=True)
                pl.run("dve", lambda E: E.tensor_mul(SCOL[:, 6:7], SCOL[:, 3:4], SCOL[:, 5:6]), guard=True)
                pl.run("dve", lambda E: E.tensor_mul(SCOL[:, 7:8], SCOL[:, 4:5], SCOL[:, 5:6]), guard=True)
                pl.run("dve", lambda E, CMB=CMB: E.tensor_scalar_mul(CMB, CMB, SCOL[:, 6:7]), guard=True)
                pl.run("dve", lambda E, CMB=CMB, T8=T8: E.scalar_tensor_tensor(
                    CMB, T8, SCOL[:, 7:8], CMB, ALU.mult, ALU.add), inc=[("dve", 1)])
            cmb_dve = pl.val("dve")

            CBCF = F32S[:, 0:1024]
            CROW = SROWS[0:1, 0:1024]

            def compute_cbc(e):
                pl.wait("pe", "dve", cmb_dve)
                for half in range(2):
                    pl.bank_wait(4)
                    for b4 in range(4):
                        blk = half * 4 + b4
                        pl.run("pe", lambda E, blk=blk, b4=b4, e=e: E.transpose(
                            PS[4][0:1, b4 * 128:(b4 + 1) * 128],
                            RTR[:, 64 + blk * 8 + e:64 + blk * 8 + e + 1], IDF[:, :]))
                    pl.run("pe", lambda E: E.drain(), inc=[("pe", 1)])
                    pl.wait("dve", "pe", pl.val("pe"))
                    pl.run("dve", lambda E, half=half: E.tensor_copy(
                        CROW[:, half * 512:(half + 1) * 512], PS[4][0:1, 0:512]),
                        inc=[("dve", 1)])
                    pl.bank_set(4, "dve")
                pl.wait("pe", "dve", pl.val("dve"))
                for qt in range(2):
                    pl.bank_wait(4)
                    pl.run("pe", lambda E, qt=qt: E.matmul(
                        PS[4][:, :], ONEF[:, :], CROW[:, qt * 512:(qt + 1) * 512],
                        start=True, stop=True), inc=[("pe", 1)])
                    pl.wait("dve", "pe", pl.val("pe"))
                    pl.run("dve", lambda E, qt=qt: E.tensor_copy(
                        CBCF[:, qt * 512:(qt + 1) * 512], PS[4][:, :]), inc=[("dve", 1)])
                    pl.bank_set(4, "dve")
                    pl.wait("pe", "dve", pl.val("dve"))

            def gu_tile(i, ei):
                if ei == 0:
                    return QN[i]
                return QN[11 + i] if i < 5 else QP[i - 5]

            def gate_up(e, ei, with_bias, dst_fn):
                noch = FIC if not with_bias else FSC
                srcs = []
                for oi in range(noch):
                    srcs.append(((gw[e, oi] if not with_bias else sgw[oi]), 2048))
                    srcs.append(((uw[e, oi] if not with_bias else suw[oi]), 2048))
                fd = Feeder(ring, srcs, 2)
                for oi in range(noch):
                    sg_ = fd.get(2 * oi)
                    su_ = fd.get(2 * oi + 1)
                    ring.wait_ready("pe", su_)
                    ring.wait_ready("pe", sg_)
                    pl.wait("pe", "dve", lnx_dve)
                    for qt in range(2):
                        bank = qt
                        bank2 = 2 + qt
                        pl.bank_wait(bank)
                        for k in range(HC):
                            pl.run("pe", lambda E, sg_=sg_, k=k, qt=qt, bank=bank: E.matmul(
                                PS[bank][:, :], ring.ap(sg_, k * 128, 128),
                                LNX[k][:, qt * 512:(qt + 1) * 512],
                                start=(k == 0), stop=(k == HC - 1)),
                                inc=([("pe", 1)] if k == HC - 1 else None))
                        gate_pe = pl.val("pe")
                        pl.bank_wait(bank2)
                        for k in range(HC):
                            pl.run("pe", lambda E, su_=su_, k=k, qt=qt, bank2=bank2: E.matmul(
                                PS[bank2][:, :], ring.ap(su_, k * 128, 128),
                                LNX[k][:, qt * 512:(qt + 1) * 512],
                                start=(k == 0), stop=(k == HC - 1)),
                                inc=([("pe", 1)] if k == HC - 1 else None))
                        up_pe = pl.val("pe")
                        gcol = qt * 512
                        pl.wait("act", "pe", gate_pe)
                        if with_bias:
                            pl.run("act", lambda E, bank=bank, gcol=gcol, oi=oi: E.activation(
                                GSC[:, gcol:gcol + 512], PS[bank][:, :], AF.Silu,
                                bias=bias_col(80 + oi)), inc=[("act", 1)])
                        else:
                            pl.run("act", lambda E, bank=bank, gcol=gcol: E.activation(
                                GSC[:, gcol:gcol + 512], PS[bank][:, :], AF.Silu),
                                inc=[("act", 1)])
                        pl.bank_set(bank, "act")
                        pl.wait("dve", "act", pl.val("act"))
                        pl.wait("dve", "pe", up_pe)
                        dst_fn(oi, qt, gcol, bank2)
                        pl.bank_set(bank2, "dve")
                    ring.consumed(sg_)
                    ring.consumed(su_)

            def expert_pair(p):
                for ei in range(2):
                    e = 2 * p + ei
                    compute_cbc(e)

                    def dst_fn(oi, qt, gcol, bank2, ei=ei):
                        pl.run("dve", lambda E, gcol=gcol, qt=qt, bank2=bank2:
                               E.scalar_tensor_tensor(
                                   GSC[:, 1024 + gcol:1024 + gcol + 512],
                                   PS[bank2][:, :], 1.0,
                                   CBCF[:, qt * 512:(qt + 1) * 512],
                                   ALU.mult, ALU.mult))
                        pl.run("dve", lambda E, oi=oi, qt=qt, gcol=gcol, ei=ei: E.tensor_mul(
                            gu_tile(oi, ei)[:, qt * 512:(qt + 1) * 512],
                            GSC[:, 1024 + gcol:1024 + gcol + 512],
                            GSC[:, gcol:gcol + 512]), inc=[("dve", 1)])
                    gate_up(e, ei, False, dst_fn)
                gu_dve = pl.val("dve")

                srcs = []
                for och in range(16):
                    srcs.append((dwt[2 * p, och], FI))
                    srcs.append((dwt[2 * p + 1, och], FI))
                fd = Feeder(ring, srcs, 2)
                for och in range(16):
                    sd0 = fd.get(2 * och)
                    sd1 = fd.get(2 * och + 1)
                    ring.wait_ready("pe", sd1)
                    ring.wait_ready("pe", sd0)
                    pl.wait("pe", "dve", gu_dve)
                    for qt in range(2):
                        bank = (och % 2) * 2 + qt
                        pl.bank_wait(bank)
                        psum = PS[bank]
                        for ei in range(2):
                            sd = sd0 if ei == 0 else sd1
                            for k in range(FIC):
                                pl.run("pe", lambda E, sd=sd, k=k, qt=qt, psum=psum, ei=ei:
                                       E.matmul(psum[:, :], ring.ap(sd, k * 128, 128),
                                                gu_tile(k, ei)[:, qt * 512:(qt + 1) * 512],
                                                start=(ei == 0 and k == 0),
                                                stop=(ei == 1 and k == FIC - 1)),
                                       inc=([("pe", 1)] if (ei == 1 and k == FIC - 1)
                                            else None))
                        add_to_x(och, qt, psum, pl.val("pe"))
                        pl.bank_set(bank, "dve")
                    ring.consumed(sd0)
                    ring.consumed(sd1)

            for p in range(4):
                pl.phase(f"ffn_pair")
                expert_pair(p)

            pl.phase("ffn_shared")
            def sgu_tile(i):
                return QN[i] if i < 16 else QP[i - 16]

            def sdst_fn(oi, qt, gcol, bank2):
                pl.run("dve", lambda E, oi=oi, qt=qt, gcol=gcol, bank2=bank2:
                       E.scalar_tensor_tensor(
                           sgu_tile(oi)[:, qt * 512:(qt + 1) * 512],
                           PS[bank2][:, :], bias_col(102 + oi),
                           GSC[:, gcol:gcol + 512], ALU.add, ALU.mult),
                       inc=[("dve", 1)])
            gate_up(0, 0, True, sdst_fn)
            sgu_dve = pl.val("dve")
            srcs = []
            for och in range(16):
                srcs.append((sdw[och][:, 0:2048], 2048))
                srcs.append((sdw[och][:, 2048:2816], 768))
            fd = Feeder(ring, srcs, 2)
            for och in range(16):
                sda = fd.get(2 * och)
                sdb = fd.get(2 * och + 1)
                ring.wait_ready("pe", sdb)
                ring.wait_ready("pe", sda)
                pl.wait("pe", "dve", sgu_dve)
                for qt in range(2):
                    bank = (och % 2) * 2 + qt
                    pl.bank_wait(bank)
                    psum = PS[bank]
                    for k in range(FSC):
                        sd = sda if k < 16 else sdb
                        kk = k if k < 16 else k - 16
                        pl.run("pe", lambda E, sd=sd, kk=kk, k=k, qt=qt, psum=psum: E.matmul(
                            psum[:, :], ring.ap(sd, kk * 128, 128),
                            sgu_tile(k)[:, qt * 512:(qt + 1) * 512],
                            start=(k == 0), stop=(k == FSC - 1)),
                            inc=([("pe", 1)] if k == FSC - 1 else None))
                    add_to_x(och, qt, psum, pl.val("pe"), bias_idx=124 + och)
                    pl.bank_set(bank, "dve")
                ring.consumed(sda)
                ring.consumed(sdb)

        def attn_b():
            lnx_dve = layernorm()
            RQC = wtab(0)
            RQS = wtab(1)
            RKC = wtab(2)
            RKS = wtab(3)
            pl.run("pool", lambda E: E.dma_start(out=RQC, in_=rb_q[0]), inc=[("wt", 16)])
            pl.run("pool", lambda E: E.dma_start(out=RQS, in_=rb_q[1]), inc=[("wt", 16)])
            pl.run("pool", lambda E: E.dma_start(out=RKC, in_=rb_k[0]), inc=[("wt", 16)])
            pl.run("pool", lambda E: E.dma_start(out=RKS, in_=rb_k[1]), inc=[("wt", 16)])
            rb_w = pl.val("wt")

            kv_store_io = [0]
            fd = Feeder(ring, [(wqkv[oi], 2048) for oi in range(48)], 2)
            for oi in range(48):
                s = fd.get(oi)
                ring.wait_ready("pe", s)
                pl.wait("pe", "dve", lnx_dve)
                for qt in range(2):
                    bank = qt
                    pl.bank_wait(bank)
                    for k in range(HC):
                        pl.run("pe", lambda E, s=s, k=k, qt=qt, bank=bank: E.matmul(
                            PS[bank][:, :], ring.ap(s, k * 128, 128),
                            LNX[k][:, qt * 512:(qt + 1) * 512],
                            start=(k == 0), stop=(k == HC - 1)),
                            inc=([("pe", 1)] if k == HC - 1 else None))
                    mmv = pl.val("pe")
                    pl.wait("act", "pe", mmv)
                    pl.run("act", lambda E, oi=oi, qt=qt, bank=bank: E.activation(
                        ROPS[:, qt * 512:(qt + 1) * 512], PS[bank][:, :], AF.Identity,
                        bias=bias_col(16 + oi)), inc=[("act", 1)])
                    pl.bank_set(bank, "act")
                pj_act = pl.val("act")
                h = oi % 16
                if oi < 32:
                    is_q = oi < 16
                    dst = QN[h] if is_q else QP[h % 2]
                    src = ROPS[:, 0:1024]
                    rot = ROPS[:, 1024:2048]
                    ctab = RQC if is_q else RKC
                    stab = RQS if is_q else RKS
                    pl.wait("dve", "act", pj_act)
                    pl.wait("dve", "wt", rb_w)
                    if not is_q:
                        pl.wait("dve", "io", kv_store_io[0])
                    pl.run("dve", lambda E: E.tensor_scalar_mul(rot[0:16, :], src[32:48, :], -1.0))
                    pl.run("dve", lambda E: E.tensor_copy(rot[32:48, :], src[0:16, :]))
                    pl.run("dve", lambda E, dst=dst, ctab=ctab: E.tensor_mul(
                        dst[:, :], src[:, :], ctab))
                    pl.run("dve", lambda E, stab=stab: E.tensor_mul(
                        rot[0:48, :], rot[0:48, :], stab[0:48, :]))
                    pl.run("dve", lambda E, dst=dst: E.tensor_add(
                        dst[0:48, :], dst[0:48, :], rot[0:48, :]), inc=[("dve", 1)])
                    if not is_q:
                        pl.wait("sp", "dve", pl.val("dve"))
                        pl.run("sp", lambda E, h=h, dst=dst: E.dma_start(
                            out=ag_bk_in[h * 128:(h + 1) * 128, :], in_=dst[:, :]),
                            inc=[("io", 16)])
                        kv_store_io[0] = pl.val("io")
                else:
                    pl.wait("pe", "act", pj_act)
                    pl.wait("pe", "io", kv_store_io[0])
                    for blk in range(8):
                        pb = 6 + blk % 2
                        pl.bank_wait(pb)
                        pl.run("pe", lambda E, blk=blk: E.transpose(
                            PT[blk % 2][:, 0:128], ROPS[:, blk * 128:(blk + 1) * 128],
                            IDB[:, :]))
                        pl.run("pe", lambda E: E.drain(), inc=[("pe", 1)])
                        pl.wait("dve", "pe", pl.val("pe"))
                        pl.run("dve", lambda E, blk=blk: E.tensor_copy(
                            QP[2 + blk // 4][:, (blk % 4) * 128:(blk % 4) * 128 + 128],
                            PT[blk % 2][:, 0:128]), inc=[("dve", 1)])
                        pl.bank_set(pb, "dve")
                    pl.wait("sp", "dve", pl.val("dve"))
                    for half in range(2):
                        pl.run("sp", lambda E, h=h, half=half: E.dma_start(
                            out=ag_bv_in[h * 1024 + half * 512:h * 1024 + (half + 1) * 512, :]
                            .rearrange("(a p) d -> p a d", p=128),
                            in_=QP[2 + half][:, 0:512]), inc=[("io", 16)])
                    kv_store_io[0] = pl.val("io")
                ring.consumed(s)

            pl.wait("pool", "io", pl.val("io"))
            pl.run("pool", lambda E: E.collective_compute(
                "AllGather", ALU.bypass,
                replica_groups=[[0, 1], [2, 3], [4, 5], [6, 7]],
                ins=[ag_bk_in.ap()], outs=[ag_bk_out.ap()]), inc=[("cc1", 1)])
            pl.run("pool", lambda E: E.collective_compute(
                "AllGather", ALU.bypass,
                replica_groups=[[0, 1], [2, 3], [4, 5], [6, 7]],
                ins=[ag_bv_in.ap()], outs=[ag_bv_out.ap()]), inc=[("cc2", 1)])

            pl.phase("attn_b_heads")
            pl.wait("sp", "cc1", 1)
            pl.wait("sp", "cc2", 1)
            for h in range(NH):
                pl.wait("sp", "pe", pl.val("pe"))
                pl.wait("sp", "act", pl.val("act"))
                for r in range(2):
                    pl.run("sp", lambda E, h=h, r=r: E.dma_start(
                        out=KVH[0][:, r * 1024:(r + 1) * 1024],
                        in_=ag_bk_out[r * 2048 + h * 128:r * 2048 + (h + 1) * 128, :]),
                        inc=[("io", 16)])
                    pl.run("sp", lambda E, h=h, r=r: E.dma_start(
                        out=KVH[1][:, r * 1024:(r + 1) * 1024],
                        in_=ag_bv_out[r * NH * 1024 + h * 1024:
                                      r * NH * 1024 + (h + 1) * 1024, :]
                        .rearrange("(a p) d -> p a d", p=128)), inc=[("io", 16)])
                kv_io = pl.val("io")
                pl.wait("pe", "io", kv_io)

                def scores_ops(qt, j0, nj, psum, h=h):
                    pl.run("pe", lambda E, qt=qt, j0=j0, nj=nj, psum=psum, h=h: E.matmul(
                        psum[:, 0:nj * 128], QN[h][:, qt * 128:(qt + 1) * 128],
                        KVH[0][:, j0 * 128:(j0 + nj) * 128],
                        start=True, stop=True), inc=[("pe", 1)])

                def vblk_ap(j):
                    return KVH[1][:, j * 128:(j + 1) * 128]

                def probs_of(qi, j0, nj):
                    t = LNX[2 * qi] if j0 < 8 else LNX[2 * qi + 1]
                    c0 = (j0 % 8) * 128
                    return t[:, c0:c0 + nj * 128]

                attn_inner(h, scores_ops, vblk_ap, probs_of)

            proj_add(lambda o: wdense[o], 16,
                     lambda k, qt: QN[k][:, qt * 512:(qt + 1) * 512],
                     bias_base=64, pre_pe=[("act", pl.val("act"))])

        def tap_store(i):
            if not TAPS:
                return
            pl.wait("pool", "dve", pl.val("dve"))
            pl.wait("pool", "act", pl.val("act"))
            pl.wait("pool", "pe", pl.val("pe"))
            for c in range(HC):
                pl.run("pool", lambda E, c=c, i=i: E.dma_start(
                    out=taps[i][c, :, :], in_=X[c][:, :]), inc=[("out", 16)])

        if N_SUB >= 1:
            pl.phase("attn_a")
            attn_a()
            tap_store(0)
        if N_SUB >= 2:
            pl.phase("ffn1")
            ffn()
            tap_store(1)
        if N_SUB >= 3:
            pl.phase("attn_b")
            attn_b()
            tap_store(2)
        if N_SUB >= 4:
            pl.phase("ffn2")
            ffn()
            tap_store(3)
        pl.phase("fin")

        pl.wait("pool", "dve", pl.val("dve"))
        pl.wait("pool", "act", pl.val("act"))
        pl.wait("pool", "pe", pl.val("pe"))
        for c in range(HC):
            pl.run("pool", lambda E, c=c: E.dma_start(out=yout[c, :, :], in_=X[c][:, :]),
                   inc=[("out", 16)])
        pl.wait("pool", "out", pl.val("out"))
        pl.wait("sp", "out", pl.val("out"))

        blk = E(nc.Block())

        def walk(Eng, lst):
            import contextlib as _ctx
            cur = None
            ctx = _ctx.nullcontext()
            ctx.__enter__()
            for f in lst:
                if isinstance(f, str):
                    ctx.__exit__(None, None, None)
                    ctx = nc.named_scope(f)
                    ctx.__enter__()
                else:
                    f(Eng)
            ctx.__exit__(None, None, None)

        @blk.tensor
        def _(Eng):
            walk(Eng, pl.q["pe"])

        @blk.vector
        def _(Eng):
            walk(Eng, pl.q["dve"])

        @blk.scalar
        def _(Eng):
            walk(Eng, pl.q["act"])

        @blk.sync
        def _(Eng):
            walk(Eng, pl.q["sp"])

        @blk.gpsimd
        def _(Eng):
            walk(Eng, pl.q["pool"])

    return nc


def kernel(**inputs):
    nc = build_nc()
    in_maps = [build_host_inputs(c, inputs) for c in range(NCORES)]
    res = run_bass_kernel_spmd(nc, in_maps, core_ids=list(range(NCORES)))
    out = np.zeros((B, S, H), dtype=np.float32)
    for c in range(NCORES):
        par, bidx = c % 2, c // 2
        y = np.asarray(res.results[c]["yout"])
        xt = y.reshape(H, TC).T.reshape(NBLK, 128, H)
        for i in range(NBLK):
            p = 2 * i + par
            out[bidx, p * 128:(p + 1) * 128, :] = xt[i]
    return out
